# version 17
# baseline (speedup 1.0000x reference)
import math
import numpy as np
import ml_dtypes
import concourse.bass as bass
import concourse.mybir as mybir
from concourse.bass_utils import run_bass_kernel_spmd

F32 = mybir.dt.float32
BF16 = mybir.dt.bfloat16
I16 = mybir.dt.int16
AF = mybir.ActivationFunctionType
ALU = mybir.AluOpType
AX = mybir.AxisListType
BF = ml_dtypes.bfloat16

B, S, D = 2, 8192, 1024
H, HD = 16, 64
NCHUNK = 16
CS = 512
NG = 4
TOK = NG * CS
LN_EPS = 1e-5
MASKVAL = -30000.0
DEV_STAGE = 4

SCHED = [(c, g) for c in range(NCHUNK) for g in range(NG) if 4 * g + 3 >= c]
NSTEP = len(SCHED)


class Sem:
    __slots__ = ("h", "n")

    def __init__(self, h):
        self.h = h
        self.n = 0


class KB:
    def __init__(self):
        self.q = {k: [] for k in "tvsgy"}
        self.auto = {}

    def on(self, e, fn):
        self.q[e].append(fn)

    def wait(self, e, sem, val=None):
        v = sem.n if val is None else val
        if v > 0:
            self.on(e, lambda E, h=sem.h, vv=v: E.wait_ge(h, vv))

    def call(self, e, meth, *args, sem=None, inc=1, **kw):
        if e in self.auto and meth != "dma_start":
            own = self.auto[e]
            if own.n > 0:
                self.on(e, lambda E, h=own.h, vv=own.n: E.wait_ge(h, vv))
            assert sem is None or sem is own
            sem = own
        if sem is not None:
            sem.n += inc

        def fn(E, meth=meth, args=args, kw=kw, sem=sem, inc=inc):
            ins = getattr(E, meth)(*args, **kw)
            if sem is not None:
                ins.then_inc(sem.h, inc)

        self.on(e, fn)
        return sem.n if sem is not None else None

    def dma(self, e, out, in_, sem=None, **kw):
        return self.call(e, "dma_start", out, in_, sem=sem, inc=16, **kw)

    def mm(self, out, lhsT, rhs, start=True, stop=True, sem=None, tp=None):
        kw = {}
        if tp is not None:
            kw["tile_position"] = tp
        return self.call("t", "matmul", out, lhsT, rhs, start=start, stop=stop,
                         sem=sem, **kw)

    def tpose(self, out, in_, iden, sem=None):
        return self.call("t", "transpose", out, in_, iden, sem=sem)

    def act(self, out, in_, func, bias=0.0, scale=1.0, accum=None, sem=None):
        kw = {}
        if accum is not None:
            kw["accum_out"] = accum
        return self.call("s", "activation", out, in_, func, bias=bias,
                         scale=scale, sem=sem, **kw)

    def tt(self, out, a, c, op, sem=None):
        return self.call("v", "tensor_tensor", out, a, c, op, sem=sem)

    def ts(self, out, a, s1, s2, op0, op1=ALU.bypass, sem=None):
        return self.call("v", "tensor_scalar", out, a, s1, s2, op0, op1,
                         sem=sem)

    def stt(self, out, in0, scalar, in1, op0, op1, sem=None):
        return self.call("v", "scalar_tensor_tensor", out, in0, scalar, in1,
                         op0, op1, sem=sem)

    def red(self, kind, out, in_, sem=None, **kw):
        return self.call("v", kind, out, in_, AX.X, sem=sem, **kw)

    def vcopy(self, out, in_, sem=None):
        return self.call("v", "tensor_copy", out, in_, sem=sem)


def build_program():
    nc = bass.Bass("TRN2", target_bir_lowering=False)

    def din(name, shape, dt=F32):
        return nc.dram_tensor(name, shape, dt, kind="ExternalInput").ap()

    x_own = din("x_own", [TOK, D])
    x_pref = din("x_pref", [S, D])
    offs_d = din("offs_h", [128, NSTEP])
    ropeq_d = din("ropeq", [128, 2 * TOK], BF16)
    ropek_d = din("ropek", [128, 2 * S], BF16)
    dt_d = din("dtiles", [128, 4 * CS], I16)
    idb_d = din("idb_h", [128, 128], BF16)
    idf_d = din("idf_h", [128, 128])
    prot_d = din("prot_h", [128, 128], BF16)
    oh16_d = din("oh16_h", [128, 16 * H], BF16)
    sel16_d = din("sel16_h", [16, 128 * (H // 2)], BF16)
    oh8_d = din("oh8_h", [8, 8 * 128], BF16)
    wqkv_d = din("Wqkv", [D, 3 * D])
    wo_d = din("Wo", [D, D])
    w1_d = din("W1", [D, 4 * D])
    w2_d = din("W2", [4 * D, D])
    wg_d = din("Wg", [D, 8])
    mw1_d = din("MW1", [8, D, 4 * D])
    mw2_d = din("MW2", [8, 4 * D, D])
    ws1_d = din("Ws1", [D, 4 * D])
    ws2_d = din("Ws2", [4 * D, D])
    bq_f_d = din("bq_f", [128, 8])
    bk_f_d = din("bk_f", [128, 8])
    b1_f_d = din("b1_f", [128, 32])
    bs1_f_d = din("bs1_f", [128, 32])
    b1m_f_d = din("b1m_f", [128, 8 * 32])
    g2_f_d = din("g2_f", [128, 8])
    b2l_f_d = din("b2l_f", [128, 8])
    g1_r_d = din("g1_r", [1, D])
    b1l_r_d = din("b1l_r", [1, D])
    bv_r_d = din("bv_r", [1, D])
    bo_r_d = din("bo_r", [1, D])
    b2_r_d = din("b2_r", [1, D])
    bs2_r_d = din("bs2_r", [1, D])
    b2m_r_d = din("b2m_r", [1, 8 * D])
    out_d = nc.dram_tensor("out", [TOK, D], F32, kind="ExternalOutput").ap()

    b = KB()
    from contextlib import ExitStack
    with ExitStack() as stack:
        en = stack.enter_context

        def sb(name, shape, dt):
            return en(nc.sbuf_tensor(name, shape, dt))

        idb = sb("idb", [128, 128], BF16)
        idf = sb("idf", [128, 128], F32)
        oh8 = sb("oh8", [8, 8 * 128], BF16)
        ones_r = sb("ones_r", [1, CS], BF16)
        ones2 = sb("ones2", [128, 128], BF16)
        biasf = sb("biasf", [128, 352], F32)
        BQ, BK, B1C, BS1C, B1MC, LN2C = 0, 8, 16, 48, 80, 336
        brow = sb("brow", [1, D], BF16)
        st1 = sb("st1", [1, 16], F32)
        stats = sb("stats", [128, 48], F32)
        tmp1 = sb("tmp1", [128, 1024], F32)
        xsta = sb("xsta", [128, 4 * D], F32)
        ntok = sb("ntok", [128, 4 * D], BF16)
        nf = sb("nf", [128, 8 * CS], BF16)
        kbuf = sb("kbuf", [128, 8 * CS], BF16)
        vbuf = sb("vbuf", [128, 4 * D], BF16)
        kpre = sb("kpre", [128, 8 * CS], BF16)
        acc = [sb(f"acc{g}", [128, 8 * CS], F32) for g in range(NG)]

        bstack = ExitStack()

        def sbB(name, shape, dt):
            return bstack.enter_context(nc.sbuf_tensor(name, shape, dt))

        prot = sbB("prot", [128, 128], BF16)
        lns = sbB("lns", [128, 640], F32)
        oh16 = sbB("oh16", [128, 16 * H], BF16)
        sel16 = sbB("sel16", [16, 128 * (H // 2)], BF16)
        dti = sbB("dti", [128, 4 * CS], I16)
        offs = sbB("offs", [128, NSTEP], F32)
        bvrow = sbB("bvrow", [1, D], BF16)
        g1row = sbB("g1row", [128, D], BF16)
        b1row = sbB("b1row", [128, D], BF16)
        ropeb = sbB("ropeb", [128, 2 * CS], BF16)
        wstr = sbB("wstr", [128, 4 * D], BF16)
        pt_a = sbB("pt_a", [128, 2 * CS], BF16)
        pt_b = sbB("pt_b", [128, 2 * CS], BF16)
        pt_c = sbB("pt_c", [128, 2 * CS], BF16)
        cmpb = sbB("cmpb", [128, 4 * CS], BF16)
        lsum = [sbB(f"lsum{g}", [16, CS], F32) for g in range(NG)]
        dstack = ExitStack()
        qg = [dstack.enter_context(
            nc.sbuf_tensor(f"qg{g}", [128, 8 * CS], BF16, side="right"))
            for g in range(NG)]
        ffn_bufs = {}

        def ffn_alloc(stage):
            if stage in ffn_bufs:
                return
            if stage == 1:
                ffn_bufs[1] = (sb("actb1", [128, 4 * 2048], BF16),
                               sb("wp1a", [128, 4 * 1024], BF16),
                               sb("wp2a", [128, 4 * 1024], BF16))
            else:
                ffn_bufs[2] = (sb("actb2", [128, 4 * 2048], BF16),
                               sb("wp1b", [128, 4 * 1024], BF16),
                               sb("wp2b", [128, 4 * 1024], BF16),
                               sb("cwT", [8, TOK], BF16),
                               sb("cwb", [128, TOK], BF16),
                               sb("wgb", [128, 64], BF16))

        pst = en(nc.psum_tensor("pst", [128, 2048], F32))
        pav = en(nc.psum_tensor("pav", [128, 1024], F32))
        ps_l = en(nc.psum_tensor("ps_l", [128, 512], F32))
        ps_x = en(nc.psum_tensor("ps_x", [128, 512], F32))

        sx = Sem(en(nc.semaphore("sx")))
        sg = Sem(en(nc.semaphore("sg")))
        sxs = [Sem(en(nc.semaphore(f"sxs{i}"))) for i in range(4)]
        sxr = Sem(en(nc.semaphore("sxr")))
        swa = Sem(en(nc.semaphore("swa")))
        swb = Sem(en(nc.semaphore("swb")))
        sw1 = Sem(en(nc.semaphore("sw1")))
        sw2 = Sem(en(nc.semaphore("sw2")))
        swr = Sem(en(nc.semaphore("swr")))
        sw = Sem(en(nc.semaphore("sw")))
        st = Sem(en(nc.semaphore("st")))
        sv = Sem(en(nc.semaphore("sv")))
        ss = Sem(en(nc.semaphore("ss")))
        b.auto = {"v": sv, "s": ss}
        so = Sem(en(nc.semaphore("so")))
        block = en(nc.Block())

        def wv4(m4):
            return pst[:, m4 * 512:(m4 + 1) * 512]

        for dst, src in ((idb[:], idb_d), (idf[:], idf_d), (prot[:], prot_d),
                         (oh16[:], oh16_d), (sel16[:], sel16_d),
                         (oh8[:], oh8_d), (dti[:], dt_d), (offs[:], offs_d),
                         (biasf[:, BQ:BQ + 8], bq_f_d),
                         (biasf[:, BK:BK + 8], bk_f_d),
                         (biasf[:, B1C:B1C + 32], b1_f_d),
                         (biasf[:, BS1C:BS1C + 32], bs1_f_d),
                         (biasf[:, B1MC:B1MC + 256], b1m_f_d),
                         (biasf[:, LN2C:LN2C + 8], g2_f_d),
                         (biasf[:, LN2C + 8:LN2C + 16], b2l_f_d)):
            b.dma("y", dst, src, sem=sx)
        b.dma("g", g1row[0:1, :], g1_r_d, sem=sw)
        b.dma("g", b1row[0:1, :], b1l_r_d, sem=sw)
        b.dma("g", bvrow[0:1, :], bv_r_d, sem=sw)
        b.call("g", "memset", ones_r[:], 1.0)
        b.call("g", "memset", ones2[:], 1.0, sem=sg)
        b.call("v", "memset", st1[:], 0.0)
        b.call("v", "memset", stats[:], 0.0, sem=sv)
        for g in range(NG):
            b.call("v", "memset", acc[g][:], 0.0)
            b.call("v", "memset", lsum[g][:], 0.0, sem=sv)
        b.wait("t", sw)
        b.wait("t", sg)
        b.wait("t", sx)
        for row in (g1row, b1row):
            for h4 in range(2):
                sl = slice(h4 * 512, (h4 + 1) * 512)
                b.wait("t", sv)
                b.mm(ps_x[:, 0:512], ones_r[0:1, 0:128], row[0:1, sl], sem=st)
                b.wait("v", st)
                b.vcopy(row[:, sl], ps_x[:, 0:512], sem=sv)
        b.wait("t", sv)

        lncons = [[0, 0, 0] for _ in range(4)]

        def xload(srcap, row0, slot):
            sxk = sxs[slot]
            b.wait("y", sv, lncons[slot][0])
            b.wait("y", ss, lncons[slot][1])
            b.wait("y", st, lncons[slot][2])
            return b.dma("y", xsta[:, slot * D:(slot + 1) * D],
                         srcap[row0:row0 + 128, :], sem=sxk)

        def stat_tile(srcap, pos, t):
            slot = t % 4
            xl = xload(srcap, pos, slot)
            xt = xsta[:, slot * D:(slot + 1) * D]
            b.wait("v", sxs[slot], xl)
            b.red("reduce_sum", lns[:, t:t + 1], xt, sem=sv)
            lncons[slot][0] = sv.n
            b.wait("s", sxs[slot], xl)
            b.wait("s", sv)
            b.act(xt, xt, AF.Square, accum=lns[:, 80 + t:81 + t], sem=ss)
            lncons[slot][1] = ss.n

        lns_ready = [0]

        def ln_chain(t0, t1, own_xn=False):
            b.wait("v", ss)
            n = t1 - t0
            me = lns[:, 160 + t0:160 + t1]
            va = lns[:, 240 + t0:240 + t1]
            r_ = lns[:, 320 + t0:320 + t1]
            rt = lns[:, 400 + t0:400 + t1]
            bi = lns[:, 480 + t0:480 + t1]
            b.ts(me, lns[:, t0:t1], 1.0 / D, None, ALU.mult)
            b.ts(va, lns[:, 80 + t0:80 + t1], 1.0 / D, None, ALU.mult)
            b.tt(r_, me, me, ALU.mult)
            b.tt(va, va, r_, ALU.subtract)
            b.ts(r_, va, LN_EPS, None, ALU.add, sem=sv)
            b.wait("s", sv)
            b.act(r_, r_, AF.Ln, sem=ss)
            b.wait("s", ss)
            b.act(r_, r_, AF.Exp, scale=-0.5, sem=ss)
            b.wait("v", ss)
            b.ts(rt, va, LN_EPS, None, ALU.add)
            b.call("v", "reciprocal", rt, rt)
            b.tt(rt, va, rt, ALU.mult)
            b.ts(rt, rt, LN_EPS, None, ALU.add, sem=sv)
            b.wait("s", sv)
            b.act(rt, rt, AF.Ln, sem=ss)
            b.wait("s", ss)
            b.act(rt, rt, AF.Exp, scale=-0.5, sem=ss)
            b.wait("v", ss)
            b.tt(rt, r_, rt, ALU.mult)
            b.tt(bi, me, rt, ALU.mult)
            b.ts(bi, bi, -1.0, None, ALU.mult)
            if own_xn:
                bo_ = lns[:, 560 + t0 - 64:560 + t1 - 64]
                b.tt(bo_, me, r_, ALU.mult)
                b.ts(bo_, bo_, -1.0, None, ALU.mult)
            b.call("v", "engine_nop", sem=sv)
            lns_ready[0] = sv.n

        def ln_apply(srcap, pos, tbase, mode):
            xl = [0] * 4
            for tt_ in range(4):
                xl[tt_] = xload(srcap, pos + tt_ * 128, tbase % 4 * 0 + tt_)
            for tt_ in range(4):
                t = tbase + tt_
                xt = xsta[:, tt_ * D:(tt_ + 1) * D]
                b.wait("s", sxs[tt_], xl[tt_])
                b.wait("s", sv, lns_ready[0])
                if mode == "normed":
                    sc, bi = lns[:, 400 + t:401 + t], lns[:, 480 + t:481 + t]
                else:
                    sc, bi = lns[:, 320 + t:321 + t], lns[:, 560 + t - 64:
                                                         561 + t - 64]
                b.act(ntok[:, tt_ * D:(tt_ + 1) * D], xt, AF.Identity,
                      bias=bi, scale=sc, sem=ss)
                lncons[tt_][1] = ss.n
                lncons[tt_][2] = st.n
            ln_s = ss.n
            if mode == "normed":
                b.wait("v", ss)
                for tt_ in range(4):
                    nslot = ntok[:, tt_ * D:(tt_ + 1) * D]
                    b.tt(nslot, nslot, g1row[:], ALU.mult)
                    b.tt(nslot, nslot, b1row[:], ALU.add, sem=sv)
                    lncons[tt_][0] = sv.n
            return ln_s, sv.n

        tp_state = [0, 0]

        def tpose4(dst, ln_s, ln_v):
            b.wait("t", ss, ln_s)
            b.wait("t", sv, ln_v)
            for tt_ in range(4):
                b.wait("t", sv, tp_state[0])
                last = 0
                for f in range(8):
                    last = b.mm(pst[:, 1024 + f * 128:1024 + (f + 1) * 128],
                                ntok[:, tt_ * D + f * 128:
                                     tt_ * D + (f + 1) * 128],
                                idb[:], sem=st)
                tp_state[1] = last
                lncons[tt_][2] = last
                b.wait("v", st, last)
                src3 = pst[:, 1024:2048].rearrange("p (f c) -> p f c", c=128)
                dst3 = dst.rearrange("p (f c) -> p f c", c=CS)[
                    :, :, tt_ * 128:(tt_ + 1) * 128]
                tp_state[0] = b.call("v", "tensor_copy", dst3, src3, sem=sv)
            return tp_state[0]

        bhist = []

        def stream_proj(src2_fn, rhs_fn, psum_fn, nm, nk=8, bias_mm=None,
                        qwaits=None):
            hist = bhist
            w_ = nm * 128
            for kk in range(0, nk, 4):
                u = (kk // 4) % 2
                sl = u * 2 * D
                swk = (swa, swb)[u]
                if len(hist) >= 2:
                    b.wait("g", st, hist[-2])
                wl = b.dma("g",
                           wstr[:, sl:sl + 4 * w_].rearrange(
                               "p (b c) -> p b c", b=4),
                           src2_fn(kk).rearrange("(b p) c -> p b c", p=128),
                           sem=swk)
                if kk == 0 and qwaits is not None:
                    for ws_, wv_2 in qwaits:
                        b.wait("t", ss, ws_)
                        b.wait("t", sv, wv_2)
                b.wait("t", swk, wl)
                last = None
                for bb2 in range(4):
                    k = kk + bb2
                    for m in range(nm):
                        last = b.mm(psum_fn(m),
                                    wstr[:, sl + bb2 * w_ + m * 128:
                                         sl + bb2 * w_ + (m + 1) * 128],
                                    rhs_fn(k), start=(k == 0),
                                    stop=(k == nk - 1 and bias_mm is None),
                                    sem=st)
                hist.append(last)
            if bias_mm is not None:
                bias_mm()
            return hist[-1]

        slot_hist = [(0, 0)] * 4
        rope_hist = [(0, 0), (0, 0)]

        def qk_proj_rope(is_q, dst, nfr=0):
            col0 = 0 if is_q else D
            bcol = BQ if is_q else BK
            evd = [0] * 8
            b.wait("s", sv)
            b.wait("t", sv, nfr)
            for wv in range(2):
                lastmm = stream_proj(
                    lambda kk: wqkv_d[kk * 128:(kk + 4) * 128,
                                      col0 + wv * 512:col0 + (wv + 1) * 512],
                    lambda k: nf[:, k * CS:(k + 1) * CS],
                    wv4, nm=4,
                    qwaits=[slot_hist[m4] for m4 in range(4)])
                for m4 in range(4):
                    m = wv * 4 + m4
                    b.wait("s", st, lastmm)
                    evd[m] = b.act(kpre[:, m * CS:(m + 1) * CS], wv4(m4),
                                   AF.Identity,
                                   bias=biasf[:, bcol + m:bcol + m + 1],
                                   sem=ss)
                    slot_hist[m4] = (evd[m], sv.n)
            for m in range(8):
                pc = pav[:, (m % 2) * 512:(m % 2) * 512 + 512]
                b.wait("t", ss, evd[m])
                b.wait("t", sv, max(rope_hist[m % 2][0], fl_acc[m % 2]))
                rot = b.mm(pc, prot[:], kpre[:, m * CS:(m + 1) * CS], sem=st)
                b.wait("v", st, rot)
                b.tt(tmp1[:, 0:512], kpre[:, m * CS:(m + 1) * CS],
                     ropeb[:, 0:CS], ALU.mult)
                b.tt(tmp1[:, 512:1024], pc, ropeb[:, CS:2 * CS], ALU.mult)
                rv = b.tt(dst[:, m * CS:(m + 1) * CS], tmp1[:, 0:512],
                          tmp1[:, 512:1024], ALU.add, sem=sv)
                rope_hist[m % 2] = (rv, rot)

        def norms_update(src, stcol):
            sqd = [0, 0]
            rdd = [0, 0]
            for m in range(8):
                sq = pt_a[:, (m % 2) * 512:(m % 2) * 512 + 512]
                b.wait("s", st, sqd[m % 2])
                sqe = b.act(sq, src[:, m * CS:(m + 1) * CS], AF.Square,
                            sem=ss)
                psn = (ps_l, ps_x)[m % 2]
                b.wait("t", ss, sqe)
                if m >= 2:
                    b.wait("t", sv, rdd[m % 2])
                sqd[m % 2] = b.mm(psn[0:1, 0:512], ones2[:, 0:1], sq, sem=st)
                b.wait("v", st, sqd[m % 2])
                b.red("reduce_max", st1[0:1, 8:9], psn[0:1, 0:512])
                rdd[m % 2] = b.tt(st1[0:1, stcol:stcol + 1],
                                  st1[0:1, stcol:stcol + 1],
                                  st1[0:1, 8:9], ALU.max, sem=sv)
            b.wait("t", sv)

        fl_exp = [0, 0]
        fl_avl = [0, 0, 0]
        fl_acc = [0, 0]
        fl_lev = [0]
        fl_stage = [0]

        for g in range(NG):
            for tt_ in range(4):
                stat_tile(x_own, (g * 4 + tt_) * 128, 64 + g * 4 + tt_)
        ln_chain(64, 80, own_xn=True)
        for g in range(NG):
            b.wait("y", sv)
            b.dma("y", ropeb[:, 0:CS], ropeq_d[0:128, g * CS:(g + 1) * CS],
                  sem=sxr)
            rope_ld = b.dma("y", ropeb[:, CS:2 * CS],
                            ropeq_d[0:128, TOK + g * CS: TOK + (g + 1) * CS],
                            sem=sxr)
            ln_s, ln_v = ln_apply(x_own, g * CS, 64 + g * 4, "normed")
            nfr = tpose4(nf, ln_s, ln_v)
            b.wait("v", sxr, rope_ld)
            qk_proj_rope(True, qg[g], nfr)
            norms_update(kpre, 0)
            for cc in range(4 * g, 4 * g + 4):
                for tt_ in range(4):
                    stat_tile(x_pref, cc * CS + tt_ * 128, cc * 4 + tt_)

        sched_i = 0
        ln_chain(0, 64)
        ln_pipe = ln_apply(x_pref, 0, 0, "normed")
        for c in range(NCHUNK):
            b.wait("y", sv)
            b.dma("y", ropeb[:, 0:CS], ropek_d[0:128, c * CS:(c + 1) * CS],
                  sem=sxr)
            rope_ld = b.dma("y", ropeb[:, CS:2 * CS],
                            ropek_d[0:128, S + c * CS: S + (c + 1) * CS],
                            sem=sxr)
            ln_s, ln_v = ln_pipe
            nfr = tpose4(nf, ln_s, ln_v)
            if c + 1 < NCHUNK:
                ln_pipe = ln_apply(x_pref, (c + 1) * CS, (c + 1) * 4,
                                   "normed")
            b.wait("v", sxr, rope_ld)
            qk_proj_rope(False, kbuf, nfr)
            for vh in range(2):
                b.wait("t", sv, nfr)
                for m4 in range(4):
                    b.wait("t", ss, slot_hist[m4][0])
                    b.wait("t", sv, slot_hist[m4][1])
                hist = bhist
                for kk in range(0, 8, 4):
                    u = (kk // 4) % 2
                    sl = u * 2 * D
                    swk = (swa, swb)[u]
                    if len(hist) >= 2:
                        b.wait("g", st, hist[-2])
                    wl = b.dma("g",
                               wstr[:, sl:sl + 2048].rearrange(
                                   "p (b c) -> p b c", b=4),
                               wqkv_d[kk * 128:(kk + 4) * 128,
                                      2 * D + vh * 512:
                                      2 * D + (vh + 1) * 512].rearrange(
                                   "(b p) c -> p b c", p=128), sem=swk)
                    b.wait("t", swk, wl)
                    last = None
                    for bb2 in range(4):
                        k = kk + bb2
                        for tt_ in range(4):
                            last = b.mm(wv4(tt_),
                                        nf[:, k * CS + tt_ * 128:
                                           k * CS + (tt_ + 1) * 128],
                                        wstr[:, sl + bb2 * 512:
                                             sl + bb2 * 512 + 512],
                                        start=(k == 0), stop=False, sem=st)
                    hist.append(last)
                for tt_ in range(4):
                    b.mm(wv4(tt_), ones_r[0:1, 0:128],
                         bvrow[0:1, vh * 512:(vh + 1) * 512],
                         start=False, stop=True, sem=st)
                vlast = st.n
                for tt_ in range(4):
                    b.wait("s", st, vlast)
                    ev = b.act(vbuf[:, tt_ * D + vh * 512:
                                    tt_ * D + (vh + 1) * 512],
                               wv4(tt_), AF.Identity, sem=ss)
                    slot_hist[tt_] = (ev, sv.n)
            norms_update(kpre, 1)
            b.wait("s", sv)
            b.act(st1[0:1, 2:3], st1[0:1, 0:1], AF.Ln, sem=ss)
            b.act(st1[0:1, 3:4], st1[0:1, 1:2], AF.Ln, sem=ss)
            b.wait("v", ss)
            b.tt(st1[0:1, 8:9], st1[0:1, 2:3], st1[0:1, 3:4], ALU.add)
            b.ts(st1[0:1, 8:9], st1[0:1, 8:9], 0.5, -math.log(8.0),
                 ALU.mult, ALU.add, sem=sv)
            b.wait("s", sv)
            b.act(st1[0:1, 5:6], st1[0:1, 8:9], AF.Exp, sem=ss)
            b.wait("v", ss)
            b.vcopy(pt_a[0:1, 0:1], st1[0:1, 5:6])
            b.vcopy(st1[0:1, 5:6], pt_a[0:1, 0:1])
            b.tt(st1[0:1, 8:9], st1[0:1, 4:5], st1[0:1, 5:6], ALU.subtract)
            b.ts(st1[0:1, 7:8], st1[0:1, 5:6], -1.0, None, ALU.mult, sem=sv)
            b.wait("s", sv)
            b.act(st1[0:1, 6:7], st1[0:1, 8:9], AF.Exp, sem=ss)
            b.wait("v", ss)
            b.vcopy(st1[0:1, 4:5], st1[0:1, 5:6])
            b.vcopy(pt_a[0:1, 0:2], st1[0:1, 6:8], sem=sv)
            b.wait("t", sv)
            b.wait("t", ss)
            b.mm(ps_x[:, 0:2], ones_r[0:1, 0:128], pt_a[0:1, 0:2], sem=st)
            b.wait("v", st)
            b.wait("v", ss)
            b.vcopy(stats[:, 1:2], ps_x[:, 0:1])
            stats_rdy = b.vcopy(stats[:, 0:1], ps_x[:, 1:2], sem=sv)
            kv_rdy_v = sv.n
            kv_rdy_s = ss.n

            first_of_chunk = True
            while sched_i < NSTEP and SCHED[sched_i][0] == c:
                _, g = SCHED[sched_i]
                step = sched_i
                sched_i += 1
                candidate = (4 * g <= c)
                cmp_done = 0
                if candidate:
                    for kt in range(4):
                        cmp_done = b.ts(cmpb[:, kt * CS:(kt + 1) * CS],
                                        dti[:, kt * CS:(kt + 1) * CS],
                                        offs[:, step:step + 1], None,
                                        ALU.is_le, sem=sv)
                first_l = True

                def emit_avl(p):
                    nonlocal first_l
                    pt3, rdy, rdy_sem, hp_, kt_, g_, last = p
                    ptb = (pt_a, pt_b, pt_c)[pt3]
                    b.wait("t", rdy_sem, rdy)
                    if first_l:
                        b.wait("t", sv, fl_lev[0])
                    b.mm(ps_l[0:16, 0:512],
                         oh16[:, 16 * (2 * hp_):16 * (2 * hp_) + 16],
                         ptb[:, 0:512], start=first_l, stop=False)
                    b.mm(ps_l[0:16, 0:512],
                         oh16[:, 16 * (2 * hp_ + 1):16 * (2 * hp_ + 1) + 16],
                         ptb[:, 512:1024], start=False, stop=last)
                    first_l = False
                    if kt_ == 0:
                        b.wait("t", sv, max(fl_acc[0], fl_acc[1],
                                            rope_hist[0][0],
                                            rope_hist[1][0]))
                    b.mm(pav[0:64, 0:512],
                         vbuf[:, kt_ * D + hp_ * 128:kt_ * D + hp_ * 128 + 64],
                         ptb[:, 0:512], tp=(0, 0),
                         start=(kt_ == 0), stop=(kt_ == 3))
                    av = b.mm(pav[64:128, 512:1024],
                              vbuf[:, kt_ * D + hp_ * 128 + 64:
                                   kt_ * D + hp_ * 128 + 128],
                              ptb[:, 512:1024], tp=(0, 64),
                              start=(kt_ == 0), stop=(kt_ == 3), sem=st)
                    fl_avl[pt3] = av
                    if kt_ == 3:
                        b.wait("v", st, av)
                        aslice = acc[g_][:, hp_ * CS:(hp_ + 1) * CS]
                        fl_acc[0] = b.stt(aslice[0:64, :], aslice[0:64, :],
                                          stats[0:64, 1:2], pav[0:64, 0:512],
                                          ALU.mult, ALU.add, sem=sv)
                        fl_acc[1] = b.stt(aslice[64:128, :],
                                          aslice[64:128, :],
                                          stats[64:128, 1:2],
                                          pav[64:128, 512:1024],
                                          ALU.mult, ALU.add, sem=sv)

                pending = []
                for hp in range(8):
                    for kt in range(4):
                        sl_ = fl_stage[0] % 2
                        pt3 = fl_stage[0] % 3
                        fl_stage[0] += 1
                        pslot = pst[:, sl_ * 1024:(sl_ + 1) * 1024]
                        ptb = (pt_a, pt_b, pt_c)[pt3]
                        kc = hp * CS + kt * 128
                        b.wait("t", ss, fl_exp[sl_])
                        if first_of_chunk:
                            b.wait("t", sv, kv_rdy_v)
                            b.wait("t", ss, kv_rdy_s)
                            first_of_chunk = False
                        b.mm(pslot[:, 0:512], kbuf[0:64, kc:kc + 128],
                             qg[g][0:64, hp * CS:(hp + 1) * CS], tp=(0, 0))
                        p2 = b.mm(pslot[:, 512:1024],
                                  kbuf[64:128, kc:kc + 128],
                                  qg[g][64:128, hp * CS:(hp + 1) * CS],
                                  tp=(64, 0), sem=st)
                        b.wait("s", st, p2)
                        b.wait("s", st, fl_avl[pt3])
                        ex = b.act(ptb[:, 0:1024], pslot[:, 0:1024], AF.Exp,
                                   bias=stats[:, 0:1], sem=ss)
                        fl_exp[sl_] = ex
                        if candidate:
                            b.wait("v", ss, ex)
                            b.tt(ptb[:, 0:512], ptb[:, 0:512],
                                 cmpb[:, kt * CS:(kt + 1) * CS], ALU.mult)
                            mk = b.tt(ptb[:, 512:1024], ptb[:, 512:1024],
                                      cmpb[:, kt * CS:(kt + 1) * CS],
                                      ALU.mult, sem=sv)
                            rdy, rdy_sem = mk, sv
                        else:
                            rdy, rdy_sem = ex, ss
                        if len(pending) >= 2:
                            emit_avl(pending.pop(0))
                        pending.append((pt3, rdy, rdy_sem, hp, kt, g,
                                        hp == 7 and kt == 3))
                for p_ in pending:
                    emit_avl(p_)
                pending = []
                b.wait("v", st)
                fl_lev[0] = b.stt(lsum[g][0:16, :], lsum[g][0:16, :],
                                  stats[0:16, 1:2], ps_l[0:16, 0:512],
                                  ALU.mult, ALU.add, sem=sv)
                if c == 4 * g + 3:
                    b.call("v", "reciprocal", lsum[g][0:16, :],
                           lsum[g][0:16, :])
                    b.vcopy(pt_a[0:16, 0:512], lsum[g][0:16, :], sem=sv)
                    b.wait("t", sv)
                    for hp in range(8):
                        pc = pav[:, (hp % 2) * 512:(hp % 2) * 512 + 512]
                        b.wait("t", sv)
                        b.mm(pc, sel16[:, hp * 128:(hp + 1) * 128],
                             pt_a[0:16, 0:512], sem=st)
                        b.wait("v", st)
                        b.tt(qg[g][:, hp * CS:(hp + 1) * CS],
                             acc[g][:, hp * CS:(hp + 1) * CS], pc, ALU.mult,
                             sem=sv)
                    b.wait("t", sv)
                    fl_exp[0] = ss.n
                    fl_exp[1] = ss.n
                    fl_avl[0] = st.n
                    fl_avl[1] = st.n
                    fl_avl[2] = st.n
                    fl_acc[0] = sv.n
                    fl_acc[1] = sv.n
                    rope_hist[0] = (sv.n, st.n)
                    rope_hist[1] = (sv.n, st.n)

        def racc(f, g):
            return acc[g][:, f * CS:(f + 1) * CS]

        xnf = [kbuf, vbuf, kpre, nf]

        def x1t(k):
            if k < 2:
                return pt_a[:, k * 512:(k + 1) * 512]
            if k < 4:
                return pt_b[:, (k - 2) * 512:(k - 2) * 512 + 512]
            return cmpb[:, (k - 4) * 512:(k - 4) * 512 + 512]

        b.wait("g", st)
        brow_ld = b.dma("g", brow[0:1, :], bo_r_d, sem=swr)
        for g in range(NG):
            for tt_ in range(4):
                sl = tt_ * D
                sxk = sxs[tt_]
                t_ = 64 + g * 4 + tt_
                xl = xload(x_own, g * CS + tt_ * 128, tt_)
                b.wait("s", sxs[tt_], xl)
                b.wait("s", sv, lns_ready[0])
                b.act(ntok[:, tt_ * D:(tt_ + 1) * D], xsta[:, sl:sl + D],
                      AF.Identity, bias=lns[:, 560 + t_ - 64:561 + t_ - 64],
                      scale=lns[:, 320 + t_:321 + t_], sem=ss)
                lncons[tt_][1] = ss.n
                b.wait("t", sxs[tt_], xl)
                b.wait("t", ss)
                b.wait("t", sv)
                for f in range(8):
                    b.mm(pst[:, f * 128:(f + 1) * 128],
                         xsta[:, sl + f * 128: sl + (f + 1) * 128], idf[:],
                         sem=None)
                for f in range(8):
                    b.mm(pst[:, 1024 + f * 128:1024 + (f + 1) * 128],
                         ntok[:, tt_ * D + f * 128: tt_ * D + (f + 1) * 128],
                         idb[:], sem=(st if f == 7 else None))
                b.wait("v", st)
                xn3 = xnf[g].rearrange("p (f c) -> p f c", c=CS)[
                    :, :, tt_ * 128:(tt_ + 1) * 128]
                b.vcopy(xn3, pst[:, 1024:2048].rearrange(
                    "p (f c) -> p f c", c=128))
                ac3 = acc[g].rearrange("p (f c) -> p f c", c=CS)[
                    :, :, tt_ * 128:(tt_ + 1) * 128]
                b.tt(ac3, pst[:, 0:1024].rearrange("p (f c) -> p f c", c=128),
                     xn3, ALU.add, sem=sv)
                lncons[tt_] = [sv.n, ss.n, st.n]
                b.wait("t", sv)
            b.wait("t", ss)
            b.wait("t", swr, brow_ld)
            for wv_ in range(2):
                def bias_mm(wv2=wv_):
                    for m4 in range(4):
                        m = wv2 * 4 + m4
                        b.mm(wv4(m4), brow[0:1, m * 128:(m + 1) * 128],
                             ones_r[0:1, 0:512], start=False, stop=True,
                             sem=st)
                b.wait("t", sv)
                stream_proj(lambda kk, w2=wv_: wo_d[kk * 128:(kk + 4) * 128,
                                                    w2 * 512:(w2 + 1) * 512],
                            lambda k: qg[g][:, k * CS:(k + 1) * CS],
                            wv4, nm=4, bias_mm=bias_mm)
                b.wait("v", st)
                for m4 in range(4):
                    m = wv_ * 4 + m4
                    b.tt(racc(m, g), racc(m, g), wv4(m4), ALU.add)
                    b.tt(x1t(m), xnf[g][:, m * CS:(m + 1) * CS], wv4(m4),
                         ALU.add, sem=sv)
            if DEV_STAGE >= 2:
                b.wait("t", sv)
                sqb = ntok
                for f in range(8):
                    b.mm(pst[:, 0:512], ones2[:], x1t(f), start=(f == 0),
                         stop=(f == 7), sem=st)
                sqmm = [0, 0]
                for f in range(8):
                    sq = sqb[:, (f % 2) * 512:(f % 2) * 512 + 512]
                    if f >= 2:
                        b.wait("v", st, sqmm[f % 2])
                    b.tt(sq, x1t(f), x1t(f), ALU.mult, sem=sv)
                    b.wait("t", sv)
                    sqmm[f % 2] = b.mm(pst[:, 512:1024], ones2[:], sq,
                                       start=(f == 0), stop=(f == 7), sem=st)
                b.wait("v", st)
                mb = tmp1[:, 0:512]
                rb = tmp1[:, 512:1024]
                b.ts(mb, pst[:, 0:512], 1.0 / D, None, ALU.mult)
                b.ts(rb, pst[:, 512:1024], 1.0 / D, None, ALU.mult)
                b.tt(xsta[:, 0:512], mb, mb, ALU.mult)
                b.tt(rb, rb, xsta[:, 0:512], ALU.subtract)
                b.ts(rb, rb, LN_EPS, None, ALU.add, sem=sv)
                b.wait("s", sv)
                b.act(rb, rb, AF.Ln, sem=ss)
                b.wait("s", ss)
                b.act(rb, rb, AF.Exp, scale=-0.5, sem=ss)
                b.wait("v", ss)
                for f in range(8):
                    qsl = qg[g][:, f * CS:(f + 1) * CS]
                    b.tt(qsl, x1t(f), mb, ALU.subtract)
                    b.tt(qsl, qsl, rb, ALU.mult)
                    b.ts(qsl, qsl,
                         biasf[:, LN2C + f:LN2C + f + 1],
                         biasf[:, LN2C + 8 + f:LN2C + 8 + f + 1],
                         ALU.mult, ALU.add, sem=sv)
                for tt_ in range(4):
                    lncons[tt_] = [sv.n, ss.n, st.n]
                b.wait("t", sv)

        ffn_ld = [0, 0]

        def ffn_phase(rhs_fn, w1_src, w2_src, b1col, actfn, b2rhs_fn,
                      evac, Fp, prescale=False):
            nf_ = Fp // 128
            npiece = 4096 // Fp

            def w1sl(k):
                if Fp == 512:
                    return ffn_bufs[1][1][:, k * Fp:(k + 1) * Fp]
                buf = ffn_bufs[1][1] if k < 4 else ffn_bufs[2][1]
                return buf[:, (k % 4) * Fp:(k % 4 + 1) * Fp]

            def w2sl(kt):
                if nf_ <= 4:
                    return ffn_bufs[1][2][:, kt * 1024:(kt + 1) * 1024]
                buf = ffn_bufs[1][2] if kt < 4 else ffn_bufs[2][2]
                return buf[:, (kt % 4) * 1024:(kt % 4 + 1) * 1024]

            def asl(f, g4):
                buf = ffn_bufs[1][0] if f < 4 else ffn_bufs[2][0]
                return buf[:, (f % 4) * TOK + g4 * CS:
                           (f % 4) * TOK + (g4 + 1) * CS]

            cwbuf = ffn_bufs[2][4] if 2 in ffn_bufs else None
            for p in range(npiece):
                b.wait("g", st, ffn_ld[0])
                for k in range(8):
                    b.dma("g", w1sl(k), w1_src(k, p), sem=sw1)
                w1_ld = sw1.n
                b.wait("g", st, ffn_ld[1])
                for kt in range(nf_):
                    b.dma("g", w2sl(kt), w2_src(kt, p), sem=sw2)
                w2_ld = sw2.n
                b.wait("t", sw1, w1_ld)
                for f in range(nf_):
                    for g4 in range(4):
                        sl = (f * 4 + g4) % 4
                        b.wait("t", ss, slot_hist[sl][0])
                        b.wait("t", sv, slot_hist[sl][1])
                        for k in range(8):
                            b.mm(wv4(sl),
                                 w1sl(k)[:, f * 128:(f + 1) * 128],
                                 rhs_fn(g4, k), start=(k == 0), stop=(k == 7),
                                 sem=(st if k == 7 else None))
                        b.wait("s", st)
                        if f == nf_ - 1 and g4 == 3:
                            ffn_ld[0] = st.n
                        aslice = asl(f, g4)
                        sev = b.act(aslice, wv4(sl), actfn,
                                    bias=b1col(p, f), sem=ss)
                        vev = sv.n
                        if prescale:
                            b.wait("v", ss)
                            vev = b.tt(aslice, aslice,
                                       cwbuf[:, g4 * CS:(g4 + 1) * CS],
                                       ALU.mult, sem=sv)
                        slot_hist[sl] = (sev, vev)
                b.wait("t", sw2, w2_ld)
                b.wait("t", ss)
                if prescale:
                    b.wait("t", sv)
                for m in range(8):
                    for g4 in range(4):
                        sl = (m * 4 + g4) % 4
                        b.wait("t", ss, slot_hist[sl][0])
                        b.wait("t", sv, slot_hist[sl][1])
                        last_bias = (p == 0 and b2rhs_fn)
                        for kt in range(nf_):
                            b.mm(wv4(sl),
                                 w2sl(kt)[:, m * 128:(m + 1) * 128],
                                 asl(kt, g4),
                                 start=(kt == 0),
                                 stop=(kt == nf_ - 1 and not last_bias),
                                 sem=(None if (kt < nf_ - 1 or last_bias)
                                      else st))
                        if last_bias:
                            b.mm(wv4(sl), brow[0:1, m * 128:(m + 1) * 128],
                                 b2rhs_fn(g4), start=False, stop=True,
                                 sem=st)
                        b.wait("v", st)
                        if m == 7 and g4 == 3:
                            ffn_ld[1] = st.n
                        vev = evac(m, g4, wv4(sl))
                        slot_hist[sl] = (ss.n, vev)

        bstack.close()

        if DEV_STAGE >= 2:
            ffn_alloc(1)
            b.wait("g", st)
            b.wait("g", sv)
            rl = b.dma("g", brow[0:1, :], b2_r_d, sem=swr)
            b.wait("t", swr, rl)

            def mlp_evac(m, g4, ps):
                return b.tt(racc(m, g4), racc(m, g4), ps, ALU.add, sem=sv)

            ffn_phase(lambda g4, k: qg[g4][:, k * CS:(k + 1) * CS],
                      lambda k, p: w1_d[k * 128:(k + 1) * 128,
                                        p * 512:(p + 1) * 512],
                      lambda kt, p: w2_d[p * 512 + kt * 128:
                                         p * 512 + (kt + 1) * 128, 0:1024],
                      lambda p, f: biasf[:, B1C + p * 4 + f:
                                         B1C + p * 4 + f + 1],
                      AF.Gelu,
                      lambda g4: ones_r[0:1, 0:512],
                      mlp_evac, Fp=512)

        dstack.close()

        if DEV_STAGE >= 3:
            ffn_alloc(1)
            ffn_alloc(2)
            cwT = ffn_bufs[2][3]
            cwb = ffn_bufs[2][4]
            wgb = ffn_bufs[2][5]
            b.wait("g", st)
            for k in range(8):
                rl = b.dma("g", wgb[:, k * 8:(k + 1) * 8],
                           wg_d[k * 128:(k + 1) * 128, :], sem=swr)
            b.wait("t", swr, rl)
            b.wait("t", sv)
            b.wait("t", ss)
            for t16 in range(16):
                g4, tt_ = divmod(t16, 4)
                for k in range(8):
                    b.mm(ps_l[:, t16 * 8:(t16 + 1) * 8],
                         xnf[g4][:, k * CS + tt_ * 128:
                                 k * CS + (tt_ + 1) * 128],
                         wgb[:, k * 8:(k + 1) * 8], start=(k == 0),
                         stop=(k == 7), sem=(st if k == 7 else None))
            b.wait("s", st)
            b.act(tmp1[:, 0:128], ps_l[:, 0:128], AF.Exp, sem=ss)
            b.wait("v", ss)
            for t16 in range(16):
                e8 = tmp1[:, t16 * 8:(t16 + 1) * 8]
                scr = tmp1[:, 128:136]
                scrb = tmp1[:, 136:144]
                b.red("reduce_sum", stats[:, 2:3], e8)
                b.red("reduce_max", stats[:, 3:4], e8)
                b.ts(scr, e8, stats[:, 3:4], None, ALU.is_ge)
                b.stt(scrb, scr, -1e30, e8, ALU.mult, ALU.add)
                b.red("reduce_max", stats[:, 4:5], scrb)
                b.ts(scr, e8, stats[:, 4:5], None, ALU.is_ge)
                b.tt(stats[:, 5:6], stats[:, 3:4], stats[:, 4:5], ALU.add)
                b.stt(stats[:, 5:6], stats[:, 2:3], 1e-9, stats[:, 5:6],
                      ALU.mult, ALU.add)
                b.call("v", "reciprocal", stats[:, 6:7], stats[:, 5:6])
                b.tt(scr, e8, scr, ALU.mult)
                b.ts(ntok[:, t16 * 8:(t16 + 1) * 8], scr, stats[:, 6:7],
                     None, ALU.mult, sem=sv)
                b.wait("t", sv)
                b.mm(ps_x[0:8, (t16 % 2) * 128:(t16 % 2) * 128 + 128],
                     ntok[:, t16 * 8:(t16 + 1) * 8], idb[:], sem=st)
                b.wait("v", st)
                b.vcopy(cwT[0:8, t16 * 128:(t16 + 1) * 128],
                        ps_x[0:8, (t16 % 2) * 128:(t16 % 2) * 128 + 128],
                        sem=sv)
            for e in range(8):
                b.wait("t", sv)
                b.wait("t", ss)
                for g4 in range(4):
                    b.wait("t", ss)
                    b.mm(pav[:, (g4 % 2) * 512:(g4 % 2) * 512 + 512],
                         oh8[:, e * 128:(e + 1) * 128],
                         cwT[0:8, g4 * 512:(g4 + 1) * 512], sem=st)
                    b.wait("s", st)
                    b.wait("s", sv)
                    b.act(cwb[:, g4 * CS:(g4 + 1) * CS],
                          pav[:, (g4 % 2) * 512:(g4 % 2) * 512 + 512],
                          AF.Identity, sem=ss)
                b.wait("g", ss)
                b.wait("g", st)
                rl = b.dma("g", brow[0:1, :],
                           b2m_r_d[0:1, e * D:(e + 1) * D], sem=swr)
                b.wait("t", swr, rl)
                b.wait("t", ss)

                def moe_evac(m, g4, ps):
                    return b.tt(racc(m, g4), racc(m, g4), ps, ALU.add,
                                sem=sv)

                ffn_phase(lambda g4, k: xnf[g4][:, k * CS:(k + 1) * CS],
                          lambda k, p, e2=e: mw1_d[e2][k * 128:(k + 1) * 128,
                                                       p * 1024:(p + 1) * 1024],
                          lambda kt, p, e2=e: mw2_d[e2][p * 1024 + kt * 128:
                                                        p * 1024 + (kt + 1) * 128,
                                                        0:1024],
                          lambda p, f, e2=e: biasf[:, B1MC + e2 * 32 + p * 8 + f:
                                                   B1MC + e2 * 32 + p * 8 + f + 1],
                          AF.Silu,
                          lambda g4: cwb[0:1, g4 * CS:(g4 + 1) * CS],
                          moe_evac, Fp=1024, prescale=True)

        if DEV_STAGE >= 4:
            b.wait("g", st)
            b.wait("g", sv)
            rl = b.dma("g", brow[0:1, :], bs2_r_d, sem=swr)
            b.wait("t", swr, rl)

            def sh_evac(m, g4, ps):
                return b.stt(racc(m, g4), ps, 0.1, racc(m, g4), ALU.mult,
                             ALU.add, sem=sv)

            ffn_phase(lambda g4, k: xnf[g4][:, k * CS:(k + 1) * CS],
                      lambda k, p: ws1_d[k * 128:(k + 1) * 128,
                                         p * 1024:(p + 1) * 1024],
                      lambda kt, p: ws2_d[p * 1024 + kt * 128:
                                          p * 1024 + (kt + 1) * 128, 0:1024],
                      lambda p, f: biasf[:, BS1C + p * 8 + f:
                                         BS1C + p * 8 + f + 1],
                      AF.Silu,
                      lambda g4: ones_r[0:1, 0:512],
                      sh_evac, Fp=1024)

        for g in range(NG):
            for tt_ in range(4):
                b.wait("t", sv)
                b.wait("t", ss)
                for f in range(8):
                    b.mm(pst[:, f * 128:(f + 1) * 128],
                         racc(f, g)[:, tt_ * 128:(tt_ + 1) * 128], idf[:],
                         sem=(st if f == 7 else None))
                b.wait("v", st)
                b.wait("v", so)
                b.vcopy(tmp1[:, 0:1024], pst[:, 0:1024], sem=sv)
                b.wait("y", sv)
                row0 = g * CS + tt_ * 128
                b.dma("y", out_d[row0:row0 + 128, :], tmp1[:, 0:1024], sem=so)
        b.wait("y", so)

        @block.tensor
        def _(E):
            for fn in b.q["t"]:
                fn(E)

        @block.vector
        def _(E):
            for fn in b.q["v"]:
                fn(E)

        @block.scalar
        def _(E):
            for fn in b.q["s"]:
                fn(E)

        @block.gpsimd
        def _(E):
            for fn in b.q["g"]:
                fn(E)

        @block.sync
        def _(E):
            for fn in b.q["y"]:
                fn(E)

    return nc


def _host_constants():
    inv = 1.0 / (10000.0 ** (np.arange(0, HD, 2, dtype=np.float64) / HD))
    t = np.arange(S, dtype=np.float64)
    fr = t[:, None] * inv[None, :]
    emb = np.concatenate([fr, fr], axis=1)
    cos = np.cos(emb).astype(np.float32)
    sin = np.sin(emb).astype(np.float32)
    cosf = np.concatenate([cos.T, cos.T], axis=0)
    sinf = np.concatenate([sin.T, sin.T], axis=0)
    iden = np.eye(128, dtype=np.float32)
    prot = np.zeros((128, 128), dtype=np.float32)
    for blk in range(2):
        o = blk * 64
        for m in range(32):
            prot[o + m + 32, o + m] = -1.0
            prot[o + m, o + m + 32] = 1.0
    oh16 = np.zeros((128, 16 * H), dtype=np.float32)
    for h in range(H):
        oh16[:, 16 * h + h] = 1.0
    sel16 = np.zeros((16, 128 * (H // 2)), dtype=np.float32)
    for hp in range(H // 2):
        sel16[2 * hp, hp * 128: hp * 128 + 64] = 1.0
        sel16[2 * hp + 1, hp * 128 + 64: hp * 128 + 128] = 1.0
    oh8 = np.zeros((8, 8 * 128), dtype=np.float32)
    for e in range(8):
        oh8[e, e * 128:(e + 1) * 128] = 1.0
    dt4 = np.zeros((128, 4 * CS), dtype=np.int16)
    for kt in range(4):
        p = np.arange(128)[:, None]
        q = np.arange(CS)[None, :]
        dt4[:, kt * CS:(kt + 1) * CS] = (128 * kt + p - q).astype(np.int16)
    return cosf, sinf, iden, prot, oh16, sel16, oh8, dt4


_PROG_CACHE = {}


def kernel(**inp):
    cosf, sinf, iden, prot, oh16, sel16, oh8, dt4 = _host_constants()
    x = np.asarray(inp["L0_attn_x"], dtype=np.float32)
    wqkv = np.ascontiguousarray(inp["L0_attn_Wqkv"], dtype=np.float32)
    bqkv = np.asarray(inp["L0_attn_bqkv"], dtype=np.float32)

    def bf(a):
        return np.ascontiguousarray(a).astype(BF)

    common = {
        "offs_h": None,
        "ropek": bf(np.concatenate([cosf, sinf], axis=1)),
        "dtiles": dt4,
        "idb_h": bf(iden), "idf_h": iden, "prot_h": bf(prot),
        "oh16_h": bf(oh16), "sel16_h": bf(sel16), "oh8_h": bf(oh8),
        "Wqkv": wqkv,
        "Wo": np.ascontiguousarray(inp["L0_attn_Wo"], dtype=np.float32),
        "W1": np.ascontiguousarray(inp["L0_attn_W1"], dtype=np.float32),
        "W2": np.ascontiguousarray(inp["L0_attn_W2"], dtype=np.float32),
        "Wg": np.ascontiguousarray(inp["L0_ffn_Wg"], dtype=np.float32),
        "MW1": np.ascontiguousarray(inp["L0_ffn_W1"], dtype=np.float32),
        "MW2": np.ascontiguousarray(inp["L0_ffn_W2"], dtype=np.float32),
        "Ws1": np.ascontiguousarray(inp["L0_ffn_Ws1"], dtype=np.float32),
        "Ws2": np.ascontiguousarray(inp["L0_ffn_Ws2"], dtype=np.float32),
        "bq_f": np.ascontiguousarray(bqkv[0:D].reshape(8, 128).T),
        "bk_f": np.ascontiguousarray(bqkv[D:2 * D].reshape(8, 128).T),
        "b1_f": np.ascontiguousarray(
            np.asarray(inp["L0_attn_b1"], np.float32).reshape(32, 128).T),
        "bs1_f": np.ascontiguousarray(
            np.asarray(inp["L0_ffn_bs1"], np.float32).reshape(32, 128).T),
        "b1m_f": np.ascontiguousarray(
            np.asarray(inp["L0_ffn_b1"], np.float32).reshape(8, 32, 128)
            .transpose(2, 0, 1).reshape(128, 256)),
        "g2_f": np.ascontiguousarray(
            np.asarray(inp["L0_attn_ln2_g"], np.float32).reshape(8, 128).T),
        "b2l_f": np.ascontiguousarray(
            np.asarray(inp["L0_attn_ln2_b"], np.float32).reshape(8, 128).T),
        "g1_r": np.asarray(inp["L0_attn_ln1_g"], np.float32)[None, :],
        "b1l_r": np.asarray(inp["L0_attn_ln1_b"], np.float32)[None, :],
        "bv_r": bqkv[2 * D:3 * D][None, :],
        "bo_r": np.asarray(inp["L0_attn_bo"], np.float32)[None, :],
        "b2_r": np.asarray(inp["L0_attn_b2"], np.float32)[None, :],
        "bs2_r": np.asarray(inp["L0_ffn_bs2"], np.float32)[None, :],
        "b2m_r": np.ascontiguousarray(
            np.asarray(inp["L0_ffn_b2"], np.float32).reshape(1, 8 * D)),
    }

    in_maps = []
    for core in range(8):
        bb, j = divmod(core, 4)
        chunks = [4 * g + j for g in range(NG)]
        m = dict(common)
        m["x_pref"] = x[bb]
        m["x_own"] = np.ascontiguousarray(
            np.concatenate([x[bb, ch * CS:(ch + 1) * CS] for ch in chunks]))
        offs = np.zeros((128, NSTEP), dtype=np.float32)
        for si, (c, g) in enumerate(SCHED):
            offs[:, si] = (4 * g + j - c) * CS
        m["offs_h"] = offs
        cols = np.concatenate(
            [np.arange(ch * CS, (ch + 1) * CS) for ch in chunks])
        m["ropeq"] = bf(np.concatenate(
            [cosf[:, cols] / 8.0, sinf[:, cols] / 8.0], axis=1))
        in_maps.append(m)

    key = DEV_STAGE
    if key not in _PROG_CACHE:
        _PROG_CACHE[key] = build_program()
    nc = _PROG_CACHE[key]
    res = run_bass_kernel_spmd(nc, in_maps, core_ids=list(range(8)))
    out = np.zeros((B, S, D), dtype=np.float32)
    for core in range(8):
        bb, j = divmod(core, 4)
        r = res.results[core]["out"]
        for g in range(NG):
            ch = 4 * g + j
            out[bb, ch * CS:(ch + 1) * CS] = r[g * CS:(g + 1) * CS]
    return out


# version 19
# speedup vs baseline: 1.0293x; 1.0293x over previous
import math
import numpy as np
import ml_dtypes
import concourse.bass as bass
import concourse.mybir as mybir
from concourse.bass_utils import run_bass_kernel_spmd

F32 = mybir.dt.float32
BF16 = mybir.dt.bfloat16
I16 = mybir.dt.int16
AF = mybir.ActivationFunctionType
ALU = mybir.AluOpType
AX = mybir.AxisListType
BF = ml_dtypes.bfloat16

B, S, D = 2, 8192, 1024
H, HD = 16, 64
NCHUNK = 16
CS = 512
NG = 4
TOK = NG * CS
LN_EPS = 1e-5
MASKVAL = -30000.0
DEV_STAGE = 4

SCHED = [(c, g) for c in range(NCHUNK) for g in range(NG) if 4 * g + 3 >= c]
NSTEP = len(SCHED)


class Sem:
    __slots__ = ("h", "n")

    def __init__(self, h):
        self.h = h
        self.n = 0


class KB:
    def __init__(self):
        self.q = {k: [] for k in "tvsgy"}
        self.auto = {}

    def on(self, e, fn):
        self.q[e].append(fn)

    def wait(self, e, sem, val=None):
        v = sem.n if val is None else val
        if v > 0:
            self.on(e, lambda E, h=sem.h, vv=v: E.wait_ge(h, vv))

    def call(self, e, meth, *args, sem=None, inc=1, **kw):
        if e in self.auto and meth != "dma_start":
            own = self.auto[e]
            if own.n > 0:
                self.on(e, lambda E, h=own.h, vv=own.n: E.wait_ge(h, vv))
            assert sem is None or sem is own
            sem = own
        if sem is not None:
            sem.n += inc

        def fn(E, meth=meth, args=args, kw=kw, sem=sem, inc=inc):
            ins = getattr(E, meth)(*args, **kw)
            if sem is not None:
                ins.then_inc(sem.h, inc)

        self.on(e, fn)
        return sem.n if sem is not None else None

    def dma(self, e, out, in_, sem=None, **kw):
        return self.call(e, "dma_start", out, in_, sem=sem, inc=16, **kw)

    def mm(self, out, lhsT, rhs, start=True, stop=True, sem=None, tp=None):
        kw = {}
        if tp is not None:
            kw["tile_position"] = tp
        return self.call("t", "matmul", out, lhsT, rhs, start=start, stop=stop,
                         sem=sem, **kw)

    def tpose(self, out, in_, iden, sem=None):
        return self.call("t", "transpose", out, in_, iden, sem=sem)

    def act(self, out, in_, func, bias=0.0, scale=1.0, accum=None, sem=None):
        kw = {}
        if accum is not None:
            kw["accum_out"] = accum
        return self.call("s", "activation", out, in_, func, bias=bias,
                         scale=scale, sem=sem, **kw)

    def tt(self, out, a, c, op, sem=None):
        return self.call("v", "tensor_tensor", out, a, c, op, sem=sem)

    def ts(self, out, a, s1, s2, op0, op1=ALU.bypass, sem=None):
        return self.call("v", "tensor_scalar", out, a, s1, s2, op0, op1,
                         sem=sem)

    def stt(self, out, in0, scalar, in1, op0, op1, sem=None):
        return self.call("v", "scalar_tensor_tensor", out, in0, scalar, in1,
                         op0, op1, sem=sem)

    def red(self, kind, out, in_, sem=None, **kw):
        return self.call("v", kind, out, in_, AX.X, sem=sem, **kw)

    def vcopy(self, out, in_, sem=None):
        return self.call("v", "tensor_copy", out, in_, sem=sem)


def build_program():
    nc = bass.Bass("TRN2", target_bir_lowering=False)

    def din(name, shape, dt=F32):
        return nc.dram_tensor(name, shape, dt, kind="ExternalInput").ap()

    x_own = din("x_own", [TOK, D])
    x_pref = din("x_pref", [S, D])
    offs_d = din("offs_h", [128, NSTEP])
    ropeq_d = din("ropeq", [128, 2 * TOK], BF16)
    ropek_d = din("ropek", [128, 2 * S], BF16)
    dt_d = din("dtiles", [128, 4 * CS], I16)
    idb_d = din("idb_h", [128, 128], BF16)
    idf_d = din("idf_h", [128, 128])
    prot_d = din("prot_h", [128, 128], BF16)
    oh16_d = din("oh16_h", [128, 16 * H], BF16)
    sel16_d = din("sel16_h", [16, 128 * (H // 2)], BF16)
    oh8_d = din("oh8_h", [8, 8 * 128], BF16)
    wqkv_d = din("Wqkv", [D, 3 * D])
    wo_d = din("Wo", [D, D])
    w1_d = din("W1", [D, 4 * D])
    w2_d = din("W2", [4 * D, D])
    wg_d = din("Wg", [D, 8])
    mw1_d = din("MW1", [8, D, 4 * D])
    mw2_d = din("MW2", [8, 4 * D, D])
    ws1_d = din("Ws1", [D, 4 * D])
    ws2_d = din("Ws2", [4 * D, D])
    bq_f_d = din("bq_f", [128, 8])
    bk_f_d = din("bk_f", [128, 8])
    b1_f_d = din("b1_f", [128, 32])
    bs1_f_d = din("bs1_f", [128, 32])
    b1m_f_d = din("b1m_f", [128, 8 * 32])
    g2_f_d = din("g2_f", [128, 8])
    b2l_f_d = din("b2l_f", [128, 8])
    g1_r_d = din("g1_r", [1, D])
    b1l_r_d = din("b1l_r", [1, D])
    bv_r_d = din("bv_r", [1, D])
    bo_r_d = din("bo_r", [1, D])
    b2_r_d = din("b2_r", [1, D])
    bs2_r_d = din("bs2_r", [1, D])
    b2m_r_d = din("b2m_r", [1, 8 * D])
    out_d = nc.dram_tensor("out", [TOK, D], F32, kind="ExternalOutput").ap()

    b = KB()
    from contextlib import ExitStack
    with ExitStack() as stack:
        en = stack.enter_context

        def sb(name, shape, dt):
            return en(nc.sbuf_tensor(name, shape, dt))

        idb = sb("idb", [128, 128], BF16)
        idf = sb("idf", [128, 128], F32)
        oh8 = sb("oh8", [8, 8 * 128], BF16)
        ones_r = sb("ones_r", [1, CS], BF16)
        ones2 = sb("ones2", [128, 128], BF16)
        biasf = sb("biasf", [128, 352], F32)
        BQ, BK, B1C, BS1C, B1MC, LN2C = 0, 8, 16, 48, 80, 336
        brow = sb("brow", [1, D], BF16)
        st1 = sb("st1", [1, 16], F32)
        stats = sb("stats", [128, 48], F32)
        tmp1 = sb("tmp1", [128, 1024], F32)
        xsta = sb("xsta", [128, 4 * D], F32)
        ntok = sb("ntok", [128, 4 * D], BF16)
        nf = sb("nf", [128, 8 * CS], BF16)
        kbuf = sb("kbuf", [128, 8 * CS], BF16)
        vbuf = sb("vbuf", [128, 4 * D], BF16)
        kpre = sb("kpre", [128, 8 * CS], BF16)
        acc = [sb(f"acc{g}", [128, 8 * CS], F32) for g in range(NG)]

        bstack = ExitStack()

        def sbB(name, shape, dt):
            return bstack.enter_context(nc.sbuf_tensor(name, shape, dt))

        prot = sbB("prot", [128, 128], BF16)
        lns = sbB("lns", [128, 640], F32)
        oh16 = sbB("oh16", [128, 16 * H], BF16)
        sel16 = sbB("sel16", [16, 128 * (H // 2)], BF16)
        dti = sbB("dti", [128, 4 * CS], I16)
        offs = sbB("offs", [128, NSTEP], F32)
        bvrow = sbB("bvrow", [1, D], BF16)
        g1row = sbB("g1row", [128, D], BF16)
        b1row = sbB("b1row", [128, D], BF16)
        ropeb = sbB("ropeb", [128, 2 * CS], BF16)
        wstr = sbB("wstr", [128, 4 * D], BF16)
        pt_a = sbB("pt_a", [128, 2 * CS], BF16)
        pt_b = sbB("pt_b", [128, 2 * CS], BF16)
        pt_c = sbB("pt_c", [128, 2 * CS], BF16)
        cmpb = sbB("cmpb", [128, 4 * CS], BF16)
        lsum = [sbB(f"lsum{g}", [16, CS], F32) for g in range(NG)]
        dstack = ExitStack()
        qg = [dstack.enter_context(
            nc.sbuf_tensor(f"qg{g}", [128, 8 * CS], BF16, side="right"))
            for g in range(NG)]
        ffn_bufs = {}

        def ffn_alloc(stage):
            if stage in ffn_bufs:
                return
            if stage == 1:
                ffn_bufs[1] = (sb("actb1", [128, 4 * 2048], BF16),
                               sb("wp1a", [128, 4 * 1024], BF16),
                               sb("wp2a", [128, 4 * 1024], BF16))
            else:
                ffn_bufs[2] = (sb("actb2", [128, 4 * 2048], BF16),
                               sb("wp1b", [128, 4 * 1024], BF16),
                               sb("wp2b", [128, 4 * 1024], BF16),
                               sb("cwT", [8, TOK], BF16),
                               sb("cwb", [128, TOK], BF16),
                               sb("wgb", [128, 64], BF16))

        pst = en(nc.psum_tensor("pst", [128, 2048], F32))
        pav = en(nc.psum_tensor("pav", [128, 1024], F32))
        ps_l = en(nc.psum_tensor("ps_l", [128, 512], F32))
        ps_x = en(nc.psum_tensor("ps_x", [128, 512], F32))

        sx = Sem(en(nc.semaphore("sx")))
        sg = Sem(en(nc.semaphore("sg")))
        sxs = [Sem(en(nc.semaphore(f"sxs{i}"))) for i in range(4)]
        sxr = Sem(en(nc.semaphore("sxr")))
        swa = Sem(en(nc.semaphore("swa")))
        swb = Sem(en(nc.semaphore("swb")))
        sw1 = Sem(en(nc.semaphore("sw1")))
        sw2 = Sem(en(nc.semaphore("sw2")))
        swr = Sem(en(nc.semaphore("swr")))
        sw = Sem(en(nc.semaphore("sw")))
        st = Sem(en(nc.semaphore("st")))
        sv = Sem(en(nc.semaphore("sv")))
        ss = Sem(en(nc.semaphore("ss")))
        b.auto = {"v": sv, "s": ss}
        so = Sem(en(nc.semaphore("so")))
        block = en(nc.Block())

        def wv4(m4):
            return pst[:, m4 * 512:(m4 + 1) * 512]

        for dst, src in ((idb[:], idb_d), (idf[:], idf_d), (prot[:], prot_d),
                         (oh16[:], oh16_d), (sel16[:], sel16_d),
                         (oh8[:], oh8_d), (dti[:], dt_d), (offs[:], offs_d),
                         (biasf[:, BQ:BQ + 8], bq_f_d),
                         (biasf[:, BK:BK + 8], bk_f_d),
                         (biasf[:, B1C:B1C + 32], b1_f_d),
                         (biasf[:, BS1C:BS1C + 32], bs1_f_d),
                         (biasf[:, B1MC:B1MC + 256], b1m_f_d),
                         (biasf[:, LN2C:LN2C + 8], g2_f_d),
                         (biasf[:, LN2C + 8:LN2C + 16], b2l_f_d)):
            b.dma("y", dst, src, sem=sx)
        b.dma("g", g1row[0:1, :], g1_r_d, sem=sw)
        b.dma("g", b1row[0:1, :], b1l_r_d, sem=sw)
        b.dma("g", bvrow[0:1, :], bv_r_d, sem=sw)
        b.call("g", "memset", ones_r[:], 1.0)
        b.call("g", "memset", ones2[:], 1.0, sem=sg)
        b.call("v", "memset", st1[:], 0.0)
        b.call("v", "memset", stats[:], 0.0, sem=sv)
        for g in range(NG):
            b.call("v", "memset", acc[g][:], 0.0)
            b.call("v", "memset", lsum[g][:], 0.0, sem=sv)
        b.wait("t", sw)
        b.wait("t", sg)
        b.wait("t", sx)
        for row in (g1row, b1row):
            for h4 in range(2):
                sl = slice(h4 * 512, (h4 + 1) * 512)
                b.wait("t", sv)
                b.mm(ps_x[:, 0:512], ones_r[0:1, 0:128], row[0:1, sl], sem=st)
                b.wait("v", st)
                b.vcopy(row[:, sl], ps_x[:, 0:512], sem=sv)
        b.wait("t", sv)

        lncons = [[0, 0, 0] for _ in range(4)]

        def xload(srcap, row0, slot):
            sxk = sxs[slot]
            b.wait("y", sv, lncons[slot][0])
            b.wait("y", ss, lncons[slot][1])
            b.wait("y", st, lncons[slot][2])
            return b.dma("y", xsta[:, slot * D:(slot + 1) * D],
                         srcap[row0:row0 + 128, :], sem=sxk)

        def stat_tile(srcap, pos, t):
            slot = t % 4
            xl = xload(srcap, pos, slot)
            xt = xsta[:, slot * D:(slot + 1) * D]
            b.wait("s", sxs[slot], xl)
            b.act(xt, xt, AF.Identity, accum=lns[:, t:t + 1])
            b.act(xt, xt, AF.Square, accum=lns[:, 80 + t:81 + t], sem=ss)
            lncons[slot][1] = ss.n

        lns_ready = [0]

        def ln_chain(t0, t1, own_xn=False):
            b.wait("v", ss)
            n = t1 - t0
            me = lns[:, 160 + t0:160 + t1]
            va = lns[:, 240 + t0:240 + t1]
            r_ = lns[:, 320 + t0:320 + t1]
            rt = lns[:, 400 + t0:400 + t1]
            bi = lns[:, 480 + t0:480 + t1]
            b.ts(me, lns[:, t0:t1], 1.0 / D, None, ALU.mult)
            b.ts(va, lns[:, 80 + t0:80 + t1], 1.0 / D, None, ALU.mult)
            b.tt(r_, me, me, ALU.mult)
            b.tt(va, va, r_, ALU.subtract)
            b.ts(r_, va, LN_EPS, None, ALU.add, sem=sv)
            b.wait("s", sv)
            b.act(r_, r_, AF.Ln, sem=ss)
            b.wait("s", ss)
            b.act(r_, r_, AF.Exp, scale=-0.5, sem=ss)
            b.wait("v", ss)
            b.ts(rt, va, LN_EPS, None, ALU.add)
            b.call("v", "reciprocal", rt, rt)
            b.tt(rt, va, rt, ALU.mult)
            b.ts(rt, rt, LN_EPS, None, ALU.add, sem=sv)
            b.wait("s", sv)
            b.act(rt, rt, AF.Ln, sem=ss)
            b.wait("s", ss)
            b.act(rt, rt, AF.Exp, scale=-0.5, sem=ss)
            b.wait("v", ss)
            b.tt(rt, r_, rt, ALU.mult)
            b.tt(bi, me, rt, ALU.mult)
            b.ts(bi, bi, -1.0, None, ALU.mult)
            if own_xn:
                bo_ = lns[:, 560 + t0 - 64:560 + t1 - 64]
                b.tt(bo_, me, r_, ALU.mult)
                b.ts(bo_, bo_, -1.0, None, ALU.mult)
            b.call("v", "engine_nop", sem=sv)
            lns_ready[0] = sv.n

        def ln_apply(srcap, pos, tbase, mode):
            xl = [0] * 4
            for tt_ in range(4):
                xl[tt_] = xload(srcap, pos + tt_ * 128, tbase % 4 * 0 + tt_)
            for tt_ in range(4):
                t = tbase + tt_
                xt = xsta[:, tt_ * D:(tt_ + 1) * D]
                b.wait("s", sxs[tt_], xl[tt_])
                b.wait("s", sv, lns_ready[0])
                if mode == "normed":
                    sc, bi = lns[:, 400 + t:401 + t], lns[:, 480 + t:481 + t]
                else:
                    sc, bi = lns[:, 320 + t:321 + t], lns[:, 560 + t - 64:
                                                         561 + t - 64]
                b.act(ntok[:, tt_ * D:(tt_ + 1) * D], xt, AF.Identity,
                      bias=bi, scale=sc, sem=ss)
                lncons[tt_][1] = ss.n
                lncons[tt_][2] = st.n
            ln_s = ss.n
            if mode == "normed":
                b.wait("v", ss)
                for tt_ in range(4):
                    nslot = ntok[:, tt_ * D:(tt_ + 1) * D]
                    b.tt(nslot, nslot, g1row[:], ALU.mult)
                    b.tt(nslot, nslot, b1row[:], ALU.add, sem=sv)
                    lncons[tt_][0] = sv.n
            return ln_s, sv.n

        tp_state = [0, 0]

        def tpose4(dst, ln_s, ln_v):
            b.wait("t", ss, ln_s)
            b.wait("t", sv, ln_v)
            for tt_ in range(4):
                b.wait("t", sv, tp_state[0])
                last = 0
                for f in range(8):
                    last = b.mm(pst[:, 1024 + f * 128:1024 + (f + 1) * 128],
                                ntok[:, tt_ * D + f * 128:
                                     tt_ * D + (f + 1) * 128],
                                idb[:], sem=st)
                tp_state[1] = last
                lncons[tt_][2] = last
                b.wait("v", st, last)
                src3 = pst[:, 1024:2048].rearrange("p (f c) -> p f c", c=128)
                dst3 = dst.rearrange("p (f c) -> p f c", c=CS)[
                    :, :, tt_ * 128:(tt_ + 1) * 128]
                tp_state[0] = b.call("v", "tensor_copy", dst3, src3, sem=sv)
            return tp_state[0]

        bhist = []

        def stream_proj(src2_fn, rhs_fn, psum_fn, nm, nk=8, bias_mm=None,
                        qwaits=None):
            hist = bhist
            w_ = nm * 128
            for kk in range(0, nk, 4):
                u = (kk // 4) % 2
                sl = u * 2 * D
                swk = (swa, swb)[u]
                if len(hist) >= 2:
                    b.wait("g", st, hist[-2])
                wl = b.dma("g",
                           wstr[:, sl:sl + 4 * w_].rearrange(
                               "p (b c) -> p b c", b=4),
                           src2_fn(kk).rearrange("(b p) c -> p b c", p=128),
                           sem=swk)
                if kk == 0 and qwaits is not None:
                    for ws_, wv_2 in qwaits:
                        b.wait("t", ss, ws_)
                        b.wait("t", sv, wv_2)
                b.wait("t", swk, wl)
                last = None
                for bb2 in range(4):
                    k = kk + bb2
                    for m in range(nm):
                        last = b.mm(psum_fn(m),
                                    wstr[:, sl + bb2 * w_ + m * 128:
                                         sl + bb2 * w_ + (m + 1) * 128],
                                    rhs_fn(k), start=(k == 0),
                                    stop=(k == nk - 1 and bias_mm is None),
                                    sem=st)
                hist.append(last)
            if bias_mm is not None:
                bias_mm()
            return hist[-1]

        slot_hist = [(0, 0)] * 4
        rope_hist = [(0, 0), (0, 0)]

        def qk_proj_rope(is_q, dst, nfr=0):
            col0 = 0 if is_q else D
            bcol = BQ if is_q else BK
            evd = proj_waves(col0, bcol, nfr, rope_dst=dst)
            for m in range(4, 8):
                rope_tile(m, evd, dst)

        def band_dma(src2, u):
            sl = u * 2 * D
            swk = (swa, swb)[u]
            if len(bhist) >= 2:
                b.wait("g", st, bhist[-2 + u])
            wl = b.dma("g",
                       wstr[:, sl:sl + 2048].rearrange(
                           "p (b c) -> p b c", b=4),
                       src2.rearrange("(b p) c -> p b c", p=128), sem=swk)
            return (sl, swk, wl)

        def band_mms(dinfo, kk0, qwaits=None):
            sl, swk, wl = dinfo
            if qwaits is not None:
                for ws_, wv_2 in qwaits:
                    b.wait("t", ss, ws_)
                    b.wait("t", sv, wv_2)
            b.wait("t", swk, wl)
            last = None
            for bb2 in range(4):
                k = kk0 + bb2
                for m in range(4):
                    last = b.mm(wv4(m),
                                wstr[:, sl + bb2 * 512 + m * 128:
                                     sl + bb2 * 512 + (m + 1) * 128],
                                nf[:, k * CS:(k + 1) * CS],
                                start=(k == 0), stop=(k == 7),
                                sem=st)
            bhist.append(last)

        def proj_waves(col0, bcol, nfr, pre=None, rope_dst=None):
            evd = [0] * 8
            b.wait("s", sv)
            b.wait("t", sv, nfr)
            for wv in range(2):
                if wv == 0 and pre is not None:
                    d0, d1 = pre
                else:
                    d0 = band_dma(wqkv_d[0:512,
                                         col0 + wv * 512:
                                         col0 + (wv + 1) * 512], 0)
                    d1 = band_dma(wqkv_d[512:1024,
                                         col0 + wv * 512:
                                         col0 + (wv + 1) * 512], 1)
                if wv == 1 and rope_dst is not None:
                    for m in range(4):
                        rope_tile(m, evd, rope_dst)
                band_mms(d0, 0,
                         qwaits=[slot_hist[m4] for m4 in range(4)])
                band_mms(d1, 4)
                for m4 in range(4):
                    m = wv * 4 + m4
                    b.wait("s", st, bhist[-1])
                    evd[m] = b.act(kpre[:, m * CS:(m + 1) * CS], wv4(m4),
                                   AF.Identity,
                                   bias=biasf[:, bcol + m:bcol + m + 1],
                                   sem=ss)
                    slot_hist[m4] = (evd[m], sv.n)
            return evd

        def rope_tile(m, evd, dst):
            pc = pav[:, (m % 2) * 512:(m % 2) * 512 + 512]
            b.wait("t", ss, evd[m])
            b.wait("t", sv, max(rope_hist[m % 2][0], fl_acc[m % 2]))
            rot = b.mm(pc, prot[:], kpre[:, m * CS:(m + 1) * CS], sem=st)
            b.wait("v", st, rot)
            b.tt(tmp1[:, 0:512], kpre[:, m * CS:(m + 1) * CS],
                 ropeb[:, 0:CS], ALU.mult)
            b.tt(tmp1[:, 512:1024], pc, ropeb[:, CS:2 * CS], ALU.mult)
            rv = b.tt(dst[:, m * CS:(m + 1) * CS], tmp1[:, 0:512],
                      tmp1[:, 512:1024], ALU.add, sem=sv)
            rope_hist[m % 2] = (rv, rot)

        def norms_update(src, stcol):
            sqd = [0, 0]
            rdd = [0, 0]
            for m in range(8):
                sq = pt_a[:, (m % 2) * 512:(m % 2) * 512 + 512]
                b.wait("s", st, sqd[m % 2])
                sqe = b.act(sq, src[:, m * CS:(m + 1) * CS], AF.Square,
                            sem=ss)
                psn = (ps_l, ps_x)[m % 2]
                b.wait("t", ss, sqe)
                if m >= 2:
                    b.wait("t", sv, rdd[m % 2])
                sqd[m % 2] = b.mm(psn[0:1, 0:512], ones2[:, 0:1], sq, sem=st)
                b.wait("v", st, sqd[m % 2])
                b.red("reduce_max", st1[0:1, 8:9], psn[0:1, 0:512])
                rdd[m % 2] = b.tt(st1[0:1, stcol:stcol + 1],
                                  st1[0:1, stcol:stcol + 1],
                                  st1[0:1, 8:9], ALU.max, sem=sv)
            b.wait("t", sv)

        fl_exp = [0, 0]
        fl_avl = [0, 0, 0]
        fl_acc = [0, 0]
        fl_lev = [0]
        fl_stage = [0]

        for g in range(NG):
            for tt_ in range(4):
                stat_tile(x_own, (g * 4 + tt_) * 128, 64 + g * 4 + tt_)
        ln_chain(64, 80, own_xn=True)
        for g in range(NG):
            b.wait("y", sv)
            b.dma("y", ropeb[:, 0:CS], ropeq_d[0:128, g * CS:(g + 1) * CS],
                  sem=sxr)
            rope_ld = b.dma("y", ropeb[:, CS:2 * CS],
                            ropeq_d[0:128, TOK + g * CS: TOK + (g + 1) * CS],
                            sem=sxr)
            ln_s, ln_v = ln_apply(x_own, g * CS, 64 + g * 4, "normed")
            nfr = tpose4(nf, ln_s, ln_v)
            b.wait("v", sxr, rope_ld)
            qk_proj_rope(True, qg[g], nfr)
            norms_update(kpre, 0)
            for cc in range(4 * g, 4 * g + 4):
                for tt_ in range(4):
                    stat_tile(x_pref, cc * CS + tt_ * 128, cc * 4 + tt_)

        sched_i = 0
        ln_chain(0, 64)
        ln_pipe = ln_apply(x_pref, 0, 0, "normed")
        kw0_pre = (band_dma(wqkv_d[0:512, D:D + 512], 0),
                   band_dma(wqkv_d[512:1024, D:D + 512], 1))
        for c in range(NCHUNK):
            b.wait("y", sv)
            b.dma("y", ropeb[:, 0:CS], ropek_d[0:128, c * CS:(c + 1) * CS],
                  sem=sxr)
            rope_ld = b.dma("y", ropeb[:, CS:2 * CS],
                            ropek_d[0:128, S + c * CS: S + (c + 1) * CS],
                            sem=sxr)
            ln_s, ln_v = ln_pipe
            nfr = tpose4(nf, ln_s, ln_v)
            if c + 1 < NCHUNK:
                ln_pipe = ln_apply(x_pref, (c + 1) * CS, (c + 1) * 4,
                                   "normed")
            b.wait("v", sxr, rope_ld)
            evd = proj_waves(D, BK, nfr, pre=kw0_pre)

            def v_dma(vh, kk):
                u = (kk // 4) % 2
                sl = u * 2 * D
                swk = (swa, swb)[u]
                if len(bhist) >= 2:
                    b.wait("g", st, bhist[-2 + u])
                wl = b.dma("g",
                           wstr[:, sl:sl + 2048].rearrange(
                               "p (b c) -> p b c", b=4),
                           wqkv_d[kk * 128:(kk + 4) * 128,
                                  2 * D + vh * 512:
                                  2 * D + (vh + 1) * 512].rearrange(
                               "(b p) c -> p b c", p=128), sem=swk)
                return (sl, swk, wl)

            def v_mms(kk, dinfo):
                sl, swk, wl = dinfo
                if kk == 0:
                    b.wait("t", sv, nfr)
                    for m4 in range(4):
                        b.wait("t", ss, slot_hist[m4][0])
                        b.wait("t", sv, slot_hist[m4][1])
                b.wait("t", swk, wl)
                last = None
                for bb2 in range(4):
                    k = kk + bb2
                    for tt_ in range(4):
                        last = b.mm(wv4(tt_),
                                    nf[:, k * CS + tt_ * 128:
                                       k * CS + (tt_ + 1) * 128],
                                    wstr[:, sl + bb2 * 512:
                                         sl + bb2 * 512 + 512],
                                    start=(k == 0), stop=False, sem=st)
                bhist.append(last)

            def v_fin(vh):
                for tt_ in range(4):
                    b.mm(wv4(tt_), ones_r[0:1, 0:128],
                         bvrow[0:1, vh * 512:(vh + 1) * 512],
                         start=False, stop=True, sem=st)
                vlast = st.n
                for tt_ in range(4):
                    b.wait("s", st, vlast)
                    ev = b.act(vbuf[:, tt_ * D + vh * 512:
                                    tt_ * D + (vh + 1) * 512],
                               wv4(tt_), AF.Identity, sem=ss)
                    slot_hist[tt_] = (ev, sv.n)

            d00 = v_dma(0, 0)
            d01 = v_dma(0, 4)
            rope_tile(0, evd, kbuf)
            rope_tile(1, evd, kbuf)
            v_mms(0, d00)
            rope_tile(2, evd, kbuf)
            rope_tile(3, evd, kbuf)
            v_mms(4, d01)
            rope_tile(4, evd, kbuf)
            rope_tile(5, evd, kbuf)
            v_fin(0)
            d10 = v_dma(1, 0)
            d11 = v_dma(1, 4)
            rope_tile(6, evd, kbuf)
            rope_tile(7, evd, kbuf)
            v_mms(0, d10)
            v_mms(4, d11)
            v_fin(1)
            norms_update(kpre, 1)
            if c + 1 < NCHUNK:
                kw0_pre = (band_dma(wqkv_d[0:512, D:D + 512], 0),
                           band_dma(wqkv_d[512:1024, D:D + 512], 1))
            b.wait("s", sv)
            b.act(st1[0:1, 2:3], st1[0:1, 0:1], AF.Ln, sem=ss)
            b.act(st1[0:1, 3:4], st1[0:1, 1:2], AF.Ln, sem=ss)
            b.wait("v", ss)
            b.tt(st1[0:1, 8:9], st1[0:1, 2:3], st1[0:1, 3:4], ALU.add)
            b.ts(st1[0:1, 8:9], st1[0:1, 8:9], 0.5, -math.log(8.0),
                 ALU.mult, ALU.add, sem=sv)
            b.wait("s", sv)
            b.act(st1[0:1, 5:6], st1[0:1, 8:9], AF.Exp, sem=ss)
            b.wait("v", ss)
            b.vcopy(pt_a[0:1, 0:1], st1[0:1, 5:6])
            b.vcopy(st1[0:1, 5:6], pt_a[0:1, 0:1])
            b.tt(st1[0:1, 8:9], st1[0:1, 4:5], st1[0:1, 5:6], ALU.subtract)
            b.ts(st1[0:1, 7:8], st1[0:1, 5:6], -1.0, None, ALU.mult, sem=sv)
            b.wait("s", sv)
            b.act(st1[0:1, 6:7], st1[0:1, 8:9], AF.Exp, sem=ss)
            b.wait("v", ss)
            b.vcopy(st1[0:1, 4:5], st1[0:1, 5:6])
            b.vcopy(pt_a[0:1, 0:2], st1[0:1, 6:8], sem=sv)
            b.wait("t", sv)
            b.wait("t", ss)
            b.mm(ps_x[:, 0:2], ones_r[0:1, 0:128], pt_a[0:1, 0:2], sem=st)
            b.wait("v", st)
            b.wait("v", ss)
            b.vcopy(stats[:, 1:2], ps_x[:, 0:1])
            stats_rdy = b.vcopy(stats[:, 0:1], ps_x[:, 1:2], sem=sv)
            kv_rdy_v = sv.n
            kv_rdy_s = ss.n

            first_of_chunk = True
            while sched_i < NSTEP and SCHED[sched_i][0] == c:
                _, g = SCHED[sched_i]
                step = sched_i
                sched_i += 1
                candidate = (4 * g <= c)
                cmp_done = 0
                if candidate:
                    for kt in range(4):
                        cmp_done = b.ts(cmpb[:, kt * CS:(kt + 1) * CS],
                                        dti[:, kt * CS:(kt + 1) * CS],
                                        offs[:, step:step + 1], None,
                                        ALU.is_le, sem=sv)
                first_l = True

                def emit_avl(p):
                    nonlocal first_l
                    pt3, rdy, rdy_sem, hp_, kt_, g_, last = p
                    ptb = (pt_a, pt_b, pt_c)[pt3]
                    b.wait("t", rdy_sem, rdy)
                    if first_l:
                        b.wait("t", sv, fl_lev[0])
                    b.mm(ps_l[0:16, 0:512],
                         oh16[:, 16 * (2 * hp_):16 * (2 * hp_) + 16],
                         ptb[:, 0:512], start=first_l, stop=False)
                    b.mm(ps_l[0:16, 0:512],
                         oh16[:, 16 * (2 * hp_ + 1):16 * (2 * hp_ + 1) + 16],
                         ptb[:, 512:1024], start=False, stop=last)
                    first_l = False
                    if kt_ == 0:
                        b.wait("t", sv, max(fl_acc[0], fl_acc[1],
                                            rope_hist[0][0],
                                            rope_hist[1][0]))
                    b.mm(pav[0:64, 0:512],
                         vbuf[:, kt_ * D + hp_ * 128:kt_ * D + hp_ * 128 + 64],
                         ptb[:, 0:512], tp=(0, 0),
                         start=(kt_ == 0), stop=(kt_ == 3))
                    av = b.mm(pav[64:128, 512:1024],
                              vbuf[:, kt_ * D + hp_ * 128 + 64:
                                   kt_ * D + hp_ * 128 + 128],
                              ptb[:, 512:1024], tp=(0, 64),
                              start=(kt_ == 0), stop=(kt_ == 3), sem=st)
                    fl_avl[pt3] = av
                    if kt_ == 3:
                        b.wait("v", st, av)
                        aslice = acc[g_][:, hp_ * CS:(hp_ + 1) * CS]
                        fl_acc[0] = b.stt(aslice[0:64, :], aslice[0:64, :],
                                          stats[0:64, 1:2], pav[0:64, 0:512],
                                          ALU.mult, ALU.add, sem=sv)
                        fl_acc[1] = b.stt(aslice[64:128, :],
                                          aslice[64:128, :],
                                          stats[64:128, 1:2],
                                          pav[64:128, 512:1024],
                                          ALU.mult, ALU.add, sem=sv)

                pending = []
                for hp in range(8):
                    for kt in range(4):
                        sl_ = fl_stage[0] % 2
                        pt3 = fl_stage[0] % 3
                        fl_stage[0] += 1
                        pslot = pst[:, sl_ * 1024:(sl_ + 1) * 1024]
                        ptb = (pt_a, pt_b, pt_c)[pt3]
                        kc = hp * CS + kt * 128
                        b.wait("t", ss, fl_exp[sl_])
                        if first_of_chunk:
                            b.wait("t", sv, kv_rdy_v)
                            b.wait("t", ss, kv_rdy_s)
                            first_of_chunk = False
                        b.mm(pslot[:, 0:512], kbuf[0:64, kc:kc + 128],
                             qg[g][0:64, hp * CS:(hp + 1) * CS], tp=(0, 0))
                        p2 = b.mm(pslot[:, 512:1024],
                                  kbuf[64:128, kc:kc + 128],
                                  qg[g][64:128, hp * CS:(hp + 1) * CS],
                                  tp=(64, 0), sem=st)
                        b.wait("s", st, p2)
                        b.wait("s", st, fl_avl[pt3])
                        ex = b.act(ptb[:, 0:1024], pslot[:, 0:1024], AF.Exp,
                                   bias=stats[:, 0:1], sem=ss)
                        fl_exp[sl_] = ex
                        if candidate:
                            b.wait("v", ss, ex)
                            b.tt(ptb[:, 0:512], ptb[:, 0:512],
                                 cmpb[:, kt * CS:(kt + 1) * CS], ALU.mult)
                            mk = b.tt(ptb[:, 512:1024], ptb[:, 512:1024],
                                      cmpb[:, kt * CS:(kt + 1) * CS],
                                      ALU.mult, sem=sv)
                            rdy, rdy_sem = mk, sv
                        else:
                            rdy, rdy_sem = ex, ss
                        if len(pending) >= 2:
                            emit_avl(pending.pop(0))
                        pending.append((pt3, rdy, rdy_sem, hp, kt, g,
                                        hp == 7 and kt == 3))
                for p_ in pending:
                    emit_avl(p_)
                pending = []
                b.wait("v", st)
                fl_lev[0] = b.stt(lsum[g][0:16, :], lsum[g][0:16, :],
                                  stats[0:16, 1:2], ps_l[0:16, 0:512],
                                  ALU.mult, ALU.add, sem=sv)
                if c == 4 * g + 3:
                    b.call("v", "reciprocal", lsum[g][0:16, :],
                           lsum[g][0:16, :])
                    b.vcopy(pt_a[0:16, 0:512], lsum[g][0:16, :], sem=sv)
                    b.wait("t", sv)
                    for hp in range(8):
                        pc = pav[:, (hp % 2) * 512:(hp % 2) * 512 + 512]
                        b.wait("t", sv)
                        b.mm(pc, sel16[:, hp * 128:(hp + 1) * 128],
                             pt_a[0:16, 0:512], sem=st)
                        b.wait("v", st)
                        b.tt(qg[g][:, hp * CS:(hp + 1) * CS],
                             acc[g][:, hp * CS:(hp + 1) * CS], pc, ALU.mult,
                             sem=sv)
                    b.wait("t", sv)
                    fl_exp[0] = ss.n
                    fl_exp[1] = ss.n
                    fl_avl[0] = st.n
                    fl_avl[1] = st.n
                    fl_avl[2] = st.n
                    fl_acc[0] = sv.n
                    fl_acc[1] = sv.n
                    rope_hist[0] = (sv.n, st.n)
                    rope_hist[1] = (sv.n, st.n)

        def racc(f, g):
            return acc[g][:, f * CS:(f + 1) * CS]

        xnf = [kbuf, vbuf, kpre, nf]

        def x1t(k):
            if k < 2:
                return pt_a[:, k * 512:(k + 1) * 512]
            if k < 4:
                return pt_b[:, (k - 2) * 512:(k - 2) * 512 + 512]
            return cmpb[:, (k - 4) * 512:(k - 4) * 512 + 512]

        b.wait("g", st)
        brow_ld = b.dma("g", brow[0:1, :], bo_r_d, sem=swr)
        for g in range(NG):
            for tt_ in range(4):
                sl = tt_ * D
                sxk = sxs[tt_]
                t_ = 64 + g * 4 + tt_
                xl = xload(x_own, g * CS + tt_ * 128, tt_)
                b.wait("s", sxs[tt_], xl)
                b.wait("s", sv, lns_ready[0])
                b.act(ntok[:, tt_ * D:(tt_ + 1) * D], xsta[:, sl:sl + D],
                      AF.Identity, bias=lns[:, 560 + t_ - 64:561 + t_ - 64],
                      scale=lns[:, 320 + t_:321 + t_], sem=ss)
                lncons[tt_][1] = ss.n
                b.wait("t", sxs[tt_], xl)
                b.wait("t", ss)
                b.wait("t", sv)
                for f in range(8):
                    b.mm(pst[:, f * 128:(f + 1) * 128],
                         xsta[:, sl + f * 128: sl + (f + 1) * 128], idf[:],
                         sem=None)
                for f in range(8):
                    b.mm(pst[:, 1024 + f * 128:1024 + (f + 1) * 128],
                         ntok[:, tt_ * D + f * 128: tt_ * D + (f + 1) * 128],
                         idb[:], sem=(st if f == 7 else None))
                b.wait("v", st)
                xn3 = xnf[g].rearrange("p (f c) -> p f c", c=CS)[
                    :, :, tt_ * 128:(tt_ + 1) * 128]
                b.vcopy(xn3, pst[:, 1024:2048].rearrange(
                    "p (f c) -> p f c", c=128))
                ac3 = acc[g].rearrange("p (f c) -> p f c", c=CS)[
                    :, :, tt_ * 128:(tt_ + 1) * 128]
                b.tt(ac3, pst[:, 0:1024].rearrange("p (f c) -> p f c", c=128),
                     xn3, ALU.add, sem=sv)
                lncons[tt_] = [sv.n, ss.n, st.n]
                b.wait("t", sv)
            b.wait("t", ss)
            b.wait("t", swr, brow_ld)
            for wv_ in range(2):
                def bias_mm(wv2=wv_):
                    for m4 in range(4):
                        m = wv2 * 4 + m4
                        b.mm(wv4(m4), brow[0:1, m * 128:(m + 1) * 128],
                             ones_r[0:1, 0:512], start=False, stop=True,
                             sem=st)
                b.wait("t", sv)
                stream_proj(lambda kk, w2=wv_: wo_d[kk * 128:(kk + 4) * 128,
                                                    w2 * 512:(w2 + 1) * 512],
                            lambda k: qg[g][:, k * CS:(k + 1) * CS],
                            wv4, nm=4, bias_mm=bias_mm)
                b.wait("v", st)
                for m4 in range(4):
                    m = wv_ * 4 + m4
                    b.tt(racc(m, g), racc(m, g), wv4(m4), ALU.add)
                    b.tt(x1t(m), xnf[g][:, m * CS:(m + 1) * CS], wv4(m4),
                         ALU.add, sem=sv)
            if DEV_STAGE >= 2:
                b.wait("t", sv)
                sqb = ntok
                for f in range(8):
                    b.mm(pst[:, 0:512], ones2[:], x1t(f), start=(f == 0),
                         stop=(f == 7), sem=st)
                sqmm = [0, 0]
                for f in range(8):
                    sq = sqb[:, (f % 2) * 512:(f % 2) * 512 + 512]
                    if f >= 2:
                        b.wait("v", st, sqmm[f % 2])
                    b.tt(sq, x1t(f), x1t(f), ALU.mult, sem=sv)
                    b.wait("t", sv)
                    sqmm[f % 2] = b.mm(pst[:, 512:1024], ones2[:], sq,
                                       start=(f == 0), stop=(f == 7), sem=st)
                b.wait("v", st)
                mb = tmp1[:, 0:512]
                rb = tmp1[:, 512:1024]
                b.ts(mb, pst[:, 0:512], 1.0 / D, None, ALU.mult)
                b.ts(rb, pst[:, 512:1024], 1.0 / D, None, ALU.mult)
                b.tt(xsta[:, 0:512], mb, mb, ALU.mult)
                b.tt(rb, rb, xsta[:, 0:512], ALU.subtract)
                b.ts(rb, rb, LN_EPS, None, ALU.add, sem=sv)
                b.wait("s", sv)
                b.act(rb, rb, AF.Ln, sem=ss)
                b.wait("s", ss)
                b.act(rb, rb, AF.Exp, scale=-0.5, sem=ss)
                b.wait("v", ss)
                for f in range(8):
                    qsl = qg[g][:, f * CS:(f + 1) * CS]
                    b.tt(qsl, x1t(f), mb, ALU.subtract)
                    b.tt(qsl, qsl, rb, ALU.mult)
                    b.ts(qsl, qsl,
                         biasf[:, LN2C + f:LN2C + f + 1],
                         biasf[:, LN2C + 8 + f:LN2C + 8 + f + 1],
                         ALU.mult, ALU.add, sem=sv)
                for tt_ in range(4):
                    lncons[tt_] = [sv.n, ss.n, st.n]
                b.wait("t", sv)

        ffn_ld = [0, 0]

        def ffn_phase(rhs_fn, w1_src, w2_src, b1col, actfn, b2rhs_fn,
                      evac, Fp, prescale=False):
            nf_ = Fp // 128
            npiece = 4096 // Fp

            def w1sl(k):
                if Fp == 512:
                    return ffn_bufs[1][1][:, k * Fp:(k + 1) * Fp]
                buf = ffn_bufs[1][1] if k < 4 else ffn_bufs[2][1]
                return buf[:, (k % 4) * Fp:(k % 4 + 1) * Fp]

            def w2sl(kt):
                if nf_ <= 4:
                    return ffn_bufs[1][2][:, kt * 1024:(kt + 1) * 1024]
                buf = ffn_bufs[1][2] if kt < 4 else ffn_bufs[2][2]
                return buf[:, (kt % 4) * 1024:(kt % 4 + 1) * 1024]

            def asl(f, g4):
                buf = ffn_bufs[1][0] if f < 4 else ffn_bufs[2][0]
                return buf[:, (f % 4) * TOK + g4 * CS:
                           (f % 4) * TOK + (g4 + 1) * CS]

            cwbuf = ffn_bufs[2][4] if 2 in ffn_bufs else None
            for p in range(npiece):
                b.wait("g", st, ffn_ld[0])
                for k in range(8):
                    b.dma("g", w1sl(k), w1_src(k, p), sem=sw1)
                w1_ld = sw1.n
                b.wait("g", st, ffn_ld[1])
                for kt in range(nf_):
                    b.dma("g", w2sl(kt), w2_src(kt, p), sem=sw2)
                w2_ld = sw2.n
                b.wait("t", sw1, w1_ld)
                for f in range(nf_):
                    for g4 in range(4):
                        sl = (f * 4 + g4) % 4
                        b.wait("t", ss, slot_hist[sl][0])
                        b.wait("t", sv, slot_hist[sl][1])
                        for k in range(8):
                            b.mm(wv4(sl),
                                 w1sl(k)[:, f * 128:(f + 1) * 128],
                                 rhs_fn(g4, k), start=(k == 0), stop=(k == 7),
                                 sem=(st if k == 7 else None))
                        b.wait("s", st)
                        if f == nf_ - 1 and g4 == 3:
                            ffn_ld[0] = st.n
                        aslice = asl(f, g4)
                        sev = b.act(aslice, wv4(sl), actfn,
                                    bias=b1col(p, f), sem=ss)
                        vev = sv.n
                        if prescale:
                            b.wait("v", ss)
                            vev = b.tt(aslice, aslice,
                                       cwbuf[:, g4 * CS:(g4 + 1) * CS],
                                       ALU.mult, sem=sv)
                        slot_hist[sl] = (sev, vev)
                b.wait("t", sw2, w2_ld)
                b.wait("t", ss)
                if prescale:
                    b.wait("t", sv)
                for m in range(8):
                    for g4 in range(4):
                        sl = (m * 4 + g4) % 4
                        b.wait("t", ss, slot_hist[sl][0])
                        b.wait("t", sv, slot_hist[sl][1])
                        last_bias = (p == 0 and b2rhs_fn)
                        for kt in range(nf_):
                            b.mm(wv4(sl),
                                 w2sl(kt)[:, m * 128:(m + 1) * 128],
                                 asl(kt, g4),
                                 start=(kt == 0),
                                 stop=(kt == nf_ - 1 and not last_bias),
                                 sem=(None if (kt < nf_ - 1 or last_bias)
                                      else st))
                        if last_bias:
                            b.mm(wv4(sl), brow[0:1, m * 128:(m + 1) * 128],
                                 b2rhs_fn(g4), start=False, stop=True,
                                 sem=st)
                        b.wait("v", st)
                        if m == 7 and g4 == 3:
                            ffn_ld[1] = st.n
                        vev = evac(m, g4, wv4(sl))
                        slot_hist[sl] = (ss.n, vev)

        bstack.close()

        if DEV_STAGE >= 2:
            ffn_alloc(1)
            b.wait("g", st)
            b.wait("g", sv)
            rl = b.dma("g", brow[0:1, :], b2_r_d, sem=swr)
            b.wait("t", swr, rl)

            def mlp_evac(m, g4, ps):
                return b.tt(racc(m, g4), racc(m, g4), ps, ALU.add, sem=sv)

            ffn_phase(lambda g4, k: qg[g4][:, k * CS:(k + 1) * CS],
                      lambda k, p: w1_d[k * 128:(k + 1) * 128,
                                        p * 512:(p + 1) * 512],
                      lambda kt, p: w2_d[p * 512 + kt * 128:
                                         p * 512 + (kt + 1) * 128, 0:1024],
                      lambda p, f: biasf[:, B1C + p * 4 + f:
                                         B1C + p * 4 + f + 1],
                      AF.Gelu,
                      lambda g4: ones_r[0:1, 0:512],
                      mlp_evac, Fp=512)

        dstack.close()

        if DEV_STAGE >= 3:
            ffn_alloc(1)
            ffn_alloc(2)
            cwT = ffn_bufs[2][3]
            cwb = ffn_bufs[2][4]
            wgb = ffn_bufs[2][5]
            b.wait("g", st)
            for k in range(8):
                rl = b.dma("g", wgb[:, k * 8:(k + 1) * 8],
                           wg_d[k * 128:(k + 1) * 128, :], sem=swr)
            b.wait("t", swr, rl)
            b.wait("t", sv)
            b.wait("t", ss)
            for t16 in range(16):
                g4, tt_ = divmod(t16, 4)
                for k in range(8):
                    b.mm(ps_l[:, t16 * 8:(t16 + 1) * 8],
                         xnf[g4][:, k * CS + tt_ * 128:
                                 k * CS + (tt_ + 1) * 128],
                         wgb[:, k * 8:(k + 1) * 8], start=(k == 0),
                         stop=(k == 7), sem=(st if k == 7 else None))
            b.wait("s", st)
            b.act(tmp1[:, 0:128], ps_l[:, 0:128], AF.Exp, sem=ss)
            b.wait("v", ss)
            for t16 in range(16):
                e8 = tmp1[:, t16 * 8:(t16 + 1) * 8]
                scr = tmp1[:, 128:136]
                scrb = tmp1[:, 136:144]
                b.red("reduce_sum", stats[:, 2:3], e8)
                b.red("reduce_max", stats[:, 3:4], e8)
                b.ts(scr, e8, stats[:, 3:4], None, ALU.is_ge)
                b.stt(scrb, scr, -1e30, e8, ALU.mult, ALU.add)
                b.red("reduce_max", stats[:, 4:5], scrb)
                b.ts(scr, e8, stats[:, 4:5], None, ALU.is_ge)
                b.tt(stats[:, 5:6], stats[:, 3:4], stats[:, 4:5], ALU.add)
                b.stt(stats[:, 5:6], stats[:, 2:3], 1e-9, stats[:, 5:6],
                      ALU.mult, ALU.add)
                b.call("v", "reciprocal", stats[:, 6:7], stats[:, 5:6])
                b.tt(scr, e8, scr, ALU.mult)
                b.ts(ntok[:, t16 * 8:(t16 + 1) * 8], scr, stats[:, 6:7],
                     None, ALU.mult, sem=sv)
                b.wait("t", sv)
                b.mm(ps_x[0:8, (t16 % 2) * 128:(t16 % 2) * 128 + 128],
                     ntok[:, t16 * 8:(t16 + 1) * 8], idb[:], sem=st)
                b.wait("v", st)
                b.vcopy(cwT[0:8, t16 * 128:(t16 + 1) * 128],
                        ps_x[0:8, (t16 % 2) * 128:(t16 % 2) * 128 + 128],
                        sem=sv)
            for e in range(8):
                b.wait("t", sv)
                b.wait("t", ss)
                for g4 in range(4):
                    b.wait("t", ss)
                    b.mm(pav[:, (g4 % 2) * 512:(g4 % 2) * 512 + 512],
                         oh8[:, e * 128:(e + 1) * 128],
                         cwT[0:8, g4 * 512:(g4 + 1) * 512], sem=st)
                    b.wait("s", st)
                    b.wait("s", sv)
                    b.act(cwb[:, g4 * CS:(g4 + 1) * CS],
                          pav[:, (g4 % 2) * 512:(g4 % 2) * 512 + 512],
                          AF.Identity, sem=ss)
                b.wait("g", ss)
                b.wait("g", st)
                rl = b.dma("g", brow[0:1, :],
                           b2m_r_d[0:1, e * D:(e + 1) * D], sem=swr)
                b.wait("t", swr, rl)
                b.wait("t", ss)

                def moe_evac(m, g4, ps):
                    return b.tt(racc(m, g4), racc(m, g4), ps, ALU.add,
                                sem=sv)

                ffn_phase(lambda g4, k: xnf[g4][:, k * CS:(k + 1) * CS],
                          lambda k, p, e2=e: mw1_d[e2][k * 128:(k + 1) * 128,
                                                       p * 1024:(p + 1) * 1024],
                          lambda kt, p, e2=e: mw2_d[e2][p * 1024 + kt * 128:
                                                        p * 1024 + (kt + 1) * 128,
                                                        0:1024],
                          lambda p, f, e2=e: biasf[:, B1MC + e2 * 32 + p * 8 + f:
                                                   B1MC + e2 * 32 + p * 8 + f + 1],
                          AF.Silu,
                          lambda g4: cwb[0:1, g4 * CS:(g4 + 1) * CS],
                          moe_evac, Fp=1024, prescale=True)

        if DEV_STAGE >= 4:
            b.wait("g", st)
            b.wait("g", sv)
            rl = b.dma("g", brow[0:1, :], bs2_r_d, sem=swr)
            b.wait("t", swr, rl)

            def sh_evac(m, g4, ps):
                return b.stt(racc(m, g4), ps, 0.1, racc(m, g4), ALU.mult,
                             ALU.add, sem=sv)

            ffn_phase(lambda g4, k: xnf[g4][:, k * CS:(k + 1) * CS],
                      lambda k, p: ws1_d[k * 128:(k + 1) * 128,
                                         p * 1024:(p + 1) * 1024],
                      lambda kt, p: ws2_d[p * 1024 + kt * 128:
                                          p * 1024 + (kt + 1) * 128, 0:1024],
                      lambda p, f: biasf[:, BS1C + p * 8 + f:
                                         BS1C + p * 8 + f + 1],
                      AF.Silu,
                      lambda g4: ones_r[0:1, 0:512],
                      sh_evac, Fp=1024)

        for g in range(NG):
            for tt_ in range(4):
                b.wait("t", sv)
                b.wait("t", ss)
                for f in range(8):
                    b.mm(pst[:, f * 128:(f + 1) * 128],
                         racc(f, g)[:, tt_ * 128:(tt_ + 1) * 128], idf[:],
                         sem=(st if f == 7 else None))
                b.wait("v", st)
                b.wait("v", so)
                b.vcopy(tmp1[:, 0:1024], pst[:, 0:1024], sem=sv)
                b.wait("y", sv)
                row0 = g * CS + tt_ * 128
                b.dma("y", out_d[row0:row0 + 128, :], tmp1[:, 0:1024], sem=so)
        b.wait("y", so)

        @block.tensor
        def _(E):
            for fn in b.q["t"]:
                fn(E)

        @block.vector
        def _(E):
            for fn in b.q["v"]:
                fn(E)

        @block.scalar
        def _(E):
            for fn in b.q["s"]:
                fn(E)

        @block.gpsimd
        def _(E):
            for fn in b.q["g"]:
                fn(E)

        @block.sync
        def _(E):
            for fn in b.q["y"]:
                fn(E)

    return nc


def _host_constants():
    inv = 1.0 / (10000.0 ** (np.arange(0, HD, 2, dtype=np.float64) / HD))
    t = np.arange(S, dtype=np.float64)
    fr = t[:, None] * inv[None, :]
    emb = np.concatenate([fr, fr], axis=1)
    cos = np.cos(emb).astype(np.float32)
    sin = np.sin(emb).astype(np.float32)
    cosf = np.concatenate([cos.T, cos.T], axis=0)
    sinf = np.concatenate([sin.T, sin.T], axis=0)
    iden = np.eye(128, dtype=np.float32)
    prot = np.zeros((128, 128), dtype=np.float32)
    for blk in range(2):
        o = blk * 64
        for m in range(32):
            prot[o + m + 32, o + m] = -1.0
            prot[o + m, o + m + 32] = 1.0
    oh16 = np.zeros((128, 16 * H), dtype=np.float32)
    for h in range(H):
        oh16[:, 16 * h + h] = 1.0
    sel16 = np.zeros((16, 128 * (H // 2)), dtype=np.float32)
    for hp in range(H // 2):
        sel16[2 * hp, hp * 128: hp * 128 + 64] = 1.0
        sel16[2 * hp + 1, hp * 128 + 64: hp * 128 + 128] = 1.0
    oh8 = np.zeros((8, 8 * 128), dtype=np.float32)
    for e in range(8):
        oh8[e, e * 128:(e + 1) * 128] = 1.0
    dt4 = np.zeros((128, 4 * CS), dtype=np.int16)
    for kt in range(4):
        p = np.arange(128)[:, None]
        q = np.arange(CS)[None, :]
        dt4[:, kt * CS:(kt + 1) * CS] = (128 * kt + p - q).astype(np.int16)
    return cosf, sinf, iden, prot, oh16, sel16, oh8, dt4


_PROG_CACHE = {}


def kernel(**inp):
    cosf, sinf, iden, prot, oh16, sel16, oh8, dt4 = _host_constants()
    x = np.asarray(inp["L0_attn_x"], dtype=np.float32)
    wqkv = np.ascontiguousarray(inp["L0_attn_Wqkv"], dtype=np.float32)
    bqkv = np.asarray(inp["L0_attn_bqkv"], dtype=np.float32)

    def bf(a):
        return np.ascontiguousarray(a).astype(BF)

    common = {
        "offs_h": None,
        "ropek": bf(np.concatenate([cosf, sinf], axis=1)),
        "dtiles": dt4,
        "idb_h": bf(iden), "idf_h": iden, "prot_h": bf(prot),
        "oh16_h": bf(oh16), "sel16_h": bf(sel16), "oh8_h": bf(oh8),
        "Wqkv": wqkv,
        "Wo": np.ascontiguousarray(inp["L0_attn_Wo"], dtype=np.float32),
        "W1": np.ascontiguousarray(inp["L0_attn_W1"], dtype=np.float32),
        "W2": np.ascontiguousarray(inp["L0_attn_W2"], dtype=np.float32),
        "Wg": np.ascontiguousarray(inp["L0_ffn_Wg"], dtype=np.float32),
        "MW1": np.ascontiguousarray(inp["L0_ffn_W1"], dtype=np.float32),
        "MW2": np.ascontiguousarray(inp["L0_ffn_W2"], dtype=np.float32),
        "Ws1": np.ascontiguousarray(inp["L0_ffn_Ws1"], dtype=np.float32),
        "Ws2": np.ascontiguousarray(inp["L0_ffn_Ws2"], dtype=np.float32),
        "bq_f": np.ascontiguousarray(bqkv[0:D].reshape(8, 128).T),
        "bk_f": np.ascontiguousarray(bqkv[D:2 * D].reshape(8, 128).T),
        "b1_f": np.ascontiguousarray(
            np.asarray(inp["L0_attn_b1"], np.float32).reshape(32, 128).T),
        "bs1_f": np.ascontiguousarray(
            np.asarray(inp["L0_ffn_bs1"], np.float32).reshape(32, 128).T),
        "b1m_f": np.ascontiguousarray(
            np.asarray(inp["L0_ffn_b1"], np.float32).reshape(8, 32, 128)
            .transpose(2, 0, 1).reshape(128, 256)),
        "g2_f": np.ascontiguousarray(
            np.asarray(inp["L0_attn_ln2_g"], np.float32).reshape(8, 128).T),
        "b2l_f": np.ascontiguousarray(
            np.asarray(inp["L0_attn_ln2_b"], np.float32).reshape(8, 128).T),
        "g1_r": np.asarray(inp["L0_attn_ln1_g"], np.float32)[None, :],
        "b1l_r": np.asarray(inp["L0_attn_ln1_b"], np.float32)[None, :],
        "bv_r": bqkv[2 * D:3 * D][None, :],
        "bo_r": np.asarray(inp["L0_attn_bo"], np.float32)[None, :],
        "b2_r": np.asarray(inp["L0_attn_b2"], np.float32)[None, :],
        "bs2_r": np.asarray(inp["L0_ffn_bs2"], np.float32)[None, :],
        "b2m_r": np.ascontiguousarray(
            np.asarray(inp["L0_ffn_b2"], np.float32).reshape(1, 8 * D)),
    }

    in_maps = []
    for core in range(8):
        bb, j = divmod(core, 4)
        chunks = [4 * g + j for g in range(NG)]
        m = dict(common)
        m["x_pref"] = x[bb]
        m["x_own"] = np.ascontiguousarray(
            np.concatenate([x[bb, ch * CS:(ch + 1) * CS] for ch in chunks]))
        offs = np.zeros((128, NSTEP), dtype=np.float32)
        for si, (c, g) in enumerate(SCHED):
            offs[:, si] = (4 * g + j - c) * CS
        m["offs_h"] = offs
        cols = np.concatenate(
            [np.arange(ch * CS, (ch + 1) * CS) for ch in chunks])
        m["ropeq"] = bf(np.concatenate(
            [cosf[:, cols] / 8.0, sinf[:, cols] / 8.0], axis=1))
        in_maps.append(m)

    key = DEV_STAGE
    if key not in _PROG_CACHE:
        _PROG_CACHE[key] = build_program()
    nc = _PROG_CACHE[key]
    res = run_bass_kernel_spmd(nc, in_maps, core_ids=list(range(8)))
    out = np.zeros((B, S, D), dtype=np.float32)
    for core in range(8):
        bb, j = divmod(core, 4)
        r = res.results[core]["out"]
        for g in range(NG):
            ch = 4 * g + j
            out[bb, ch * CS:(ch + 1) * CS] = r[g * CS:(g + 1) * CS]
    return out


# version 20
# speedup vs baseline: 1.0453x; 1.0156x over previous
import math
import numpy as np
import ml_dtypes
import concourse.bass as bass
import concourse.mybir as mybir
from concourse.bass_utils import run_bass_kernel_spmd

F32 = mybir.dt.float32
BF16 = mybir.dt.bfloat16
I16 = mybir.dt.int16
AF = mybir.ActivationFunctionType
ALU = mybir.AluOpType
AX = mybir.AxisListType
BF = ml_dtypes.bfloat16

B, S, D = 2, 8192, 1024
H, HD = 16, 64
NCHUNK = 16
CS = 512
NG = 4
TOK = NG * CS
LN_EPS = 1e-5
MASKVAL = -30000.0
DEV_STAGE = 4

SCHED = [(c, g) for c in range(NCHUNK) for g in range(NG) if 4 * g + 3 >= c]
NSTEP = len(SCHED)


class Sem:
    __slots__ = ("h", "n")

    def __init__(self, h):
        self.h = h
        self.n = 0


class KB:
    def __init__(self):
        self.q = {k: [] for k in "tvsgy"}
        self.auto = {}

    def on(self, e, fn):
        self.q[e].append(fn)

    def wait(self, e, sem, val=None):
        v = sem.n if val is None else val
        if v > 0:
            self.on(e, lambda E, h=sem.h, vv=v: E.wait_ge(h, vv))

    def call(self, e, meth, *args, sem=None, inc=1, **kw):
        if e in self.auto and meth != "dma_start":
            own = self.auto[e]
            if own.n > 0:
                self.on(e, lambda E, h=own.h, vv=own.n: E.wait_ge(h, vv))
            assert sem is None or sem is own
            sem = own
        if sem is not None:
            sem.n += inc

        def fn(E, meth=meth, args=args, kw=kw, sem=sem, inc=inc):
            ins = getattr(E, meth)(*args, **kw)
            if sem is not None:
                ins.then_inc(sem.h, inc)

        self.on(e, fn)
        return sem.n if sem is not None else None

    def dma(self, e, out, in_, sem=None, **kw):
        return self.call(e, "dma_start", out, in_, sem=sem, inc=16, **kw)

    def mm(self, out, lhsT, rhs, start=True, stop=True, sem=None, tp=None):
        kw = {}
        if tp is not None:
            kw["tile_position"] = tp
        return self.call("t", "matmul", out, lhsT, rhs, start=start, stop=stop,
                         sem=sem, **kw)

    def tpose(self, out, in_, iden, sem=None):
        return self.call("t", "transpose", out, in_, iden, sem=sem)

    def act(self, out, in_, func, bias=0.0, scale=1.0, accum=None, sem=None):
        kw = {}
        if accum is not None:
            kw["accum_out"] = accum
        return self.call("s", "activation", out, in_, func, bias=bias,
                         scale=scale, sem=sem, **kw)

    def tt(self, out, a, c, op, sem=None):
        return self.call("v", "tensor_tensor", out, a, c, op, sem=sem)

    def ts(self, out, a, s1, s2, op0, op1=ALU.bypass, sem=None):
        return self.call("v", "tensor_scalar", out, a, s1, s2, op0, op1,
                         sem=sem)

    def stt(self, out, in0, scalar, in1, op0, op1, sem=None):
        return self.call("v", "scalar_tensor_tensor", out, in0, scalar, in1,
                         op0, op1, sem=sem)

    def red(self, kind, out, in_, sem=None, **kw):
        return self.call("v", kind, out, in_, AX.X, sem=sem, **kw)

    def vcopy(self, out, in_, sem=None):
        return self.call("v", "tensor_copy", out, in_, sem=sem)


def build_program():
    nc = bass.Bass("TRN2", target_bir_lowering=False)

    def din(name, shape, dt=F32):
        return nc.dram_tensor(name, shape, dt, kind="ExternalInput").ap()

    x_own = din("x_own", [TOK, D])
    x_pref = din("x_pref", [S, D])
    offs_d = din("offs_h", [128, NSTEP])
    ropeq_d = din("ropeq", [128, 2 * TOK], BF16)
    ropek_d = din("ropek", [128, 2 * S], BF16)
    dt_d = din("dtiles", [128, 4 * CS], I16)
    idb_d = din("idb_h", [128, 128], BF16)
    idf_d = din("idf_h", [128, 128])
    prot_d = din("prot_h", [128, 128], BF16)
    oh16_d = din("oh16_h", [128, 16 * H], BF16)
    sel16_d = din("sel16_h", [16, 128 * (H // 2)], BF16)
    oh8_d = din("oh8_h", [8, 8 * 128], BF16)
    wqkv_d = din("Wqkv", [D, 3 * D])
    wo_d = din("Wo", [D, D])
    w1_d = din("W1", [D, 4 * D])
    w2_d = din("W2", [4 * D, D])
    wg_d = din("Wg", [D, 8])
    mw1_d = din("MW1", [8, D, 4 * D])
    mw2_d = din("MW2", [8, 4 * D, D])
    ws1_d = din("Ws1", [D, 4 * D])
    ws2_d = din("Ws2", [4 * D, D])
    bq_f_d = din("bq_f", [128, 8])
    bk_f_d = din("bk_f", [128, 8])
    b1_f_d = din("b1_f", [128, 32])
    bs1_f_d = din("bs1_f", [128, 32])
    b1m_f_d = din("b1m_f", [128, 8 * 32])
    g2_f_d = din("g2_f", [128, 8])
    b2l_f_d = din("b2l_f", [128, 8])
    g1_r_d = din("g1_r", [1, D])
    b1l_r_d = din("b1l_r", [1, D])
    bv_r_d = din("bv_r", [1, D])
    bo_r_d = din("bo_r", [1, D])
    b2_r_d = din("b2_r", [1, D])
    bs2_r_d = din("bs2_r", [1, D])
    b2m_r_d = din("b2m_r", [1, 8 * D])
    out_d = nc.dram_tensor("out", [TOK, D], F32, kind="ExternalOutput").ap()

    b = KB()
    from contextlib import ExitStack
    with ExitStack() as stack:
        en = stack.enter_context

        def sb(name, shape, dt):
            return en(nc.sbuf_tensor(name, shape, dt))

        idb = sb("idb", [128, 128], BF16)
        idf = sb("idf", [128, 128], F32)
        oh8 = sb("oh8", [8, 8 * 128], BF16)
        ones_r = sb("ones_r", [1, CS], BF16)
        ones2 = sb("ones2", [128, 128], BF16)
        biasf = sb("biasf", [128, 352], F32)
        BQ, BK, B1C, BS1C, B1MC, LN2C = 0, 8, 16, 48, 80, 336
        brow = sb("brow", [1, D], BF16)
        st1 = sb("st1", [1, 16], F32)
        stats = sb("stats", [128, 48], F32)
        tmp1 = sb("tmp1", [128, 1024], F32)
        xsta = sb("xsta", [128, 4 * D], F32)
        ntok = sb("ntok", [128, 4 * D], BF16)
        nf = sb("nf", [128, 8 * CS], BF16)
        kbuf = sb("kbuf", [128, 8 * CS], BF16)
        vbuf = sb("vbuf", [128, 4 * D], BF16)
        kpre = sb("kpre", [128, 8 * CS], BF16)
        acc = [sb(f"acc{g}", [128, 8 * CS], F32) for g in range(NG)]

        bstack = ExitStack()

        def sbB(name, shape, dt):
            return bstack.enter_context(nc.sbuf_tensor(name, shape, dt))

        prot = sbB("prot", [128, 128], BF16)
        lns = sbB("lns", [128, 640], F32)
        oh16 = sbB("oh16", [128, 16 * H], BF16)
        sel16 = sbB("sel16", [16, 128 * (H // 2)], BF16)
        dti = sbB("dti", [128, 4 * CS], I16)
        offs = sbB("offs", [128, NSTEP], F32)
        bvrow = sbB("bvrow", [1, D], BF16)
        g1row = sbB("g1row", [128, D], BF16)
        b1row = sbB("b1row", [128, D], BF16)
        ropeb = sbB("ropeb", [128, 2 * CS], BF16)
        wstr = sbB("wstr", [128, 4 * D], BF16)
        pt_a = sbB("pt_a", [128, 2 * CS], BF16)
        pt_b = sbB("pt_b", [128, 2 * CS], BF16)
        pt_c = sbB("pt_c", [128, 2 * CS], BF16)
        cmpb = sbB("cmpb", [128, 4 * CS], BF16)
        lsum = [sbB(f"lsum{g}", [16, CS], F32) for g in range(NG)]
        dstack = ExitStack()
        qg = [dstack.enter_context(
            nc.sbuf_tensor(f"qg{g}", [128, 8 * CS], BF16, side="right"))
            for g in range(NG)]
        ffn_bufs = {}

        def ffn_alloc(stage):
            if stage in ffn_bufs:
                return
            if stage == 1:
                ffn_bufs[1] = (sb("actb1", [128, 4 * 2048], BF16),
                               sb("wp1a", [128, 4 * 1024], BF16),
                               sb("wp2a", [128, 4 * 1024], BF16))
            else:
                ffn_bufs[2] = (sb("actb2", [128, 4 * 2048], BF16),
                               sb("wp1b", [128, 4 * 1024], BF16),
                               sb("wp2b", [128, 4 * 1024], BF16),
                               sb("cwT", [8, TOK], BF16),
                               sb("cwb", [128, TOK], BF16),
                               sb("wgb", [128, 64], BF16))

        pst = en(nc.psum_tensor("pst", [128, 2048], F32))
        pav = en(nc.psum_tensor("pav", [128, 1024], F32))
        ps_l = en(nc.psum_tensor("ps_l", [128, 512], F32))
        ps_x = en(nc.psum_tensor("ps_x", [128, 512], F32))

        sx = Sem(en(nc.semaphore("sx")))
        sg = Sem(en(nc.semaphore("sg")))
        sxs = [Sem(en(nc.semaphore(f"sxs{i}"))) for i in range(4)]
        sxr = Sem(en(nc.semaphore("sxr")))
        swa = Sem(en(nc.semaphore("swa")))
        swb = Sem(en(nc.semaphore("swb")))
        swc = Sem(en(nc.semaphore("swc")))
        swd = Sem(en(nc.semaphore("swd")))
        sw1 = Sem(en(nc.semaphore("sw1")))
        sw2 = Sem(en(nc.semaphore("sw2")))
        swr = Sem(en(nc.semaphore("swr")))
        sw = Sem(en(nc.semaphore("sw")))
        st = Sem(en(nc.semaphore("st")))
        sv = Sem(en(nc.semaphore("sv")))
        ss = Sem(en(nc.semaphore("ss")))
        b.auto = {"v": sv, "s": ss}
        so = Sem(en(nc.semaphore("so")))
        block = en(nc.Block())

        def wv4(m4):
            return pst[:, m4 * 512:(m4 + 1) * 512]

        for dst, src in ((idb[:], idb_d), (idf[:], idf_d), (prot[:], prot_d),
                         (oh16[:], oh16_d), (sel16[:], sel16_d),
                         (oh8[:], oh8_d), (dti[:], dt_d), (offs[:], offs_d),
                         (biasf[:, BQ:BQ + 8], bq_f_d),
                         (biasf[:, BK:BK + 8], bk_f_d),
                         (biasf[:, B1C:B1C + 32], b1_f_d),
                         (biasf[:, BS1C:BS1C + 32], bs1_f_d),
                         (biasf[:, B1MC:B1MC + 256], b1m_f_d),
                         (biasf[:, LN2C:LN2C + 8], g2_f_d),
                         (biasf[:, LN2C + 8:LN2C + 16], b2l_f_d)):
            b.dma("y", dst, src, sem=sx)
        b.dma("g", g1row[0:1, :], g1_r_d, sem=sw)
        b.dma("g", b1row[0:1, :], b1l_r_d, sem=sw)
        b.dma("g", bvrow[0:1, :], bv_r_d, sem=sw)
        b.call("g", "memset", ones_r[:], 1.0)
        b.call("g", "memset", ones2[:], 1.0, sem=sg)
        b.call("v", "memset", st1[:], 0.0)
        b.call("v", "memset", stats[:], 0.0, sem=sv)
        for g in range(NG):
            b.call("v", "memset", acc[g][:], 0.0)
            b.call("v", "memset", lsum[g][:], 0.0, sem=sv)
        b.wait("t", sw)
        b.wait("t", sg)
        b.wait("t", sx)
        for row in (g1row, b1row):
            for h4 in range(2):
                sl = slice(h4 * 512, (h4 + 1) * 512)
                b.wait("t", sv)
                b.mm(ps_x[:, 0:512], ones_r[0:1, 0:128], row[0:1, sl], sem=st)
                b.wait("v", st)
                b.vcopy(row[:, sl], ps_x[:, 0:512], sem=sv)
        b.wait("t", sv)

        lncons = [[0, 0, 0] for _ in range(4)]

        def xload(srcap, row0, slot):
            sxk = sxs[slot]
            b.wait("y", sv, lncons[slot][0])
            b.wait("y", ss, lncons[slot][1])
            b.wait("y", st, lncons[slot][2])
            return b.dma("y", xsta[:, slot * D:(slot + 1) * D],
                         srcap[row0:row0 + 128, :], sem=sxk)

        def stat_tile(srcap, pos, t):
            slot = t % 4
            xl = xload(srcap, pos, slot)
            xt = xsta[:, slot * D:(slot + 1) * D]
            b.wait("s", sxs[slot], xl)
            b.act(xt, xt, AF.Identity, accum=lns[:, t:t + 1])
            b.act(xt, xt, AF.Square, accum=lns[:, 80 + t:81 + t], sem=ss)
            lncons[slot][1] = ss.n

        lns_ready = [0]

        def ln_chain(t0, t1, own_xn=False):
            b.wait("v", ss)
            n = t1 - t0
            me = lns[:, 160 + t0:160 + t1]
            va = lns[:, 240 + t0:240 + t1]
            r_ = lns[:, 320 + t0:320 + t1]
            rt = lns[:, 400 + t0:400 + t1]
            bi = lns[:, 480 + t0:480 + t1]
            b.ts(me, lns[:, t0:t1], 1.0 / D, None, ALU.mult)
            b.ts(va, lns[:, 80 + t0:80 + t1], 1.0 / D, None, ALU.mult)
            b.tt(r_, me, me, ALU.mult)
            b.tt(va, va, r_, ALU.subtract)
            b.ts(r_, va, LN_EPS, None, ALU.add, sem=sv)
            b.wait("s", sv)
            b.act(r_, r_, AF.Ln, sem=ss)
            b.wait("s", ss)
            b.act(r_, r_, AF.Exp, scale=-0.5, sem=ss)
            b.wait("v", ss)
            b.ts(rt, va, LN_EPS, None, ALU.add)
            b.call("v", "reciprocal", rt, rt)
            b.tt(rt, va, rt, ALU.mult)
            b.ts(rt, rt, LN_EPS, None, ALU.add, sem=sv)
            b.wait("s", sv)
            b.act(rt, rt, AF.Ln, sem=ss)
            b.wait("s", ss)
            b.act(rt, rt, AF.Exp, scale=-0.5, sem=ss)
            b.wait("v", ss)
            b.tt(rt, r_, rt, ALU.mult)
            b.tt(bi, me, rt, ALU.mult)
            b.ts(bi, bi, -1.0, None, ALU.mult)
            if own_xn:
                bo_ = lns[:, 560 + t0 - 64:560 + t1 - 64]
                b.tt(bo_, me, r_, ALU.mult)
                b.ts(bo_, bo_, -1.0, None, ALU.mult)
            b.call("v", "engine_nop", sem=sv)
            lns_ready[0] = sv.n

        def ln_apply(srcap, pos, tbase, mode):
            xl = [0] * 4
            for tt_ in range(4):
                xl[tt_] = xload(srcap, pos + tt_ * 128, tbase % 4 * 0 + tt_)
            for tt_ in range(4):
                t = tbase + tt_
                xt = xsta[:, tt_ * D:(tt_ + 1) * D]
                b.wait("s", sxs[tt_], xl[tt_])
                b.wait("s", sv, lns_ready[0])
                if mode == "normed":
                    sc, bi = lns[:, 400 + t:401 + t], lns[:, 480 + t:481 + t]
                else:
                    sc, bi = lns[:, 320 + t:321 + t], lns[:, 560 + t - 64:
                                                         561 + t - 64]
                b.act(ntok[:, tt_ * D:(tt_ + 1) * D], xt, AF.Identity,
                      bias=bi, scale=sc, sem=ss)
                lncons[tt_][1] = ss.n
                lncons[tt_][2] = st.n
            ln_s = ss.n
            if mode == "normed":
                b.wait("v", ss)
                for tt_ in range(4):
                    nslot = ntok[:, tt_ * D:(tt_ + 1) * D]
                    b.tt(nslot, nslot, g1row[:], ALU.mult)
                    b.tt(nslot, nslot, b1row[:], ALU.add, sem=sv)
                    lncons[tt_][0] = sv.n
            return ln_s, sv.n

        tp_state = [0, 0]

        def tpose4(dst, ln_s, ln_v):
            b.wait("t", ss, ln_s)
            b.wait("t", sv, ln_v)
            for tt_ in range(4):
                b.wait("t", sv, tp_state[0])
                last = 0
                for f in range(8):
                    last = b.mm(pst[:, 1024 + f * 128:1024 + (f + 1) * 128],
                                ntok[:, tt_ * D + f * 128:
                                     tt_ * D + (f + 1) * 128],
                                idb[:], sem=st)
                tp_state[1] = last
                lncons[tt_][2] = last
                b.wait("v", st, last)
                src3 = pst[:, 1024:2048].rearrange("p (f c) -> p f c", c=128)
                dst3 = dst.rearrange("p (f c) -> p f c", c=CS)[
                    :, :, tt_ * 128:(tt_ + 1) * 128]
                tp_state[0] = b.call("v", "tensor_copy", dst3, src3, sem=sv)
            return tp_state[0]

        bhist = []

        def stream_proj(src2_fn, rhs_fn, psum_fn, nm, nk=8, bias_mm=None,
                        qwaits=None):
            hist = bhist
            w_ = nm * 128
            for kk in range(0, nk, 4):
                u = (kk // 4) % 2
                sl = u * 2 * D
                swk = (swa, swb)[u]
                if len(hist) >= 2:
                    b.wait("g", st, hist[-2])
                wl = b.dma("g",
                           wstr[:, sl:sl + 4 * w_].rearrange(
                               "p (b c) -> p b c", b=4),
                           src2_fn(kk).rearrange("(b p) c -> p b c", p=128),
                           sem=swk)
                if kk == 0 and qwaits is not None:
                    for ws_, wv_2 in qwaits:
                        b.wait("t", ss, ws_)
                        b.wait("t", sv, wv_2)
                b.wait("t", swk, wl)
                last = None
                for bb2 in range(4):
                    k = kk + bb2
                    for m in range(nm):
                        last = b.mm(psum_fn(m),
                                    wstr[:, sl + bb2 * w_ + m * 128:
                                         sl + bb2 * w_ + (m + 1) * 128],
                                    rhs_fn(k), start=(k == 0),
                                    stop=(k == nk - 1 and bias_mm is None),
                                    sem=st)
                hist.append(last)
            if bias_mm is not None:
                bias_mm()
            return hist[-1]

        slot_hist = [(0, 0)] * 4
        rope_hist = [(0, 0), (0, 0)]

        def qk_proj_rope(is_q, dst, nfr=0):
            col0 = 0 if is_q else D
            bcol = BQ if is_q else BK
            evd = proj_waves(col0, bcol, nfr, rope_dst=dst)
            for m in range(4, 8):
                rope_tile(m, evd, dst)

        slot_reader = [0, 0, 0, 0]

        def band_dma(src2, u):
            sl = u * 1024
            swk = (swa, swb, swc, swd)[u]
            b.wait("g", st, slot_reader[u])
            wl = b.dma("g",
                       wstr[:, sl:sl + 1024].rearrange(
                           "p (b c) -> p b c", b=2),
                       src2.rearrange("(b p) c -> p b c", p=128), sem=swk)
            return (u, sl, swk, wl)

        def band_mms(dinfo, kk0, qwaits=None, stop_k=7):
            u, sl, swk, wl = dinfo
            b.wait("t", swk, wl)
            last = None
            for bb2 in range(2):
                k = kk0 + bb2
                for m in range(4):
                    if k == 0 and qwaits is not None:
                        b.wait("t", ss, qwaits[m][0])
                        b.wait("t", sv, qwaits[m][1])
                    last = b.mm(wv4(m),
                                wstr[:, sl + bb2 * 512 + m * 128:
                                     sl + bb2 * 512 + (m + 1) * 128],
                                nf[:, k * CS:(k + 1) * CS],
                                start=(k == 0), stop=(k == stop_k),
                                sem=st)
            slot_reader[u] = last
            bhist.append(last)

        def proj_waves(col0, bcol, nfr, pre=None, rope_dst=None,
                       next_fns=None):
            evd = [0] * 8
            b.wait("s", sv)
            b.wait("t", sv, nfr)
            KK = (0, 2, 4, 6)

            def ksrc(wv, kk0):
                return wqkv_d[kk0 * 128:(kk0 + 2) * 128,
                              col0 + wv * 512:col0 + (wv + 1) * 512]

            cur = pre if pre is not None else \
                [band_dma(ksrc(0, kk0), kk0 // 2) for kk0 in KK]
            for wv in range(2):
                nxt = []
                if wv == 1 and rope_dst is not None:
                    for m in range(4):
                        rope_tile(m, evd, rope_dst)
                for ui, kk0 in enumerate(KK):
                    band_mms(cur[ui], kk0,
                             qwaits=(slot_hist if kk0 == 0 else None))
                    if wv == 0:
                        nxt.append(band_dma(ksrc(1, kk0), ui))
                    elif next_fns is not None:
                        next_fns[ui]()
                for m4 in range(4):
                    m = wv * 4 + m4
                    b.wait("s", st, bhist[-1])
                    evd[m] = b.act(kpre[:, m * CS:(m + 1) * CS], wv4(m4),
                                   AF.Identity,
                                   bias=biasf[:, bcol + m:bcol + m + 1],
                                   sem=ss)
                    slot_hist[m4] = (evd[m], sv.n)
                cur = nxt
            return evd

        def rope_tile(m, evd, dst):
            pc = pav[:, (m % 2) * 512:(m % 2) * 512 + 512]
            b.wait("t", ss, evd[m])
            b.wait("t", sv, max(rope_hist[m % 2][0], fl_acc[m % 2]))
            rot = b.mm(pc, prot[:], kpre[:, m * CS:(m + 1) * CS], sem=st)
            b.wait("v", st, rot)
            b.tt(tmp1[:, 0:512], kpre[:, m * CS:(m + 1) * CS],
                 ropeb[:, 0:CS], ALU.mult)
            b.tt(tmp1[:, 512:1024], pc, ropeb[:, CS:2 * CS], ALU.mult)
            rv = b.tt(dst[:, m * CS:(m + 1) * CS], tmp1[:, 0:512],
                      tmp1[:, 512:1024], ALU.add, sem=sv)
            rope_hist[m % 2] = (rv, rot)

        def norms_update(src, stcol):
            sqd = [0, 0]
            rdd = [0, 0]
            for m in range(8):
                sq = pt_a[:, (m % 2) * 512:(m % 2) * 512 + 512]
                b.wait("s", st, sqd[m % 2])
                sqe = b.act(sq, src[:, m * CS:(m + 1) * CS], AF.Square,
                            sem=ss)
                psn = (ps_l, ps_x)[m % 2]
                b.wait("t", ss, sqe)
                if m >= 2:
                    b.wait("t", sv, rdd[m % 2])
                sqd[m % 2] = b.mm(psn[0:1, 0:512], ones2[:, 0:1], sq, sem=st)
                b.wait("v", st, sqd[m % 2])
                b.red("reduce_max", st1[0:1, 8:9], psn[0:1, 0:512])
                rdd[m % 2] = b.tt(st1[0:1, stcol:stcol + 1],
                                  st1[0:1, stcol:stcol + 1],
                                  st1[0:1, 8:9], ALU.max, sem=sv)
            b.wait("t", sv)

        fl_exp = [0, 0]
        fl_avl = [0, 0, 0]
        fl_acc = [0, 0]
        fl_lev = [0]
        fl_stage = [0]

        for g in range(NG):
            for tt_ in range(4):
                stat_tile(x_own, (g * 4 + tt_) * 128, 64 + g * 4 + tt_)
        ln_chain(64, 80, own_xn=True)
        for g in range(NG):
            b.wait("y", sv)
            b.dma("y", ropeb[:, 0:CS], ropeq_d[0:128, g * CS:(g + 1) * CS],
                  sem=sxr)
            rope_ld = b.dma("y", ropeb[:, CS:2 * CS],
                            ropeq_d[0:128, TOK + g * CS: TOK + (g + 1) * CS],
                            sem=sxr)
            ln_s, ln_v = ln_apply(x_own, g * CS, 64 + g * 4, "normed")
            nfr = tpose4(nf, ln_s, ln_v)
            b.wait("v", sxr, rope_ld)
            qk_proj_rope(True, qg[g], nfr)
            norms_update(kpre, 0)
            for cc in range(4 * g, 4 * g + 4):
                for tt_ in range(4):
                    stat_tile(x_pref, cc * CS + tt_ * 128, cc * 4 + tt_)

        sched_i = 0
        ln_chain(0, 64)
        ln_pipe = ln_apply(x_pref, 0, 0, "normed")
        kw0_pre = [band_dma(wqkv_d[kk0 * 128:(kk0 + 2) * 128, D:D + 512],
                            kk0 // 2) for kk0 in (0, 2, 4, 6)]
        for c in range(NCHUNK):
            b.wait("y", sv)
            b.dma("y", ropeb[:, 0:CS], ropek_d[0:128, c * CS:(c + 1) * CS],
                  sem=sxr)
            rope_ld = b.dma("y", ropeb[:, CS:2 * CS],
                            ropek_d[0:128, S + c * CS: S + (c + 1) * CS],
                            sem=sxr)
            ln_s, ln_v = ln_pipe
            nfr = tpose4(nf, ln_s, ln_v)
            if c + 1 < NCHUNK:
                ln_pipe = ln_apply(x_pref, (c + 1) * CS, (c + 1) * 4,
                                   "normed")
            b.wait("v", sxr, rope_ld)
            def vsrc(vh, kk0):
                return wqkv_d[kk0 * 128:(kk0 + 2) * 128,
                              2 * D + vh * 512:2 * D + (vh + 1) * 512]

            vd = {}

            def mk_vd(vh, ui):
                def f():
                    vd[(vh, ui)] = band_dma(vsrc(vh, 2 * ui), ui)
                return f

            def v_mms(kk0, dinfo):
                u, sl, swk, wl = dinfo
                if kk0 == 0:
                    b.wait("t", sv, nfr)
                    for m4 in range(4):
                        b.wait("t", ss, slot_hist[m4][0])
                        b.wait("t", sv, slot_hist[m4][1])
                b.wait("t", swk, wl)
                last = None
                for bb2 in range(2):
                    k = kk0 + bb2
                    for tt_ in range(4):
                        last = b.mm(wv4(tt_),
                                    nf[:, k * CS + tt_ * 128:
                                       k * CS + (tt_ + 1) * 128],
                                    wstr[:, sl + bb2 * 512:
                                         sl + bb2 * 512 + 512],
                                    start=(k == 0), stop=False, sem=st)
                slot_reader[u] = last
                bhist.append(last)

            def v_fin(vh):
                for tt_ in range(4):
                    b.mm(wv4(tt_), ones_r[0:1, 0:128],
                         bvrow[0:1, vh * 512:(vh + 1) * 512],
                         start=False, stop=True, sem=st)
                vlast = st.n
                for tt_ in range(4):
                    b.wait("s", st, vlast)
                    ev = b.act(vbuf[:, tt_ * D + vh * 512:
                                    tt_ * D + (vh + 1) * 512],
                               wv4(tt_), AF.Identity, sem=ss)
                    slot_hist[tt_] = (ev, sv.n)

            evd = proj_waves(D, BK, nfr, pre=kw0_pre, rope_dst=kbuf,
                             next_fns=[mk_vd(0, ui) for ui in range(4)])
            rope_tile(4, evd, kbuf)
            v_mms(0, vd[(0, 0)])
            mk_vd(1, 0)()
            rope_tile(5, evd, kbuf)
            v_mms(2, vd[(0, 1)])
            mk_vd(1, 1)()
            rope_tile(6, evd, kbuf)
            v_mms(4, vd[(0, 2)])
            mk_vd(1, 2)()
            rope_tile(7, evd, kbuf)
            v_mms(6, vd[(0, 3)])
            mk_vd(1, 3)()
            v_fin(0)
            v_mms(0, vd[(1, 0)])
            v_mms(2, vd[(1, 1)])
            v_mms(4, vd[(1, 2)])
            v_mms(6, vd[(1, 3)])
            v_fin(1)
            norms_update(kpre, 1)
            if c + 1 < NCHUNK:
                kw0_pre = [band_dma(wqkv_d[kk0 * 128:(kk0 + 2) * 128,
                                           D:D + 512], kk0 // 2)
                           for kk0 in (0, 2, 4, 6)]
            b.wait("s", sv)
            b.act(st1[0:1, 2:3], st1[0:1, 0:1], AF.Ln, sem=ss)
            b.act(st1[0:1, 3:4], st1[0:1, 1:2], AF.Ln, sem=ss)
            b.wait("v", ss)
            b.tt(st1[0:1, 8:9], st1[0:1, 2:3], st1[0:1, 3:4], ALU.add)
            b.ts(st1[0:1, 8:9], st1[0:1, 8:9], 0.5, -math.log(8.0),
                 ALU.mult, ALU.add, sem=sv)
            b.wait("s", sv)
            b.act(st1[0:1, 5:6], st1[0:1, 8:9], AF.Exp, sem=ss)
            b.wait("v", ss)
            b.vcopy(pt_a[0:1, 0:1], st1[0:1, 5:6])
            b.vcopy(st1[0:1, 5:6], pt_a[0:1, 0:1])
            b.tt(st1[0:1, 8:9], st1[0:1, 4:5], st1[0:1, 5:6], ALU.subtract)
            b.ts(st1[0:1, 7:8], st1[0:1, 5:6], -1.0, None, ALU.mult, sem=sv)
            b.wait("s", sv)
            b.act(st1[0:1, 6:7], st1[0:1, 8:9], AF.Exp, sem=ss)
            b.wait("v", ss)
            b.vcopy(st1[0:1, 4:5], st1[0:1, 5:6])
            b.vcopy(pt_a[0:1, 0:2], st1[0:1, 6:8], sem=sv)
            b.wait("t", sv)
            b.wait("t", ss)
            b.mm(ps_x[:, 0:2], ones_r[0:1, 0:128], pt_a[0:1, 0:2], sem=st)
            b.wait("v", st)
            b.wait("v", ss)
            b.vcopy(stats[:, 1:2], ps_x[:, 0:1])
            stats_rdy = b.vcopy(stats[:, 0:1], ps_x[:, 1:2], sem=sv)
            kv_rdy_v = sv.n
            kv_rdy_s = ss.n

            first_of_chunk = True
            while sched_i < NSTEP and SCHED[sched_i][0] == c:
                _, g = SCHED[sched_i]
                step = sched_i
                sched_i += 1
                candidate = (4 * g <= c)
                cmp_done = 0
                if candidate:
                    for kt in range(4):
                        cmp_done = b.ts(cmpb[:, kt * CS:(kt + 1) * CS],
                                        dti[:, kt * CS:(kt + 1) * CS],
                                        offs[:, step:step + 1], None,
                                        ALU.is_le, sem=sv)
                first_l = True

                def emit_avl(p):
                    nonlocal first_l
                    pt3, rdy, rdy_sem, hp_, kt_, g_, last = p
                    ptb = (pt_a, pt_b, pt_c)[pt3]
                    b.wait("t", rdy_sem, rdy)
                    if first_l:
                        b.wait("t", sv, fl_lev[0])
                    b.mm(ps_l[0:16, 0:512],
                         oh16[:, 16 * (2 * hp_):16 * (2 * hp_) + 16],
                         ptb[:, 0:512], start=first_l, stop=False)
                    b.mm(ps_l[0:16, 0:512],
                         oh16[:, 16 * (2 * hp_ + 1):16 * (2 * hp_ + 1) + 16],
                         ptb[:, 512:1024], start=False, stop=last)
                    first_l = False
                    if kt_ == 0:
                        b.wait("t", sv, max(fl_acc[0], fl_acc[1],
                                            rope_hist[0][0],
                                            rope_hist[1][0]))
                    b.mm(pav[0:64, 0:512],
                         vbuf[:, kt_ * D + hp_ * 128:kt_ * D + hp_ * 128 + 64],
                         ptb[:, 0:512], tp=(0, 0),
                         start=(kt_ == 0), stop=(kt_ == 3))
                    av = b.mm(pav[64:128, 512:1024],
                              vbuf[:, kt_ * D + hp_ * 128 + 64:
                                   kt_ * D + hp_ * 128 + 128],
                              ptb[:, 512:1024], tp=(0, 64),
                              start=(kt_ == 0), stop=(kt_ == 3), sem=st)
                    fl_avl[pt3] = av
                    if kt_ == 3:
                        b.wait("v", st, av)
                        aslice = acc[g_][:, hp_ * CS:(hp_ + 1) * CS]
                        fl_acc[0] = b.stt(aslice[0:64, :], aslice[0:64, :],
                                          stats[0:64, 1:2], pav[0:64, 0:512],
                                          ALU.mult, ALU.add, sem=sv)
                        fl_acc[1] = b.stt(aslice[64:128, :],
                                          aslice[64:128, :],
                                          stats[64:128, 1:2],
                                          pav[64:128, 512:1024],
                                          ALU.mult, ALU.add, sem=sv)

                pending = []
                for hp in range(8):
                    for kt in range(4):
                        sl_ = fl_stage[0] % 2
                        pt3 = fl_stage[0] % 3
                        fl_stage[0] += 1
                        pslot = pst[:, sl_ * 1024:(sl_ + 1) * 1024]
                        ptb = (pt_a, pt_b, pt_c)[pt3]
                        kc = hp * CS + kt * 128
                        b.wait("t", ss, fl_exp[sl_])
                        if first_of_chunk:
                            b.wait("t", sv, kv_rdy_v)
                            b.wait("t", ss, kv_rdy_s)
                            first_of_chunk = False
                        b.mm(pslot[:, 0:512], kbuf[0:64, kc:kc + 128],
                             qg[g][0:64, hp * CS:(hp + 1) * CS], tp=(0, 0))
                        p2 = b.mm(pslot[:, 512:1024],
                                  kbuf[64:128, kc:kc + 128],
                                  qg[g][64:128, hp * CS:(hp + 1) * CS],
                                  tp=(64, 0), sem=st)
                        b.wait("s", st, p2)
                        b.wait("s", st, fl_avl[pt3])
                        ex = b.act(ptb[:, 0:1024], pslot[:, 0:1024], AF.Exp,
                                   bias=stats[:, 0:1], sem=ss)
                        fl_exp[sl_] = ex
                        if candidate:
                            b.wait("v", ss, ex)
                            b.tt(ptb[:, 0:512], ptb[:, 0:512],
                                 cmpb[:, kt * CS:(kt + 1) * CS], ALU.mult)
                            mk = b.tt(ptb[:, 512:1024], ptb[:, 512:1024],
                                      cmpb[:, kt * CS:(kt + 1) * CS],
                                      ALU.mult, sem=sv)
                            rdy, rdy_sem = mk, sv
                        else:
                            rdy, rdy_sem = ex, ss
                        if len(pending) >= 2:
                            emit_avl(pending.pop(0))
                        pending.append((pt3, rdy, rdy_sem, hp, kt, g,
                                        hp == 7 and kt == 3))
                for p_ in pending:
                    emit_avl(p_)
                pending = []
                b.wait("v", st)
                fl_lev[0] = b.stt(lsum[g][0:16, :], lsum[g][0:16, :],
                                  stats[0:16, 1:2], ps_l[0:16, 0:512],
                                  ALU.mult, ALU.add, sem=sv)
                if c == 4 * g + 3:
                    b.call("v", "reciprocal", lsum[g][0:16, :],
                           lsum[g][0:16, :])
                    b.vcopy(pt_a[0:16, 0:512], lsum[g][0:16, :], sem=sv)
                    b.wait("t", sv)
                    for hp in range(8):
                        pc = pav[:, (hp % 2) * 512:(hp % 2) * 512 + 512]
                        b.wait("t", sv)
                        b.mm(pc, sel16[:, hp * 128:(hp + 1) * 128],
                             pt_a[0:16, 0:512], sem=st)
                        b.wait("v", st)
                        b.tt(qg[g][:, hp * CS:(hp + 1) * CS],
                             acc[g][:, hp * CS:(hp + 1) * CS], pc, ALU.mult,
                             sem=sv)
                    b.wait("t", sv)
                    fl_exp[0] = ss.n
                    fl_exp[1] = ss.n
                    fl_avl[0] = st.n
                    fl_avl[1] = st.n
                    fl_avl[2] = st.n
                    fl_acc[0] = sv.n
                    fl_acc[1] = sv.n
                    rope_hist[0] = (sv.n, st.n)
                    rope_hist[1] = (sv.n, st.n)

        def racc(f, g):
            return acc[g][:, f * CS:(f + 1) * CS]

        xnf = [kbuf, vbuf, kpre, nf]

        def x1t(k):
            if k < 2:
                return pt_a[:, k * 512:(k + 1) * 512]
            if k < 4:
                return pt_b[:, (k - 2) * 512:(k - 2) * 512 + 512]
            return cmpb[:, (k - 4) * 512:(k - 4) * 512 + 512]

        b.wait("g", st)
        brow_ld = b.dma("g", brow[0:1, :], bo_r_d, sem=swr)
        for g in range(NG):
            for tt_ in range(4):
                sl = tt_ * D
                sxk = sxs[tt_]
                t_ = 64 + g * 4 + tt_
                xl = xload(x_own, g * CS + tt_ * 128, tt_)
                b.wait("s", sxs[tt_], xl)
                b.wait("s", sv, lns_ready[0])
                b.act(ntok[:, tt_ * D:(tt_ + 1) * D], xsta[:, sl:sl + D],
                      AF.Identity, bias=lns[:, 560 + t_ - 64:561 + t_ - 64],
                      scale=lns[:, 320 + t_:321 + t_], sem=ss)
                lncons[tt_][1] = ss.n
                b.wait("t", sxs[tt_], xl)
                b.wait("t", ss)
                b.wait("t", sv)
                for f in range(8):
                    b.mm(pst[:, f * 128:(f + 1) * 128],
                         xsta[:, sl + f * 128: sl + (f + 1) * 128], idf[:],
                         sem=None)
                for f in range(8):
                    b.mm(pst[:, 1024 + f * 128:1024 + (f + 1) * 128],
                         ntok[:, tt_ * D + f * 128: tt_ * D + (f + 1) * 128],
                         idb[:], sem=(st if f == 7 else None))
                b.wait("v", st)
                xn3 = xnf[g].rearrange("p (f c) -> p f c", c=CS)[
                    :, :, tt_ * 128:(tt_ + 1) * 128]
                b.vcopy(xn3, pst[:, 1024:2048].rearrange(
                    "p (f c) -> p f c", c=128))
                ac3 = acc[g].rearrange("p (f c) -> p f c", c=CS)[
                    :, :, tt_ * 128:(tt_ + 1) * 128]
                b.tt(ac3, pst[:, 0:1024].rearrange("p (f c) -> p f c", c=128),
                     xn3, ALU.add, sem=sv)
                lncons[tt_] = [sv.n, ss.n, st.n]
                b.wait("t", sv)
            b.wait("t", ss)
            b.wait("t", swr, brow_ld)
            for wv_ in range(2):
                def bias_mm(wv2=wv_):
                    for m4 in range(4):
                        m = wv2 * 4 + m4
                        b.mm(wv4(m4), brow[0:1, m * 128:(m + 1) * 128],
                             ones_r[0:1, 0:512], start=False, stop=True,
                             sem=st)
                b.wait("t", sv)
                stream_proj(lambda kk, w2=wv_: wo_d[kk * 128:(kk + 4) * 128,
                                                    w2 * 512:(w2 + 1) * 512],
                            lambda k: qg[g][:, k * CS:(k + 1) * CS],
                            wv4, nm=4, bias_mm=bias_mm)
                b.wait("v", st)
                for m4 in range(4):
                    m = wv_ * 4 + m4
                    b.tt(racc(m, g), racc(m, g), wv4(m4), ALU.add)
                    b.tt(x1t(m), xnf[g][:, m * CS:(m + 1) * CS], wv4(m4),
                         ALU.add, sem=sv)
            if DEV_STAGE >= 2:
                b.wait("t", sv)
                sqb = ntok
                for f in range(8):
                    b.mm(pst[:, 0:512], ones2[:], x1t(f), start=(f == 0),
                         stop=(f == 7), sem=st)
                sqmm = [0, 0]
                for f in range(8):
                    sq = sqb[:, (f % 2) * 512:(f % 2) * 512 + 512]
                    if f >= 2:
                        b.wait("v", st, sqmm[f % 2])
                    b.tt(sq, x1t(f), x1t(f), ALU.mult, sem=sv)
                    b.wait("t", sv)
                    sqmm[f % 2] = b.mm(pst[:, 512:1024], ones2[:], sq,
                                       start=(f == 0), stop=(f == 7), sem=st)
                b.wait("v", st)
                mb = tmp1[:, 0:512]
                rb = tmp1[:, 512:1024]
                b.ts(mb, pst[:, 0:512], 1.0 / D, None, ALU.mult)
                b.ts(rb, pst[:, 512:1024], 1.0 / D, None, ALU.mult)
                b.tt(xsta[:, 0:512], mb, mb, ALU.mult)
                b.tt(rb, rb, xsta[:, 0:512], ALU.subtract)
                b.ts(rb, rb, LN_EPS, None, ALU.add, sem=sv)
                b.wait("s", sv)
                b.act(rb, rb, AF.Ln, sem=ss)
                b.wait("s", ss)
                b.act(rb, rb, AF.Exp, scale=-0.5, sem=ss)
                b.wait("v", ss)
                for f in range(8):
                    qsl = qg[g][:, f * CS:(f + 1) * CS]
                    b.tt(qsl, x1t(f), mb, ALU.subtract)
                    b.tt(qsl, qsl, rb, ALU.mult)
                    b.ts(qsl, qsl,
                         biasf[:, LN2C + f:LN2C + f + 1],
                         biasf[:, LN2C + 8 + f:LN2C + 8 + f + 1],
                         ALU.mult, ALU.add, sem=sv)
                for tt_ in range(4):
                    lncons[tt_] = [sv.n, ss.n, st.n]
                b.wait("t", sv)

        ffn_ld = [0, 0]

        def ffn_phase(rhs_fn, w1_src, w2_src, b1col, actfn, b2rhs_fn,
                      evac, Fp, prescale=False):
            nf_ = Fp // 128
            npiece = 4096 // Fp

            def w1sl(k):
                if Fp == 512:
                    return ffn_bufs[1][1][:, k * Fp:(k + 1) * Fp]
                buf = ffn_bufs[1][1] if k < 4 else ffn_bufs[2][1]
                return buf[:, (k % 4) * Fp:(k % 4 + 1) * Fp]

            def w2sl(kt):
                if nf_ <= 4:
                    return ffn_bufs[1][2][:, kt * 1024:(kt + 1) * 1024]
                buf = ffn_bufs[1][2] if kt < 4 else ffn_bufs[2][2]
                return buf[:, (kt % 4) * 1024:(kt % 4 + 1) * 1024]

            def asl(f, g4):
                buf = ffn_bufs[1][0] if f < 4 else ffn_bufs[2][0]
                return buf[:, (f % 4) * TOK + g4 * CS:
                           (f % 4) * TOK + (g4 + 1) * CS]

            cwbuf = ffn_bufs[2][4] if 2 in ffn_bufs else None
            for p in range(npiece):
                b.wait("g", st, ffn_ld[0])
                for k in range(8):
                    b.dma("g", w1sl(k), w1_src(k, p), sem=sw1)
                w1_ld = sw1.n
                b.wait("g", st, ffn_ld[1])
                for kt in range(nf_):
                    b.dma("g", w2sl(kt), w2_src(kt, p), sem=sw2)
                w2_ld = sw2.n
                b.wait("t", sw1, w1_ld)
                for f in range(nf_):
                    for g4 in range(4):
                        sl = (f * 4 + g4) % 4
                        b.wait("t", ss, slot_hist[sl][0])
                        b.wait("t", sv, slot_hist[sl][1])
                        for k in range(8):
                            b.mm(wv4(sl),
                                 w1sl(k)[:, f * 128:(f + 1) * 128],
                                 rhs_fn(g4, k), start=(k == 0), stop=(k == 7),
                                 sem=(st if k == 7 else None))
                        b.wait("s", st)
                        if f == nf_ - 1 and g4 == 3:
                            ffn_ld[0] = st.n
                        aslice = asl(f, g4)
                        sev = b.act(aslice, wv4(sl), actfn,
                                    bias=b1col(p, f), sem=ss)
                        vev = sv.n
                        if prescale:
                            b.wait("v", ss)
                            vev = b.tt(aslice, aslice,
                                       cwbuf[:, g4 * CS:(g4 + 1) * CS],
                                       ALU.mult, sem=sv)
                        slot_hist[sl] = (sev, vev)
                b.wait("t", sw2, w2_ld)
                b.wait("t", ss)
                if prescale:
                    b.wait("t", sv)
                for m in range(8):
                    for g4 in range(4):
                        sl = (m * 4 + g4) % 4
                        b.wait("t", ss, slot_hist[sl][0])
                        b.wait("t", sv, slot_hist[sl][1])
                        last_bias = (p == 0 and b2rhs_fn)
                        for kt in range(nf_):
                            b.mm(wv4(sl),
                                 w2sl(kt)[:, m * 128:(m + 1) * 128],
                                 asl(kt, g4),
                                 start=(kt == 0),
                                 stop=(kt == nf_ - 1 and not last_bias),
                                 sem=(None if (kt < nf_ - 1 or last_bias)
                                      else st))
                        if last_bias:
                            b.mm(wv4(sl), brow[0:1, m * 128:(m + 1) * 128],
                                 b2rhs_fn(g4), start=False, stop=True,
                                 sem=st)
                        b.wait("v", st)
                        if m == 7 and g4 == 3:
                            ffn_ld[1] = st.n
                        vev = evac(m, g4, wv4(sl))
                        slot_hist[sl] = (ss.n, vev)

        bstack.close()

        if DEV_STAGE >= 2:
            ffn_alloc(1)
            b.wait("g", st)
            b.wait("g", sv)
            rl = b.dma("g", brow[0:1, :], b2_r_d, sem=swr)
            b.wait("t", swr, rl)

            def mlp_evac(m, g4, ps):
                return b.tt(racc(m, g4), racc(m, g4), ps, ALU.add, sem=sv)

            ffn_phase(lambda g4, k: qg[g4][:, k * CS:(k + 1) * CS],
                      lambda k, p: w1_d[k * 128:(k + 1) * 128,
                                        p * 512:(p + 1) * 512],
                      lambda kt, p: w2_d[p * 512 + kt * 128:
                                         p * 512 + (kt + 1) * 128, 0:1024],
                      lambda p, f: biasf[:, B1C + p * 4 + f:
                                         B1C + p * 4 + f + 1],
                      AF.Gelu,
                      lambda g4: ones_r[0:1, 0:512],
                      mlp_evac, Fp=512)

        dstack.close()

        if DEV_STAGE >= 3:
            ffn_alloc(1)
            ffn_alloc(2)
            cwT = ffn_bufs[2][3]
            cwb = ffn_bufs[2][4]
            wgb = ffn_bufs[2][5]
            b.wait("g", st)
            for k in range(8):
                rl = b.dma("g", wgb[:, k * 8:(k + 1) * 8],
                           wg_d[k * 128:(k + 1) * 128, :], sem=swr)
            b.wait("t", swr, rl)
            b.wait("t", sv)
            b.wait("t", ss)
            for t16 in range(16):
                g4, tt_ = divmod(t16, 4)
                for k in range(8):
                    b.mm(ps_l[:, t16 * 8:(t16 + 1) * 8],
                         xnf[g4][:, k * CS + tt_ * 128:
                                 k * CS + (tt_ + 1) * 128],
                         wgb[:, k * 8:(k + 1) * 8], start=(k == 0),
                         stop=(k == 7), sem=(st if k == 7 else None))
            b.wait("s", st)
            b.act(tmp1[:, 0:128], ps_l[:, 0:128], AF.Exp, sem=ss)
            b.wait("v", ss)
            for t16 in range(16):
                e8 = tmp1[:, t16 * 8:(t16 + 1) * 8]
                scr = tmp1[:, 128:136]
                scrb = tmp1[:, 136:144]
                b.red("reduce_sum", stats[:, 2:3], e8)
                b.red("reduce_max", stats[:, 3:4], e8)
                b.ts(scr, e8, stats[:, 3:4], None, ALU.is_ge)
                b.stt(scrb, scr, -1e30, e8, ALU.mult, ALU.add)
                b.red("reduce_max", stats[:, 4:5], scrb)
                b.ts(scr, e8, stats[:, 4:5], None, ALU.is_ge)
                b.tt(stats[:, 5:6], stats[:, 3:4], stats[:, 4:5], ALU.add)
                b.stt(stats[:, 5:6], stats[:, 2:3], 1e-9, stats[:, 5:6],
                      ALU.mult, ALU.add)
                b.call("v", "reciprocal", stats[:, 6:7], stats[:, 5:6])
                b.tt(scr, e8, scr, ALU.mult)
                b.ts(ntok[:, t16 * 8:(t16 + 1) * 8], scr, stats[:, 6:7],
                     None, ALU.mult, sem=sv)
                b.wait("t", sv)
                b.mm(ps_x[0:8, (t16 % 2) * 128:(t16 % 2) * 128 + 128],
                     ntok[:, t16 * 8:(t16 + 1) * 8], idb[:], sem=st)
                b.wait("v", st)
                b.vcopy(cwT[0:8, t16 * 128:(t16 + 1) * 128],
                        ps_x[0:8, (t16 % 2) * 128:(t16 % 2) * 128 + 128],
                        sem=sv)
            for e in range(8):
                b.wait("t", sv)
                b.wait("t", ss)
                for g4 in range(4):
                    b.wait("t", ss)
                    b.mm(pav[:, (g4 % 2) * 512:(g4 % 2) * 512 + 512],
                         oh8[:, e * 128:(e + 1) * 128],
                         cwT[0:8, g4 * 512:(g4 + 1) * 512], sem=st)
                    b.wait("s", st)
                    b.wait("s", sv)
                    b.act(cwb[:, g4 * CS:(g4 + 1) * CS],
                          pav[:, (g4 % 2) * 512:(g4 % 2) * 512 + 512],
                          AF.Identity, sem=ss)
                b.wait("g", ss)
                b.wait("g", st)
                rl = b.dma("g", brow[0:1, :],
                           b2m_r_d[0:1, e * D:(e + 1) * D], sem=swr)
                b.wait("t", swr, rl)
                b.wait("t", ss)

                def moe_evac(m, g4, ps):
                    return b.tt(racc(m, g4), racc(m, g4), ps, ALU.add,
                                sem=sv)

                ffn_phase(lambda g4, k: xnf[g4][:, k * CS:(k + 1) * CS],
                          lambda k, p, e2=e: mw1_d[e2][k * 128:(k + 1) * 128,
                                                       p * 1024:(p + 1) * 1024],
                          lambda kt, p, e2=e: mw2_d[e2][p * 1024 + kt * 128:
                                                        p * 1024 + (kt + 1) * 128,
                                                        0:1024],
                          lambda p, f, e2=e: biasf[:, B1MC + e2 * 32 + p * 8 + f:
                                                   B1MC + e2 * 32 + p * 8 + f + 1],
                          AF.Silu,
                          lambda g4: cwb[0:1, g4 * CS:(g4 + 1) * CS],
                          moe_evac, Fp=1024, prescale=True)

        if DEV_STAGE >= 4:
            b.wait("g", st)
            b.wait("g", sv)
            rl = b.dma("g", brow[0:1, :], bs2_r_d, sem=swr)
            b.wait("t", swr, rl)

            def sh_evac(m, g4, ps):
                return b.stt(racc(m, g4), ps, 0.1, racc(m, g4), ALU.mult,
                             ALU.add, sem=sv)

            ffn_phase(lambda g4, k: xnf[g4][:, k * CS:(k + 1) * CS],
                      lambda k, p: ws1_d[k * 128:(k + 1) * 128,
                                         p * 1024:(p + 1) * 1024],
                      lambda kt, p: ws2_d[p * 1024 + kt * 128:
                                          p * 1024 + (kt + 1) * 128, 0:1024],
                      lambda p, f: biasf[:, BS1C + p * 8 + f:
                                         BS1C + p * 8 + f + 1],
                      AF.Silu,
                      lambda g4: ones_r[0:1, 0:512],
                      sh_evac, Fp=1024)

        for g in range(NG):
            for tt_ in range(4):
                b.wait("t", sv)
                b.wait("t", ss)
                for f in range(8):
                    b.mm(pst[:, f * 128:(f + 1) * 128],
                         racc(f, g)[:, tt_ * 128:(tt_ + 1) * 128], idf[:],
                         sem=(st if f == 7 else None))
                b.wait("v", st)
                b.wait("v", so)
                b.vcopy(tmp1[:, 0:1024], pst[:, 0:1024], sem=sv)
                b.wait("y", sv)
                row0 = g * CS + tt_ * 128
                b.dma("y", out_d[row0:row0 + 128, :], tmp1[:, 0:1024], sem=so)
        b.wait("y", so)

        @block.tensor
        def _(E):
            for fn in b.q["t"]:
                fn(E)

        @block.vector
        def _(E):
            for fn in b.q["v"]:
                fn(E)

        @block.scalar
        def _(E):
            for fn in b.q["s"]:
                fn(E)

        @block.gpsimd
        def _(E):
            for fn in b.q["g"]:
                fn(E)

        @block.sync
        def _(E):
            for fn in b.q["y"]:
                fn(E)

    return nc


def _host_constants():
    inv = 1.0 / (10000.0 ** (np.arange(0, HD, 2, dtype=np.float64) / HD))
    t = np.arange(S, dtype=np.float64)
    fr = t[:, None] * inv[None, :]
    emb = np.concatenate([fr, fr], axis=1)
    cos = np.cos(emb).astype(np.float32)
    sin = np.sin(emb).astype(np.float32)
    cosf = np.concatenate([cos.T, cos.T], axis=0)
    sinf = np.concatenate([sin.T, sin.T], axis=0)
    iden = np.eye(128, dtype=np.float32)
    prot = np.zeros((128, 128), dtype=np.float32)
    for blk in range(2):
        o = blk * 64
        for m in range(32):
            prot[o + m + 32, o + m] = -1.0
            prot[o + m, o + m + 32] = 1.0
    oh16 = np.zeros((128, 16 * H), dtype=np.float32)
    for h in range(H):
        oh16[:, 16 * h + h] = 1.0
    sel16 = np.zeros((16, 128 * (H // 2)), dtype=np.float32)
    for hp in range(H // 2):
        sel16[2 * hp, hp * 128: hp * 128 + 64] = 1.0
        sel16[2 * hp + 1, hp * 128 + 64: hp * 128 + 128] = 1.0
    oh8 = np.zeros((8, 8 * 128), dtype=np.float32)
    for e in range(8):
        oh8[e, e * 128:(e + 1) * 128] = 1.0
    dt4 = np.zeros((128, 4 * CS), dtype=np.int16)
    for kt in range(4):
        p = np.arange(128)[:, None]
        q = np.arange(CS)[None, :]
        dt4[:, kt * CS:(kt + 1) * CS] = (128 * kt + p - q).astype(np.int16)
    return cosf, sinf, iden, prot, oh16, sel16, oh8, dt4


_PROG_CACHE = {}


def kernel(**inp):
    cosf, sinf, iden, prot, oh16, sel16, oh8, dt4 = _host_constants()
    x = np.asarray(inp["L0_attn_x"], dtype=np.float32)
    wqkv = np.ascontiguousarray(inp["L0_attn_Wqkv"], dtype=np.float32)
    bqkv = np.asarray(inp["L0_attn_bqkv"], dtype=np.float32)

    def bf(a):
        return np.ascontiguousarray(a).astype(BF)

    common = {
        "offs_h": None,
        "ropek": bf(np.concatenate([cosf, sinf], axis=1)),
        "dtiles": dt4,
        "idb_h": bf(iden), "idf_h": iden, "prot_h": bf(prot),
        "oh16_h": bf(oh16), "sel16_h": bf(sel16), "oh8_h": bf(oh8),
        "Wqkv": wqkv,
        "Wo": np.ascontiguousarray(inp["L0_attn_Wo"], dtype=np.float32),
        "W1": np.ascontiguousarray(inp["L0_attn_W1"], dtype=np.float32),
        "W2": np.ascontiguousarray(inp["L0_attn_W2"], dtype=np.float32),
        "Wg": np.ascontiguousarray(inp["L0_ffn_Wg"], dtype=np.float32),
        "MW1": np.ascontiguousarray(inp["L0_ffn_W1"], dtype=np.float32),
        "MW2": np.ascontiguousarray(inp["L0_ffn_W2"], dtype=np.float32),
        "Ws1": np.ascontiguousarray(inp["L0_ffn_Ws1"], dtype=np.float32),
        "Ws2": np.ascontiguousarray(inp["L0_ffn_Ws2"], dtype=np.float32),
        "bq_f": np.ascontiguousarray(bqkv[0:D].reshape(8, 128).T),
        "bk_f": np.ascontiguousarray(bqkv[D:2 * D].reshape(8, 128).T),
        "b1_f": np.ascontiguousarray(
            np.asarray(inp["L0_attn_b1"], np.float32).reshape(32, 128).T),
        "bs1_f": np.ascontiguousarray(
            np.asarray(inp["L0_ffn_bs1"], np.float32).reshape(32, 128).T),
        "b1m_f": np.ascontiguousarray(
            np.asarray(inp["L0_ffn_b1"], np.float32).reshape(8, 32, 128)
            .transpose(2, 0, 1).reshape(128, 256)),
        "g2_f": np.ascontiguousarray(
            np.asarray(inp["L0_attn_ln2_g"], np.float32).reshape(8, 128).T),
        "b2l_f": np.ascontiguousarray(
            np.asarray(inp["L0_attn_ln2_b"], np.float32).reshape(8, 128).T),
        "g1_r": np.asarray(inp["L0_attn_ln1_g"], np.float32)[None, :],
        "b1l_r": np.asarray(inp["L0_attn_ln1_b"], np.float32)[None, :],
        "bv_r": bqkv[2 * D:3 * D][None, :],
        "bo_r": np.asarray(inp["L0_attn_bo"], np.float32)[None, :],
        "b2_r": np.asarray(inp["L0_attn_b2"], np.float32)[None, :],
        "bs2_r": np.asarray(inp["L0_ffn_bs2"], np.float32)[None, :],
        "b2m_r": np.ascontiguousarray(
            np.asarray(inp["L0_ffn_b2"], np.float32).reshape(1, 8 * D)),
    }

    in_maps = []
    for core in range(8):
        bb, j = divmod(core, 4)
        chunks = [4 * g + j for g in range(NG)]
        m = dict(common)
        m["x_pref"] = x[bb]
        m["x_own"] = np.ascontiguousarray(
            np.concatenate([x[bb, ch * CS:(ch + 1) * CS] for ch in chunks]))
        offs = np.zeros((128, NSTEP), dtype=np.float32)
        for si, (c, g) in enumerate(SCHED):
            offs[:, si] = (4 * g + j - c) * CS
        m["offs_h"] = offs
        cols = np.concatenate(
            [np.arange(ch * CS, (ch + 1) * CS) for ch in chunks])
        m["ropeq"] = bf(np.concatenate(
            [cosf[:, cols] / 8.0, sinf[:, cols] / 8.0], axis=1))
        in_maps.append(m)

    key = DEV_STAGE
    if key not in _PROG_CACHE:
        _PROG_CACHE[key] = build_program()
    nc = _PROG_CACHE[key]
    res = run_bass_kernel_spmd(nc, in_maps, core_ids=list(range(8)))
    out = np.zeros((B, S, D), dtype=np.float32)
    for core in range(8):
        bb, j = divmod(core, 4)
        r = res.results[core]["out"]
        for g in range(NG):
            ch = 4 * g + j
            out[bb, ch * CS:(ch + 1) * CS] = r[g * CS:(g + 1) * CS]
    return out


# version 23
# speedup vs baseline: 1.0529x; 1.0073x over previous
import math
import numpy as np
import ml_dtypes
import concourse.bass as bass
import concourse.mybir as mybir
from concourse.bass_utils import run_bass_kernel_spmd

F32 = mybir.dt.float32
BF16 = mybir.dt.bfloat16
I16 = mybir.dt.int16
AF = mybir.ActivationFunctionType
ALU = mybir.AluOpType
AX = mybir.AxisListType
BF = ml_dtypes.bfloat16

B, S, D = 2, 8192, 1024
H, HD = 16, 64
NCHUNK = 16
CS = 512
NG = 4
TOK = NG * CS
LN_EPS = 1e-5
MASKVAL = -30000.0
DEV_STAGE = 4

SCHED = [(c, g) for c in range(NCHUNK) for g in range(NG) if 4 * g + 3 >= c]
NSTEP = len(SCHED)


class Sem:
    __slots__ = ("h", "n")

    def __init__(self, h):
        self.h = h
        self.n = 0


class KB:
    def __init__(self):
        self.q = {k: [] for k in "tvsgy"}
        self.auto = {}

    def on(self, e, fn):
        self.q[e].append(fn)

    def wait(self, e, sem, val=None):
        v = sem.n if val is None else val
        if v > 0:
            self.on(e, lambda E, h=sem.h, vv=v: E.wait_ge(h, vv))

    def call(self, e, meth, *args, sem=None, inc=1, **kw):
        if e in self.auto and meth != "dma_start":
            own = self.auto[e]
            if own.n > 0:
                self.on(e, lambda E, h=own.h, vv=own.n: E.wait_ge(h, vv))
            assert sem is None or sem is own
            sem = own
        if sem is not None:
            sem.n += inc

        def fn(E, meth=meth, args=args, kw=kw, sem=sem, inc=inc):
            ins = getattr(E, meth)(*args, **kw)
            if sem is not None:
                ins.then_inc(sem.h, inc)

        self.on(e, fn)
        return sem.n if sem is not None else None

    def dma(self, e, out, in_, sem=None, **kw):
        return self.call(e, "dma_start", out, in_, sem=sem, inc=16, **kw)

    def mm(self, out, lhsT, rhs, start=True, stop=True, sem=None, tp=None):
        kw = {}
        if tp is not None:
            kw["tile_position"] = tp
        return self.call("t", "matmul", out, lhsT, rhs, start=start, stop=stop,
                         sem=sem, **kw)

    def tpose(self, out, in_, iden, sem=None):
        return self.call("t", "transpose", out, in_, iden, sem=sem)

    def act(self, out, in_, func, bias=0.0, scale=1.0, accum=None, sem=None):
        kw = {}
        if accum is not None:
            kw["accum_out"] = accum
        return self.call("s", "activation", out, in_, func, bias=bias,
                         scale=scale, sem=sem, **kw)

    def tt(self, out, a, c, op, sem=None):
        return self.call("v", "tensor_tensor", out, a, c, op, sem=sem)

    def ts(self, out, a, s1, s2, op0, op1=ALU.bypass, sem=None):
        return self.call("v", "tensor_scalar", out, a, s1, s2, op0, op1,
                         sem=sem)

    def stt(self, out, in0, scalar, in1, op0, op1, sem=None):
        return self.call("v", "scalar_tensor_tensor", out, in0, scalar, in1,
                         op0, op1, sem=sem)

    def red(self, kind, out, in_, sem=None, **kw):
        return self.call("v", kind, out, in_, AX.X, sem=sem, **kw)

    def vcopy(self, out, in_, sem=None):
        return self.call("v", "tensor_copy", out, in_, sem=sem)


def build_program():
    nc = bass.Bass("TRN2", target_bir_lowering=False)

    def din(name, shape, dt=F32):
        return nc.dram_tensor(name, shape, dt, kind="ExternalInput").ap()

    x_own = din("x_own", [TOK, D])
    x_pref = din("x_pref", [S, D])
    offs_d = din("offs_h", [128, NSTEP])
    ropeq_d = din("ropeq", [128, 2 * TOK], BF16)
    ropek_d = din("ropek", [128, 2 * S], BF16)
    dt_d = din("dtiles", [128, 4 * CS], I16)
    idb_d = din("idb_h", [128, 128], BF16)
    idf_d = din("idf_h", [128, 128])
    prot_d = din("prot_h", [128, 128], BF16)
    oh16_d = din("oh16_h", [128, 16 * H], BF16)
    sel16_d = din("sel16_h", [16, 128 * (H // 2)], BF16)
    oh8_d = din("oh8_h", [8, 8 * 128], BF16)
    wqkv_d = din("Wqkv", [D, 3 * D])
    wo_d = din("Wo", [D, D])
    w1_d = din("W1", [D, 4 * D])
    w2_d = din("W2", [4 * D, D])
    wg_d = din("Wg", [D, 8])
    mw1_d = din("MW1", [8, D, 4 * D])
    mw2_d = din("MW2", [8, 4 * D, D])
    ws1_d = din("Ws1", [D, 4 * D])
    ws2_d = din("Ws2", [4 * D, D])
    bq_f_d = din("bq_f", [128, 8])
    bk_f_d = din("bk_f", [128, 8])
    b1_f_d = din("b1_f", [128, 32])
    bs1_f_d = din("bs1_f", [128, 32])
    b1m_f_d = din("b1m_f", [128, 8 * 32])
    g2_f_d = din("g2_f", [128, 8])
    b2l_f_d = din("b2l_f", [128, 8])
    g1_r_d = din("g1_r", [1, D])
    b1l_r_d = din("b1l_r", [1, D])
    bv_r_d = din("bv_r", [1, D])
    bo_r_d = din("bo_r", [1, D])
    b2_r_d = din("b2_r", [1, D])
    bs2_r_d = din("bs2_r", [1, D])
    b2m_r_d = din("b2m_r", [1, 8 * D])
    out_d = nc.dram_tensor("out", [TOK, D], F32, kind="ExternalOutput").ap()

    b = KB()
    from contextlib import ExitStack
    with ExitStack() as stack:
        en = stack.enter_context

        def sb(name, shape, dt):
            return en(nc.sbuf_tensor(name, shape, dt))

        idb = sb("idb", [128, 128], BF16)
        idf = sb("idf", [128, 128], F32)
        oh8 = sb("oh8", [8, 8 * 128], BF16)
        ones_r = sb("ones_r", [1, CS], BF16)
        ones2 = sb("ones2", [128, 128], BF16)
        biasf = sb("biasf", [128, 352], F32)
        BQ, BK, B1C, BS1C, B1MC, LN2C = 0, 8, 16, 48, 80, 336
        brow = sb("brow", [1, D], BF16)
        st1 = sb("st1", [1, 16], F32)
        stats = sb("stats", [128, 48], F32)
        tmp1 = sb("tmp1", [128, 1024], F32)
        xsta = sb("xsta", [128, 4 * D], F32)
        ntok = sb("ntok", [128, 4 * D], BF16)
        nf = sb("nf", [128, 8 * CS], BF16)
        kbuf = sb("kbuf", [128, 8 * CS], BF16)
        vbuf = sb("vbuf", [128, 4 * D], BF16)
        kpre = sb("kpre", [128, 8 * CS], BF16)
        acc = [sb(f"acc{g}", [128, 8 * CS], F32) for g in range(NG)]

        bstack = ExitStack()

        def sbB(name, shape, dt):
            return bstack.enter_context(nc.sbuf_tensor(name, shape, dt))

        prot = sbB("prot", [128, 128], BF16)
        lns = sbB("lns", [128, 640], F32)
        oh16 = sbB("oh16", [128, 16 * H], BF16)
        sel16 = sbB("sel16", [16, 128 * (H // 2)], BF16)
        dti = sbB("dti", [128, 4 * CS], I16)
        offs = sbB("offs", [128, NSTEP], F32)
        bvrow = sbB("bvrow", [1, D], BF16)
        g1row = sbB("g1row", [128, D], BF16)
        b1row = sbB("b1row", [128, D], BF16)
        ropeb = sbB("ropeb", [128, 2 * CS], BF16)
        wstr = sbB("wstr", [128, 4 * D], BF16)
        pt_a = sbB("pt_a", [128, 2 * CS], BF16)
        pt_b = sbB("pt_b", [128, 2 * CS], BF16)
        pt_c = sbB("pt_c", [128, 2 * CS], BF16)
        cmpb = sbB("cmpb", [128, 4 * CS], BF16)
        lsum = [sbB(f"lsum{g}", [16, CS], F32) for g in range(NG)]
        dstack = ExitStack()
        qg = [dstack.enter_context(
            nc.sbuf_tensor(f"qg{g}", [128, 8 * CS], BF16, side="right"))
            for g in range(NG)]
        ffn_bufs = {}

        def ffn_alloc(stage):
            if stage in ffn_bufs:
                return
            if stage == 1:
                ffn_bufs[1] = (sb("actb1", [128, 4 * 2048], BF16),
                               sb("wp1a", [128, 4 * 1024], BF16),
                               sb("wp2a", [128, 4 * 1024], BF16))
            else:
                ffn_bufs[2] = (sb("actb2", [128, 4 * 2048], BF16),
                               sb("wp1b", [128, 4 * 1024], BF16),
                               sb("wp2b", [128, 4 * 1024], BF16),
                               sb("cwT", [8, TOK], BF16),
                               sb("cwb", [128, TOK], BF16),
                               sb("wgb", [128, 64], BF16))

        pst = en(nc.psum_tensor("pst", [128, 2048], F32))
        pav = en(nc.psum_tensor("pav", [128, 1024], F32))
        ps_l = en(nc.psum_tensor("ps_l", [128, 512], F32))
        ps_x = en(nc.psum_tensor("ps_x", [128, 512], F32))

        sx = Sem(en(nc.semaphore("sx")))
        sg = Sem(en(nc.semaphore("sg")))
        sxs = [Sem(en(nc.semaphore(f"sxs{i}"))) for i in range(4)]
        sxr = Sem(en(nc.semaphore("sxr")))
        swa = Sem(en(nc.semaphore("swa")))
        swb = Sem(en(nc.semaphore("swb")))
        swc = Sem(en(nc.semaphore("swc")))
        swd = Sem(en(nc.semaphore("swd")))
        sw1 = Sem(en(nc.semaphore("sw1")))
        sw2 = Sem(en(nc.semaphore("sw2")))
        swr = Sem(en(nc.semaphore("swr")))
        sw = Sem(en(nc.semaphore("sw")))
        st = Sem(en(nc.semaphore("st")))
        sv = Sem(en(nc.semaphore("sv")))
        ss = Sem(en(nc.semaphore("ss")))
        b.auto = {"v": sv, "s": ss}
        so = Sem(en(nc.semaphore("so")))
        block = en(nc.Block())

        def wv4(m4):
            return pst[:, m4 * 512:(m4 + 1) * 512]

        for dst, src in ((idb[:], idb_d), (idf[:], idf_d), (prot[:], prot_d),
                         (oh16[:], oh16_d), (sel16[:], sel16_d),
                         (oh8[:], oh8_d), (dti[:], dt_d), (offs[:], offs_d),
                         (biasf[:, BQ:BQ + 8], bq_f_d),
                         (biasf[:, BK:BK + 8], bk_f_d),
                         (biasf[:, B1C:B1C + 32], b1_f_d),
                         (biasf[:, BS1C:BS1C + 32], bs1_f_d),
                         (biasf[:, B1MC:B1MC + 256], b1m_f_d),
                         (biasf[:, LN2C:LN2C + 8], g2_f_d),
                         (biasf[:, LN2C + 8:LN2C + 16], b2l_f_d)):
            b.dma("y", dst, src, sem=sx)
        b.dma("g", g1row[0:1, :], g1_r_d, sem=sw)
        b.dma("g", b1row[0:1, :], b1l_r_d, sem=sw)
        b.dma("g", bvrow[0:1, :], bv_r_d, sem=sw)
        b.call("g", "memset", ones_r[:], 1.0)
        b.call("g", "memset", ones2[:], 1.0, sem=sg)
        b.call("v", "memset", st1[:], 0.0)
        b.call("v", "memset", stats[:], 0.0, sem=sv)
        for g in range(NG):
            b.call("v", "memset", acc[g][:], 0.0)
            b.call("v", "memset", lsum[g][:], 0.0, sem=sv)
        b.wait("t", sw)
        b.wait("t", sg)
        b.wait("t", sx)
        for row in (g1row, b1row):
            for h4 in range(2):
                sl = slice(h4 * 512, (h4 + 1) * 512)
                b.wait("t", sv)
                b.mm(ps_x[:, 0:512], ones_r[0:1, 0:128], row[0:1, sl], sem=st)
                b.wait("v", st)
                b.vcopy(row[:, sl], ps_x[:, 0:512], sem=sv)
        b.wait("t", sv)

        lncons = [[0, 0, 0] for _ in range(4)]

        def xload(srcap, row0, slot):
            sxk = sxs[slot]
            b.wait("y", sv, lncons[slot][0])
            b.wait("y", ss, lncons[slot][1])
            b.wait("y", st, lncons[slot][2])
            return b.dma("y", xsta[:, slot * D:(slot + 1) * D],
                         srcap[row0:row0 + 128, :], sem=sxk)

        def stat_tile(srcap, pos, t):
            slot = t % 4
            xl = xload(srcap, pos, slot)
            xt = xsta[:, slot * D:(slot + 1) * D]
            b.wait("s", sxs[slot], xl)
            b.act(xt, xt, AF.Identity, accum=lns[:, t:t + 1])
            b.act(xt, xt, AF.Square, accum=lns[:, 80 + t:81 + t], sem=ss)
            lncons[slot][1] = ss.n

        lns_ready = [0]

        def ln_chain(t0, t1, own_xn=False):
            b.wait("v", ss)
            n = t1 - t0
            me = lns[:, 160 + t0:160 + t1]
            va = lns[:, 240 + t0:240 + t1]
            r_ = lns[:, 320 + t0:320 + t1]
            rt = lns[:, 400 + t0:400 + t1]
            bi = lns[:, 480 + t0:480 + t1]
            b.ts(me, lns[:, t0:t1], 1.0 / D, None, ALU.mult)
            b.ts(va, lns[:, 80 + t0:80 + t1], 1.0 / D, None, ALU.mult)
            b.tt(r_, me, me, ALU.mult)
            b.tt(va, va, r_, ALU.subtract)
            b.ts(r_, va, LN_EPS, None, ALU.add, sem=sv)
            b.wait("s", sv)
            b.act(r_, r_, AF.Ln, sem=ss)
            b.wait("s", ss)
            b.act(r_, r_, AF.Exp, scale=-0.5, sem=ss)
            b.wait("v", ss)
            b.ts(rt, va, LN_EPS, None, ALU.add)
            b.call("v", "reciprocal", rt, rt)
            b.tt(rt, va, rt, ALU.mult)
            b.ts(rt, rt, LN_EPS, None, ALU.add, sem=sv)
            b.wait("s", sv)
            b.act(rt, rt, AF.Ln, sem=ss)
            b.wait("s", ss)
            b.act(rt, rt, AF.Exp, scale=-0.5, sem=ss)
            b.wait("v", ss)
            b.tt(rt, r_, rt, ALU.mult)
            b.tt(bi, me, rt, ALU.mult)
            b.ts(bi, bi, -1.0, None, ALU.mult)
            if own_xn:
                bo_ = lns[:, 560 + t0 - 64:560 + t1 - 64]
                b.tt(bo_, me, r_, ALU.mult)
                b.ts(bo_, bo_, -1.0, None, ALU.mult)
            b.call("v", "engine_nop", sem=sv)
            lns_ready[0] = sv.n

        def ln_apply(srcap, pos, tbase, mode):
            xl = [0] * 4
            for tt_ in range(4):
                xl[tt_] = xload(srcap, pos + tt_ * 128, tbase % 4 * 0 + tt_)
            for tt_ in range(4):
                t = tbase + tt_
                xt = xsta[:, tt_ * D:(tt_ + 1) * D]
                b.wait("s", sxs[tt_], xl[tt_])
                b.wait("s", sv, lns_ready[0])
                if mode == "normed":
                    sc, bi = lns[:, 400 + t:401 + t], lns[:, 480 + t:481 + t]
                else:
                    sc, bi = lns[:, 320 + t:321 + t], lns[:, 560 + t - 64:
                                                         561 + t - 64]
                b.act(ntok[:, tt_ * D:(tt_ + 1) * D], xt, AF.Identity,
                      bias=bi, scale=sc, sem=ss)
                lncons[tt_][1] = ss.n
                lncons[tt_][2] = st.n
            ln_s = ss.n
            if mode == "normed":
                b.wait("v", ss)
                for tt_ in range(4):
                    nslot = ntok[:, tt_ * D:(tt_ + 1) * D]
                    b.tt(nslot, nslot, g1row[:], ALU.mult)
                    b.tt(nslot, nslot, b1row[:], ALU.add, sem=sv)
                    lncons[tt_][0] = sv.n
            return ln_s, sv.n

        tp_state = [0, 0]

        def tpose4(dst, ln_s, ln_v):
            b.wait("t", ss, ln_s)
            b.wait("t", sv, ln_v)
            for tt_ in range(4):
                b.wait("t", sv, tp_state[0])
                last = 0
                for f in range(8):
                    last = b.mm(pst[:, 1024 + f * 128:1024 + (f + 1) * 128],
                                ntok[:, tt_ * D + f * 128:
                                     tt_ * D + (f + 1) * 128],
                                idb[:], sem=st)
                tp_state[1] = last
                lncons[tt_][2] = last
                b.wait("v", st, last)
                src3 = pst[:, 1024:2048].rearrange("p (f c) -> p f c", c=128)
                dst3 = dst.rearrange("p (f c) -> p f c", c=CS)[
                    :, :, tt_ * 128:(tt_ + 1) * 128]
                tp_state[0] = b.call("v", "tensor_copy", dst3, src3, sem=sv)
            return tp_state[0]

        bhist = []

        def stream_proj(src2_fn, rhs_fn, psum_fn, nm, nk=8, bias_mm=None,
                        qwaits=None):
            hist = bhist
            w_ = nm * 128
            for kk in range(0, nk, 4):
                u = (kk // 4) % 2
                sl = u * 2 * D
                swk = (swa, swb)[u]
                if len(hist) >= 2:
                    b.wait("g", st, hist[-2])
                wl = b.dma("g",
                           wstr[:, sl:sl + 4 * w_].rearrange(
                               "p (b c) -> p b c", b=4),
                           src2_fn(kk).rearrange("(b p) c -> p b c", p=128),
                           sem=swk)
                if kk == 0 and qwaits is not None:
                    for ws_, wv_2 in qwaits:
                        b.wait("t", ss, ws_)
                        b.wait("t", sv, wv_2)
                b.wait("t", swk, wl)
                last = None
                for bb2 in range(4):
                    k = kk + bb2
                    for m in range(nm):
                        last = b.mm(psum_fn(m),
                                    wstr[:, sl + bb2 * w_ + m * 128:
                                         sl + bb2 * w_ + (m + 1) * 128],
                                    rhs_fn(k), start=(k == 0),
                                    stop=(k == nk - 1 and bias_mm is None),
                                    sem=st)
                hist.append(last)
            if bias_mm is not None:
                bias_mm()
            return hist[-1]

        slot_hist = [(0, 0)] * 4
        rope_hist = [(0, 0), (0, 0)]

        def qk_proj_rope(is_q, dst, nfr=0):
            col0 = 0 if is_q else D
            bcol = BQ if is_q else BK
            evd = proj_waves(col0, bcol, nfr, rope_dst=dst)
            for m in range(4, 8):
                rope_tile(m, evd, dst)

        slot_reader = [0, 0, 0, 0]

        def band_dma(src2, u):
            sl = u * 1024
            swk = (swa, swb, swc, swd)[u]
            b.wait("g", st, slot_reader[u])
            wl = b.dma("g",
                       wstr[:, sl:sl + 1024].rearrange(
                           "p (b c) -> p b c", b=2),
                       src2.rearrange("(b p) c -> p b c", p=128), sem=swk)
            return (u, sl, swk, wl)

        def band_mms(dinfo, kk0, qwaits=None, stop_k=7):
            u, sl, swk, wl = dinfo
            b.wait("t", swk, wl)
            last = None
            for bb2 in range(2):
                k = kk0 + bb2
                for m in range(4):
                    if k == 0 and qwaits is not None:
                        b.wait("t", ss, qwaits[m][0])
                        b.wait("t", sv, qwaits[m][1])
                    last = b.mm(wv4(m),
                                wstr[:, sl + bb2 * 512 + m * 128:
                                     sl + bb2 * 512 + (m + 1) * 128],
                                nf[:, k * CS:(k + 1) * CS],
                                start=(k == 0), stop=(k == stop_k),
                                sem=st)
            slot_reader[u] = last
            bhist.append(last)

        def proj_waves(col0, bcol, nfr, pre=None, rope_dst=None,
                       next_fns=None):
            evd = [0] * 8
            b.wait("s", sv)
            b.wait("t", sv, nfr)
            KK = (0, 2, 4, 6)

            def ksrc(wv, kk0):
                return wqkv_d[kk0 * 128:(kk0 + 2) * 128,
                              col0 + wv * 512:col0 + (wv + 1) * 512]

            cur = pre if pre is not None else \
                [band_dma(ksrc(0, kk0), kk0 // 2) for kk0 in KK]
            for wv in range(2):
                nxt = []
                if wv == 1 and rope_dst is not None:
                    for m in range(4):
                        rope_tile(m, evd, rope_dst)
                for ui, kk0 in enumerate(KK):
                    band_mms(cur[ui], kk0,
                             qwaits=(slot_hist if kk0 == 0 else None))
                    if wv == 0:
                        nxt.append(band_dma(ksrc(1, kk0), ui))
                    elif next_fns is not None:
                        next_fns[ui]()
                for m4 in range(4):
                    m = wv * 4 + m4
                    b.wait("s", st, bhist[-1])
                    evd[m] = b.act(kpre[:, m * CS:(m + 1) * CS], wv4(m4),
                                   AF.Identity,
                                   bias=biasf[:, bcol + m:bcol + m + 1],
                                   sem=ss)
                    slot_hist[m4] = (evd[m], sv.n)
                cur = nxt
            return evd

        def rope_tile(m, evd, dst):
            pc = pav[:, (m % 2) * 512:(m % 2) * 512 + 512]
            b.wait("t", ss, evd[m])
            b.wait("t", sv, max(rope_hist[m % 2][0], fl_acc[m % 2]))
            rot = b.mm(pc, prot[:], kpre[:, m * CS:(m + 1) * CS], sem=st)
            b.wait("v", st, rot)
            b.tt(tmp1[:, 0:512], kpre[:, m * CS:(m + 1) * CS],
                 ropeb[:, 0:CS], ALU.mult)
            b.tt(tmp1[:, 512:1024], pc, ropeb[:, CS:2 * CS], ALU.mult)
            rv = b.tt(dst[:, m * CS:(m + 1) * CS], tmp1[:, 0:512],
                      tmp1[:, 512:1024], ALU.add, sem=sv)
            rope_hist[m % 2] = (rv, rot)

        def norms_update(src, stcol):
            sqd = [0, 0]
            rdd = [0, 0]
            for m in range(8):
                sq = pt_a[:, (m % 2) * 512:(m % 2) * 512 + 512]
                b.wait("s", st, sqd[m % 2])
                sqe = b.act(sq, src[:, m * CS:(m + 1) * CS], AF.Square,
                            sem=ss)
                psn = (ps_l, ps_x)[m % 2]
                b.wait("t", ss, sqe)
                if m >= 2:
                    b.wait("t", sv, rdd[m % 2])
                sqd[m % 2] = b.mm(psn[0:1, 0:512], ones2[:, 0:1], sq, sem=st)
                b.wait("v", st, sqd[m % 2])
                b.red("reduce_max", st1[0:1, 8:9], psn[0:1, 0:512])
                rdd[m % 2] = b.tt(st1[0:1, stcol:stcol + 1],
                                  st1[0:1, stcol:stcol + 1],
                                  st1[0:1, 8:9], ALU.max, sem=sv)
            b.wait("t", sv)

        fl_exp = [0, 0]
        fl_avl = [0, 0, 0]
        fl_acc = [0, 0]
        fl_lev = [0]
        fl_stage = [0]

        for g in range(NG):
            for tt_ in range(4):
                stat_tile(x_own, (g * 4 + tt_) * 128, 64 + g * 4 + tt_)
        ln_chain(64, 80, own_xn=True)
        for g in range(NG):
            b.wait("y", sv)
            b.dma("y", ropeb[:, 0:CS], ropeq_d[0:128, g * CS:(g + 1) * CS],
                  sem=sxr)
            rope_ld = b.dma("y", ropeb[:, CS:2 * CS],
                            ropeq_d[0:128, TOK + g * CS: TOK + (g + 1) * CS],
                            sem=sxr)
            ln_s, ln_v = ln_apply(x_own, g * CS, 64 + g * 4, "normed")
            nfr = tpose4(nf, ln_s, ln_v)
            b.wait("v", sxr, rope_ld)
            qk_proj_rope(True, qg[g], nfr)
            norms_update(kpre, 0)
            for cc in range(4 * g, 4 * g + 4):
                for tt_ in range(4):
                    stat_tile(x_pref, cc * CS + tt_ * 128, cc * 4 + tt_)

        sched_i = 0
        ln_chain(0, 64)
        ln_pipe = ln_apply(x_pref, 0, 0, "normed")
        kw0_pre = [band_dma(wqkv_d[kk0 * 128:(kk0 + 2) * 128, D:D + 512],
                            kk0 // 2) for kk0 in (0, 2, 4, 6)]
        for c in range(NCHUNK):
            b.wait("y", sv)
            b.dma("y", ropeb[:, 0:CS], ropek_d[0:128, c * CS:(c + 1) * CS],
                  sem=sxr)
            rope_ld = b.dma("y", ropeb[:, CS:2 * CS],
                            ropek_d[0:128, S + c * CS: S + (c + 1) * CS],
                            sem=sxr)
            ln_s, ln_v = ln_pipe
            nfr = tpose4(nf, ln_s, ln_v)
            if c + 1 < NCHUNK:
                ln_pipe = ln_apply(x_pref, (c + 1) * CS, (c + 1) * 4,
                                   "normed")
            b.wait("v", sxr, rope_ld)
            def vsrc(vh, kk0):
                return wqkv_d[kk0 * 128:(kk0 + 2) * 128,
                              2 * D + vh * 512:2 * D + (vh + 1) * 512]

            vd = {}

            def mk_vd(vh, ui):
                def f():
                    vd[(vh, ui)] = band_dma(vsrc(vh, 2 * ui), ui)
                return f

            def v_mms(kk0, dinfo):
                u, sl, swk, wl = dinfo
                if kk0 == 0:
                    b.wait("t", sv, nfr)
                    for m4 in range(4):
                        b.wait("t", ss, slot_hist[m4][0])
                        b.wait("t", sv, slot_hist[m4][1])
                b.wait("t", swk, wl)
                last = None
                for bb2 in range(2):
                    k = kk0 + bb2
                    for tt_ in range(4):
                        last = b.mm(wv4(tt_),
                                    nf[:, k * CS + tt_ * 128:
                                       k * CS + (tt_ + 1) * 128],
                                    wstr[:, sl + bb2 * 512:
                                         sl + bb2 * 512 + 512],
                                    start=(k == 0), stop=False, sem=st)
                slot_reader[u] = last
                bhist.append(last)

            def v_fin(vh):
                for tt_ in range(4):
                    b.mm(wv4(tt_), ones_r[0:1, 0:128],
                         bvrow[0:1, vh * 512:(vh + 1) * 512],
                         start=False, stop=True, sem=st)
                vlast = st.n
                for tt_ in range(4):
                    b.wait("s", st, vlast)
                    ev = b.act(vbuf[:, tt_ * D + vh * 512:
                                    tt_ * D + (vh + 1) * 512],
                               wv4(tt_), AF.Identity, sem=ss)
                    slot_hist[tt_] = (ev, sv.n)

            evd = proj_waves(D, BK, nfr, pre=kw0_pre, rope_dst=kbuf,
                             next_fns=[mk_vd(0, ui) for ui in range(4)])
            rope_tile(4, evd, kbuf)
            v_mms(0, vd[(0, 0)])
            mk_vd(1, 0)()
            rope_tile(5, evd, kbuf)
            v_mms(2, vd[(0, 1)])
            mk_vd(1, 1)()
            rope_tile(6, evd, kbuf)
            v_mms(4, vd[(0, 2)])
            mk_vd(1, 2)()
            rope_tile(7, evd, kbuf)
            v_mms(6, vd[(0, 3)])
            mk_vd(1, 3)()
            v_fin(0)
            v_mms(0, vd[(1, 0)])
            v_mms(2, vd[(1, 1)])
            v_mms(4, vd[(1, 2)])
            v_mms(6, vd[(1, 3)])
            v_fin(1)
            norms_update(kpre, 1)
            if c + 1 < NCHUNK:
                kw0_pre = [band_dma(wqkv_d[kk0 * 128:(kk0 + 2) * 128,
                                           D:D + 512], kk0 // 2)
                           for kk0 in (0, 2, 4, 6)]
            b.wait("s", sv)
            b.act(st1[0:1, 2:3], st1[0:1, 0:1], AF.Ln, sem=ss)
            b.act(st1[0:1, 3:4], st1[0:1, 1:2], AF.Ln, sem=ss)
            b.wait("v", ss)
            b.tt(st1[0:1, 8:9], st1[0:1, 2:3], st1[0:1, 3:4], ALU.add)
            b.ts(st1[0:1, 8:9], st1[0:1, 8:9], 0.5, -math.log(8.0),
                 ALU.mult, ALU.add, sem=sv)
            b.wait("s", sv)
            b.act(st1[0:1, 5:6], st1[0:1, 8:9], AF.Exp, sem=ss)
            b.wait("v", ss)
            b.vcopy(pt_a[0:1, 0:1], st1[0:1, 5:6])
            b.vcopy(st1[0:1, 5:6], pt_a[0:1, 0:1])
            b.tt(st1[0:1, 8:9], st1[0:1, 4:5], st1[0:1, 5:6], ALU.subtract)
            b.ts(st1[0:1, 7:8], st1[0:1, 5:6], -1.0, None, ALU.mult, sem=sv)
            b.wait("s", sv)
            b.act(st1[0:1, 6:7], st1[0:1, 8:9], AF.Exp, sem=ss)
            b.wait("v", ss)
            b.vcopy(st1[0:1, 4:5], st1[0:1, 5:6])
            b.vcopy(pt_a[0:1, 0:2], st1[0:1, 6:8], sem=sv)
            b.wait("t", sv)
            b.wait("t", ss)
            b.mm(ps_x[:, 0:2], ones_r[0:1, 0:128], pt_a[0:1, 0:2], sem=st)
            b.wait("v", st)
            b.wait("v", ss)
            b.vcopy(stats[:, 1:2], ps_x[:, 0:1])
            stats_rdy = b.vcopy(stats[:, 0:1], ps_x[:, 1:2], sem=sv)
            kv_rdy_v = sv.n
            kv_rdy_s = ss.n

            first_of_chunk = True
            while sched_i < NSTEP and SCHED[sched_i][0] == c:
                _, g = SCHED[sched_i]
                step = sched_i
                sched_i += 1
                candidate = (4 * g <= c)
                cmp_done = 0
                if candidate:
                    for kt in range(4):
                        cmp_done = b.ts(cmpb[:, kt * CS:(kt + 1) * CS],
                                        dti[:, kt * CS:(kt + 1) * CS],
                                        offs[:, step:step + 1], None,
                                        ALU.is_le, sem=sv)
                first_l = True

                def emit_avl(p):
                    nonlocal first_l
                    pt3, rdy, rdy_sem, hp_, kt_, g_, last = p
                    ptb = (pt_a, pt_b, pt_c)[pt3]
                    b.wait("t", rdy_sem, rdy)
                    if first_l:
                        b.wait("t", sv, fl_lev[0])
                    b.mm(ps_l[0:16, 0:512],
                         oh16[:, 16 * (2 * hp_):16 * (2 * hp_) + 16],
                         ptb[:, 0:512], start=first_l, stop=False)
                    b.mm(ps_l[0:16, 0:512],
                         oh16[:, 16 * (2 * hp_ + 1):16 * (2 * hp_ + 1) + 16],
                         ptb[:, 512:1024], start=False, stop=last)
                    first_l = False
                    if kt_ == 0:
                        b.wait("t", sv, max(fl_acc[0], fl_acc[1],
                                            rope_hist[0][0],
                                            rope_hist[1][0]))
                    b.mm(pav[0:64, 0:512],
                         vbuf[:, kt_ * D + hp_ * 128:kt_ * D + hp_ * 128 + 64],
                         ptb[:, 0:512], tp=(0, 0),
                         start=(kt_ == 0), stop=(kt_ == 3))
                    av = b.mm(pav[64:128, 512:1024],
                              vbuf[:, kt_ * D + hp_ * 128 + 64:
                                   kt_ * D + hp_ * 128 + 128],
                              ptb[:, 512:1024], tp=(0, 64),
                              start=(kt_ == 0), stop=(kt_ == 3), sem=st)
                    fl_avl[pt3] = av
                    if kt_ == 3:
                        b.wait("v", st, av)
                        aslice = acc[g_][:, hp_ * CS:(hp_ + 1) * CS]
                        fl_acc[0] = b.stt(aslice[0:64, :], aslice[0:64, :],
                                          stats[0:64, 1:2], pav[0:64, 0:512],
                                          ALU.mult, ALU.add, sem=sv)
                        fl_acc[1] = b.stt(aslice[64:128, :],
                                          aslice[64:128, :],
                                          stats[64:128, 1:2],
                                          pav[64:128, 512:1024],
                                          ALU.mult, ALU.add, sem=sv)

                pending = []
                for hp in range(8):
                    for kt in range(4):
                        sl_ = fl_stage[0] % 2
                        pt3 = fl_stage[0] % 3
                        fl_stage[0] += 1
                        pslot = pst[:, sl_ * 1024:(sl_ + 1) * 1024]
                        ptb = (pt_a, pt_b, pt_c)[pt3]
                        kc = hp * CS + kt * 128
                        b.wait("t", ss, fl_exp[sl_])
                        if first_of_chunk:
                            b.wait("t", sv, kv_rdy_v)
                            b.wait("t", ss, kv_rdy_s)
                            first_of_chunk = False
                        b.mm(pslot[:, 0:512], kbuf[0:64, kc:kc + 128],
                             qg[g][0:64, hp * CS:(hp + 1) * CS], tp=(0, 0))
                        p2 = b.mm(pslot[:, 512:1024],
                                  kbuf[64:128, kc:kc + 128],
                                  qg[g][64:128, hp * CS:(hp + 1) * CS],
                                  tp=(64, 0), sem=st)
                        b.wait("s", st, p2)
                        b.wait("s", st, fl_avl[pt3])
                        ex = b.act(ptb[:, 0:1024], pslot[:, 0:1024], AF.Exp,
                                   bias=stats[:, 0:1], sem=ss)
                        fl_exp[sl_] = ex
                        if candidate:
                            b.wait("v", ss, ex)
                            b.tt(ptb[:, 0:512], ptb[:, 0:512],
                                 cmpb[:, kt * CS:(kt + 1) * CS], ALU.mult)
                            mk = b.tt(ptb[:, 512:1024], ptb[:, 512:1024],
                                      cmpb[:, kt * CS:(kt + 1) * CS],
                                      ALU.mult, sem=sv)
                            rdy, rdy_sem = mk, sv
                        else:
                            rdy, rdy_sem = ex, ss
                        if len(pending) >= 2:
                            emit_avl(pending.pop(0))
                        pending.append((pt3, rdy, rdy_sem, hp, kt, g,
                                        hp == 7 and kt == 3))
                for p_ in pending:
                    emit_avl(p_)
                pending = []
                b.wait("v", st)
                fl_lev[0] = b.stt(lsum[g][0:16, :], lsum[g][0:16, :],
                                  stats[0:16, 1:2], ps_l[0:16, 0:512],
                                  ALU.mult, ALU.add, sem=sv)
                if c == 4 * g + 3:
                    b.call("v", "reciprocal", lsum[g][0:16, :],
                           lsum[g][0:16, :])
                    b.vcopy(pt_a[0:16, 0:512], lsum[g][0:16, :], sem=sv)
                    b.wait("t", sv)
                    for hp in range(8):
                        pc = pav[:, (hp % 2) * 512:(hp % 2) * 512 + 512]
                        b.wait("t", sv)
                        b.mm(pc, sel16[:, hp * 128:(hp + 1) * 128],
                             pt_a[0:16, 0:512], sem=st)
                        b.wait("v", st)
                        b.tt(qg[g][:, hp * CS:(hp + 1) * CS],
                             acc[g][:, hp * CS:(hp + 1) * CS], pc, ALU.mult,
                             sem=sv)
                    b.wait("t", sv)
                    fl_exp[0] = ss.n
                    fl_exp[1] = ss.n
                    fl_avl[0] = st.n
                    fl_avl[1] = st.n
                    fl_avl[2] = st.n
                    fl_acc[0] = sv.n
                    fl_acc[1] = sv.n
                    rope_hist[0] = (sv.n, st.n)
                    rope_hist[1] = (sv.n, st.n)

        def racc(f, g):
            return acc[g][:, f * CS:(f + 1) * CS]

        xnf = [kbuf, vbuf, kpre, nf]

        def x1t(k):
            if k < 2:
                return pt_a[:, k * 512:(k + 1) * 512]
            if k < 4:
                return pt_b[:, (k - 2) * 512:(k - 2) * 512 + 512]
            return cmpb[:, (k - 4) * 512:(k - 4) * 512 + 512]

        b.wait("g", st)
        brow_ld = b.dma("g", brow[0:1, :], bo_r_d, sem=swr)
        for g in range(NG):
            for tt_ in range(4):
                sl = tt_ * D
                sxk = sxs[tt_]
                t_ = 64 + g * 4 + tt_
                xl = xload(x_own, g * CS + tt_ * 128, tt_)
                b.wait("s", sxs[tt_], xl)
                b.wait("s", sv, lns_ready[0])
                b.act(ntok[:, tt_ * D:(tt_ + 1) * D], xsta[:, sl:sl + D],
                      AF.Identity, bias=lns[:, 560 + t_ - 64:561 + t_ - 64],
                      scale=lns[:, 320 + t_:321 + t_], sem=ss)
                lncons[tt_][1] = ss.n
                b.wait("t", sxs[tt_], xl)
                b.wait("t", ss)
                b.wait("t", sv)
                for f in range(8):
                    b.mm(pst[:, f * 128:(f + 1) * 128],
                         xsta[:, sl + f * 128: sl + (f + 1) * 128], idf[:],
                         sem=None)
                for f in range(8):
                    b.mm(pst[:, 1024 + f * 128:1024 + (f + 1) * 128],
                         ntok[:, tt_ * D + f * 128: tt_ * D + (f + 1) * 128],
                         idb[:], sem=(st if f == 7 else None))
                b.wait("v", st)
                xn3 = xnf[g].rearrange("p (f c) -> p f c", c=CS)[
                    :, :, tt_ * 128:(tt_ + 1) * 128]
                b.vcopy(xn3, pst[:, 1024:2048].rearrange(
                    "p (f c) -> p f c", c=128))
                ac3 = acc[g].rearrange("p (f c) -> p f c", c=CS)[
                    :, :, tt_ * 128:(tt_ + 1) * 128]
                b.tt(ac3, pst[:, 0:1024].rearrange("p (f c) -> p f c", c=128),
                     xn3, ALU.add, sem=sv)
                lncons[tt_] = [sv.n, ss.n, st.n]
                b.wait("t", sv)
            b.wait("t", ss)
            b.wait("t", swr, brow_ld)
            for wv_ in range(2):
                def bias_mm(wv2=wv_):
                    for m4 in range(4):
                        m = wv2 * 4 + m4
                        b.mm(wv4(m4), brow[0:1, m * 128:(m + 1) * 128],
                             ones_r[0:1, 0:512], start=False, stop=True,
                             sem=st)
                b.wait("t", sv)
                stream_proj(lambda kk, w2=wv_: wo_d[kk * 128:(kk + 4) * 128,
                                                    w2 * 512:(w2 + 1) * 512],
                            lambda k: qg[g][:, k * CS:(k + 1) * CS],
                            wv4, nm=4, bias_mm=bias_mm)
                b.wait("v", st)
                for m4 in range(4):
                    m = wv_ * 4 + m4
                    b.tt(racc(m, g), racc(m, g), wv4(m4), ALU.add)
                    b.tt(x1t(m), xnf[g][:, m * CS:(m + 1) * CS], wv4(m4),
                         ALU.add, sem=sv)
            if DEV_STAGE >= 2:
                b.wait("t", sv)
                sqb = ntok
                for f in range(8):
                    b.mm(pst[:, 0:512], ones2[:], x1t(f), start=(f == 0),
                         stop=(f == 7), sem=st)
                sqmm = [0, 0]
                for f in range(8):
                    sq = sqb[:, (f % 2) * 512:(f % 2) * 512 + 512]
                    if f >= 2:
                        b.wait("v", st, sqmm[f % 2])
                    b.tt(sq, x1t(f), x1t(f), ALU.mult, sem=sv)
                    b.wait("t", sv)
                    sqmm[f % 2] = b.mm(pst[:, 512:1024], ones2[:], sq,
                                       start=(f == 0), stop=(f == 7), sem=st)
                b.wait("v", st)
                mb = tmp1[:, 0:512]
                rb = tmp1[:, 512:1024]
                b.ts(mb, pst[:, 0:512], 1.0 / D, None, ALU.mult)
                b.ts(rb, pst[:, 512:1024], 1.0 / D, None, ALU.mult)
                b.tt(xsta[:, 0:512], mb, mb, ALU.mult)
                b.tt(rb, rb, xsta[:, 0:512], ALU.subtract)
                b.ts(rb, rb, LN_EPS, None, ALU.add, sem=sv)
                b.wait("s", sv)
                b.act(rb, rb, AF.Ln, sem=ss)
                b.wait("s", ss)
                b.act(rb, rb, AF.Exp, scale=-0.5, sem=ss)
                b.wait("v", ss)
                for f in range(8):
                    qsl = qg[g][:, f * CS:(f + 1) * CS]
                    b.tt(qsl, x1t(f), mb, ALU.subtract)
                    b.tt(qsl, qsl, rb, ALU.mult)
                    b.ts(qsl, qsl,
                         biasf[:, LN2C + f:LN2C + f + 1],
                         biasf[:, LN2C + 8 + f:LN2C + 8 + f + 1],
                         ALU.mult, ALU.add, sem=sv)
                for tt_ in range(4):
                    lncons[tt_] = [sv.n, ss.n, st.n]
                b.wait("t", sv)

        ffn_ld = [0, 0]

        def ffn_phase(rhs_fn, w1_src, w2_src, b1col, actfn, b2rhs_fn,
                      evac, Fp, prescale=False):
            nf_ = Fp // 128
            npiece = 4096 // Fp

            def w1sl(k):
                if Fp == 512:
                    return ffn_bufs[1][1][:, k * Fp:(k + 1) * Fp]
                buf = ffn_bufs[1][1] if k < 4 else ffn_bufs[2][1]
                return buf[:, (k % 4) * Fp:(k % 4 + 1) * Fp]

            def w2sl(kt):
                if nf_ <= 4:
                    return ffn_bufs[1][2][:, kt * 1024:(kt + 1) * 1024]
                buf = ffn_bufs[1][2] if kt < 4 else ffn_bufs[2][2]
                return buf[:, (kt % 4) * 1024:(kt % 4 + 1) * 1024]

            def asl(f, g4):
                buf = ffn_bufs[1][0] if f < 4 else ffn_bufs[2][0]
                return buf[:, (f % 4) * TOK + g4 * CS:
                           (f % 4) * TOK + (g4 + 1) * CS]

            cwbuf = ffn_bufs[2][4] if 2 in ffn_bufs else None
            for p in range(npiece):
                b.wait("g", st, ffn_ld[0])
                for k in range(8):
                    b.dma("g", w1sl(k), w1_src(k, p), sem=sw1)
                w1_ld = sw1.n
                b.wait("g", st, ffn_ld[1])
                for kt in range(nf_):
                    b.dma("g", w2sl(kt), w2_src(kt, p), sem=sw2)
                w2_ld = sw2.n
                b.wait("t", sw1, w1_ld)
                for f in range(nf_):
                    for g4 in range(4):
                        sl = (f * 4 + g4) % 4
                        b.wait("t", ss, slot_hist[sl][0])
                        b.wait("t", sv, slot_hist[sl][1])
                        for k in range(8):
                            b.mm(wv4(sl),
                                 w1sl(k)[:, f * 128:(f + 1) * 128],
                                 rhs_fn(g4, k), start=(k == 0), stop=(k == 7),
                                 sem=(st if k == 7 else None))
                        b.wait("s", st)
                        if f == nf_ - 1 and g4 == 3:
                            ffn_ld[0] = st.n
                        aslice = asl(f, g4)
                        sev = b.act(aslice, wv4(sl), actfn,
                                    bias=b1col(p, f), sem=ss)
                        vev = sv.n
                        if prescale:
                            b.wait("v", ss)
                            vev = b.tt(aslice, aslice,
                                       cwbuf[:, g4 * CS:(g4 + 1) * CS],
                                       ALU.mult, sem=sv)
                        slot_hist[sl] = (sev, vev)
                b.wait("t", sw2, w2_ld)
                b.wait("t", ss)
                if prescale:
                    b.wait("t", sv)
                for m in range(8):
                    for g4 in range(4):
                        sl = (m * 4 + g4) % 4
                        b.wait("t", ss, slot_hist[sl][0])
                        b.wait("t", sv, slot_hist[sl][1])
                        last_bias = (p == 0 and b2rhs_fn)
                        for kt in range(nf_):
                            b.mm(wv4(sl),
                                 w2sl(kt)[:, m * 128:(m + 1) * 128],
                                 asl(kt, g4),
                                 start=(kt == 0),
                                 stop=(kt == nf_ - 1 and not last_bias),
                                 sem=(None if (kt < nf_ - 1 or last_bias)
                                      else st))
                        if last_bias:
                            b.mm(wv4(sl), brow[0:1, m * 128:(m + 1) * 128],
                                 b2rhs_fn(g4), start=False, stop=True,
                                 sem=st)
                        b.wait("v", st)
                        if m == 7 and g4 == 3:
                            ffn_ld[1] = st.n
                        vev = evac(m, g4, wv4(sl))
                        slot_hist[sl] = (ss.n, vev)

        bstack.close()

        if DEV_STAGE >= 2:
            ffn_alloc(1)
            b.wait("g", st)
            b.wait("g", sv)
            rl = b.dma("g", brow[0:1, :], b2_r_d, sem=swr)
            b.wait("t", swr, rl)

            def mlp_evac(m, g4, ps):
                return b.tt(racc(m, g4), racc(m, g4), ps, ALU.add, sem=sv)

            ffn_phase(lambda g4, k: qg[g4][:, k * CS:(k + 1) * CS],
                      lambda k, p: w1_d[k * 128:(k + 1) * 128,
                                        p * 512:(p + 1) * 512],
                      lambda kt, p: w2_d[p * 512 + kt * 128:
                                         p * 512 + (kt + 1) * 128, 0:1024],
                      lambda p, f: biasf[:, B1C + p * 4 + f:
                                         B1C + p * 4 + f + 1],
                      AF.Gelu,
                      lambda g4: ones_r[0:1, 0:512],
                      mlp_evac, Fp=512)

        dstack.close()

        if DEV_STAGE >= 3:
            ffn_alloc(1)
            ffn_alloc(2)
            cwT = ffn_bufs[2][3]
            cwb = ffn_bufs[2][4]
            wgb = ffn_bufs[2][5]
            b.wait("g", st)
            for k in range(8):
                rl = b.dma("g", wgb[:, k * 8:(k + 1) * 8],
                           wg_d[k * 128:(k + 1) * 128, :], sem=swr)
            b.wait("t", swr, rl)
            b.wait("t", sv)
            b.wait("t", ss)
            for t16 in range(16):
                g4, tt_ = divmod(t16, 4)
                for k in range(8):
                    b.mm(ps_l[:, t16 * 8:(t16 + 1) * 8],
                         xnf[g4][:, k * CS + tt_ * 128:
                                 k * CS + (tt_ + 1) * 128],
                         wgb[:, k * 8:(k + 1) * 8], start=(k == 0),
                         stop=(k == 7), sem=(st if k == 7 else None))
            b.wait("s", st)
            b.act(tmp1[:, 0:128], ps_l[:, 0:128], AF.Exp, sem=ss)
            b.wait("v", ss)
            e8a = tmp1[:, 0:128].rearrange("p (g e) -> p g e", e=8)
            scr3 = tmp1[:, 128:256].rearrange("p (g e) -> p g e", e=8)
            sc23 = tmp1[:, 320:448].rearrange("p (g e) -> p g e", e=8)
            sm = tmp1[:, 256:272]
            m1 = tmp1[:, 272:288]
            m2 = tmp1[:, 288:304]
            dn = tmp1[:, 304:320]

            def col3(ap16):
                return ap16.rearrange("p (g one) -> p g one", one=1)

            def bc8(ap16):
                return col3(ap16).broadcast_to([128, 16, 8])

            b.red("reduce_sum", col3(sm), e8a)
            b.red("reduce_max", col3(m1), e8a)
            b.tt(scr3, e8a, bc8(m1), ALU.is_ge)
            b.stt(sc23, scr3, -1e30, e8a, ALU.mult, ALU.add)
            b.red("reduce_max", col3(m2), sc23)
            b.tt(scr3, e8a, bc8(m2), ALU.is_ge)
            b.tt(dn, m1, m2, ALU.add)
            b.stt(dn, sm, 1e-9, dn, ALU.mult, ALU.add)
            b.call("v", "reciprocal", dn, dn)
            b.tt(scr3, e8a, scr3, ALU.mult)
            b.tt(ntok[:, 0:128].rearrange("p (g e) -> p g e", e=8),
                 scr3, bc8(dn), ALU.mult, sem=sv)
            for t16 in range(16):
                b.wait("t", sv)
                b.mm(ps_x[0:8, (t16 % 2) * 128:(t16 % 2) * 128 + 128],
                     ntok[:, t16 * 8:(t16 + 1) * 8], idb[:], sem=st)
                b.wait("v", st)
                b.vcopy(cwT[0:8, t16 * 128:(t16 + 1) * 128],
                        ps_x[0:8, (t16 % 2) * 128:(t16 % 2) * 128 + 128],
                        sem=sv)
            for e in range(8):
                b.wait("t", sv)
                b.wait("t", ss)
                for g4 in range(4):
                    b.wait("t", ss)
                    b.mm(pav[:, (g4 % 2) * 512:(g4 % 2) * 512 + 512],
                         oh8[:, e * 128:(e + 1) * 128],
                         cwT[0:8, g4 * 512:(g4 + 1) * 512], sem=st)
                    b.wait("s", st)
                    b.wait("s", sv)
                    b.act(cwb[:, g4 * CS:(g4 + 1) * CS],
                          pav[:, (g4 % 2) * 512:(g4 % 2) * 512 + 512],
                          AF.Identity, sem=ss)
                b.wait("g", ss)
                b.wait("g", st)
                rl = b.dma("g", brow[0:1, :],
                           b2m_r_d[0:1, e * D:(e + 1) * D], sem=swr)
                b.wait("t", swr, rl)
                b.wait("t", ss)

                def moe_evac(m, g4, ps):
                    return b.tt(racc(m, g4), racc(m, g4), ps, ALU.add,
                                sem=sv)

                ffn_phase(lambda g4, k: xnf[g4][:, k * CS:(k + 1) * CS],
                          lambda k, p, e2=e: mw1_d[e2][k * 128:(k + 1) * 128,
                                                       p * 1024:(p + 1) * 1024],
                          lambda kt, p, e2=e: mw2_d[e2][p * 1024 + kt * 128:
                                                        p * 1024 + (kt + 1) * 128,
                                                        0:1024],
                          lambda p, f, e2=e: biasf[:, B1MC + e2 * 32 + p * 8 + f:
                                                   B1MC + e2 * 32 + p * 8 + f + 1],
                          AF.Silu,
                          lambda g4: cwb[0:1, g4 * CS:(g4 + 1) * CS],
                          moe_evac, Fp=1024, prescale=True)

        if DEV_STAGE >= 4:
            b.wait("g", st)
            b.wait("g", sv)
            rl = b.dma("g", brow[0:1, :], bs2_r_d, sem=swr)
            b.wait("t", swr, rl)

            def sh_evac(m, g4, ps):
                return b.stt(racc(m, g4), ps, 0.1, racc(m, g4), ALU.mult,
                             ALU.add, sem=sv)

            ffn_phase(lambda g4, k: xnf[g4][:, k * CS:(k + 1) * CS],
                      lambda k, p: ws1_d[k * 128:(k + 1) * 128,
                                         p * 1024:(p + 1) * 1024],
                      lambda kt, p: ws2_d[p * 1024 + kt * 128:
                                          p * 1024 + (kt + 1) * 128, 0:1024],
                      lambda p, f: biasf[:, BS1C + p * 8 + f:
                                         BS1C + p * 8 + f + 1],
                      AF.Silu,
                      lambda g4: ones_r[0:1, 0:512],
                      sh_evac, Fp=1024)

        for g in range(NG):
            for tt_ in range(4):
                b.wait("t", sv)
                b.wait("t", ss)
                for f in range(8):
                    b.mm(pst[:, f * 128:(f + 1) * 128],
                         racc(f, g)[:, tt_ * 128:(tt_ + 1) * 128], idf[:],
                         sem=(st if f == 7 else None))
                b.wait("v", st)
                b.wait("v", so)
                b.vcopy(tmp1[:, 0:1024], pst[:, 0:1024], sem=sv)
                b.wait("y", sv)
                row0 = g * CS + tt_ * 128
                b.dma("y", out_d[row0:row0 + 128, :], tmp1[:, 0:1024], sem=so)
        b.wait("y", so)

        @block.tensor
        def _(E):
            for fn in b.q["t"]:
                fn(E)

        @block.vector
        def _(E):
            for fn in b.q["v"]:
                fn(E)

        @block.scalar
        def _(E):
            for fn in b.q["s"]:
                fn(E)

        @block.gpsimd
        def _(E):
            for fn in b.q["g"]:
                fn(E)

        @block.sync
        def _(E):
            for fn in b.q["y"]:
                fn(E)

    return nc


def _host_constants():
    inv = 1.0 / (10000.0 ** (np.arange(0, HD, 2, dtype=np.float64) / HD))
    t = np.arange(S, dtype=np.float64)
    fr = t[:, None] * inv[None, :]
    emb = np.concatenate([fr, fr], axis=1)
    cos = np.cos(emb).astype(np.float32)
    sin = np.sin(emb).astype(np.float32)
    cosf = np.concatenate([cos.T, cos.T], axis=0)
    sinf = np.concatenate([sin.T, sin.T], axis=0)
    iden = np.eye(128, dtype=np.float32)
    prot = np.zeros((128, 128), dtype=np.float32)
    for blk in range(2):
        o = blk * 64
        for m in range(32):
            prot[o + m + 32, o + m] = -1.0
            prot[o + m, o + m + 32] = 1.0
    oh16 = np.zeros((128, 16 * H), dtype=np.float32)
    for h in range(H):
        oh16[:, 16 * h + h] = 1.0
    sel16 = np.zeros((16, 128 * (H // 2)), dtype=np.float32)
    for hp in range(H // 2):
        sel16[2 * hp, hp * 128: hp * 128 + 64] = 1.0
        sel16[2 * hp + 1, hp * 128 + 64: hp * 128 + 128] = 1.0
    oh8 = np.zeros((8, 8 * 128), dtype=np.float32)
    for e in range(8):
        oh8[e, e * 128:(e + 1) * 128] = 1.0
    dt4 = np.zeros((128, 4 * CS), dtype=np.int16)
    for kt in range(4):
        p = np.arange(128)[:, None]
        q = np.arange(CS)[None, :]
        dt4[:, kt * CS:(kt + 1) * CS] = (128 * kt + p - q).astype(np.int16)
    return cosf, sinf, iden, prot, oh16, sel16, oh8, dt4


_PROG_CACHE = {}


def kernel(**inp):
    cosf, sinf, iden, prot, oh16, sel16, oh8, dt4 = _host_constants()
    x = np.asarray(inp["L0_attn_x"], dtype=np.float32)
    wqkv = np.ascontiguousarray(inp["L0_attn_Wqkv"], dtype=np.float32)
    bqkv = np.asarray(inp["L0_attn_bqkv"], dtype=np.float32)

    def bf(a):
        return np.ascontiguousarray(a).astype(BF)

    common = {
        "offs_h": None,
        "ropek": bf(np.concatenate([cosf, sinf], axis=1)),
        "dtiles": dt4,
        "idb_h": bf(iden), "idf_h": iden, "prot_h": bf(prot),
        "oh16_h": bf(oh16), "sel16_h": bf(sel16), "oh8_h": bf(oh8),
        "Wqkv": wqkv,
        "Wo": np.ascontiguousarray(inp["L0_attn_Wo"], dtype=np.float32),
        "W1": np.ascontiguousarray(inp["L0_attn_W1"], dtype=np.float32),
        "W2": np.ascontiguousarray(inp["L0_attn_W2"], dtype=np.float32),
        "Wg": np.ascontiguousarray(inp["L0_ffn_Wg"], dtype=np.float32),
        "MW1": np.ascontiguousarray(inp["L0_ffn_W1"], dtype=np.float32),
        "MW2": np.ascontiguousarray(inp["L0_ffn_W2"], dtype=np.float32),
        "Ws1": np.ascontiguousarray(inp["L0_ffn_Ws1"], dtype=np.float32),
        "Ws2": np.ascontiguousarray(inp["L0_ffn_Ws2"], dtype=np.float32),
        "bq_f": np.ascontiguousarray(bqkv[0:D].reshape(8, 128).T),
        "bk_f": np.ascontiguousarray(bqkv[D:2 * D].reshape(8, 128).T),
        "b1_f": np.ascontiguousarray(
            np.asarray(inp["L0_attn_b1"], np.float32).reshape(32, 128).T),
        "bs1_f": np.ascontiguousarray(
            np.asarray(inp["L0_ffn_bs1"], np.float32).reshape(32, 128).T),
        "b1m_f": np.ascontiguousarray(
            np.asarray(inp["L0_ffn_b1"], np.float32).reshape(8, 32, 128)
            .transpose(2, 0, 1).reshape(128, 256)),
        "g2_f": np.ascontiguousarray(
            np.asarray(inp["L0_attn_ln2_g"], np.float32).reshape(8, 128).T),
        "b2l_f": np.ascontiguousarray(
            np.asarray(inp["L0_attn_ln2_b"], np.float32).reshape(8, 128).T),
        "g1_r": np.asarray(inp["L0_attn_ln1_g"], np.float32)[None, :],
        "b1l_r": np.asarray(inp["L0_attn_ln1_b"], np.float32)[None, :],
        "bv_r": bqkv[2 * D:3 * D][None, :],
        "bo_r": np.asarray(inp["L0_attn_bo"], np.float32)[None, :],
        "b2_r": np.asarray(inp["L0_attn_b2"], np.float32)[None, :],
        "bs2_r": np.asarray(inp["L0_ffn_bs2"], np.float32)[None, :],
        "b2m_r": np.ascontiguousarray(
            np.asarray(inp["L0_ffn_b2"], np.float32).reshape(1, 8 * D)),
    }

    in_maps = []
    for core in range(8):
        bb, j = divmod(core, 4)
        chunks = [4 * g + j for g in range(NG)]
        m = dict(common)
        m["x_pref"] = x[bb]
        m["x_own"] = np.ascontiguousarray(
            np.concatenate([x[bb, ch * CS:(ch + 1) * CS] for ch in chunks]))
        offs = np.zeros((128, NSTEP), dtype=np.float32)
        for si, (c, g) in enumerate(SCHED):
            offs[:, si] = (4 * g + j - c) * CS
        m["offs_h"] = offs
        cols = np.concatenate(
            [np.arange(ch * CS, (ch + 1) * CS) for ch in chunks])
        m["ropeq"] = bf(np.concatenate(
            [cosf[:, cols] / 8.0, sinf[:, cols] / 8.0], axis=1))
        in_maps.append(m)

    key = DEV_STAGE
    if key not in _PROG_CACHE:
        _PROG_CACHE[key] = build_program()
    nc = _PROG_CACHE[key]
    res = run_bass_kernel_spmd(nc, in_maps, core_ids=list(range(8)))
    out = np.zeros((B, S, D), dtype=np.float32)
    for core in range(8):
        bb, j = divmod(core, 4)
        r = res.results[core]["out"]
        for g in range(NG):
            ch = 4 * g + j
            out[bb, ch * CS:(ch + 1) * CS] = r[g * CS:(g + 1) * CS]
    return out


# version 24
# speedup vs baseline: 1.0837x; 1.0293x over previous
import math
import numpy as np
import ml_dtypes
import concourse.bass as bass
import concourse.mybir as mybir
from concourse.bass_utils import run_bass_kernel_spmd

F32 = mybir.dt.float32
BF16 = mybir.dt.bfloat16
I16 = mybir.dt.int16
AF = mybir.ActivationFunctionType
ALU = mybir.AluOpType
AX = mybir.AxisListType
BF = ml_dtypes.bfloat16

B, S, D = 2, 8192, 1024
H, HD = 16, 64
NCHUNK = 16
CS = 512
NG = 4
TOK = NG * CS
LN_EPS = 1e-5
MASKVAL = -30000.0
DEV_STAGE = 4

SCHED = [(c, g) for c in range(NCHUNK) for g in range(NG) if 4 * g + 3 >= c]
NSTEP = len(SCHED)


class Sem:
    __slots__ = ("h", "n")

    def __init__(self, h):
        self.h = h
        self.n = 0


class KB:
    def __init__(self):
        self.q = {k: [] for k in "tvsgy"}
        self.auto = {}

    def on(self, e, fn):
        self.q[e].append(fn)

    def wait(self, e, sem, val=None):
        v = sem.n if val is None else val
        if v > 0:
            self.on(e, lambda E, h=sem.h, vv=v: E.wait_ge(h, vv))

    def call(self, e, meth, *args, sem=None, inc=1, **kw):
        if e in self.auto and meth != "dma_start":
            own = self.auto[e]
            if own.n > 0:
                self.on(e, lambda E, h=own.h, vv=own.n: E.wait_ge(h, vv))
            assert sem is None or sem is own
            sem = own
        if sem is not None:
            sem.n += inc

        def fn(E, meth=meth, args=args, kw=kw, sem=sem, inc=inc):
            ins = getattr(E, meth)(*args, **kw)
            if sem is not None:
                ins.then_inc(sem.h, inc)

        self.on(e, fn)
        return sem.n if sem is not None else None

    def dma(self, e, out, in_, sem=None, **kw):
        return self.call(e, "dma_start", out, in_, sem=sem, inc=16, **kw)

    def mm(self, out, lhsT, rhs, start=True, stop=True, sem=None, tp=None):
        kw = {}
        if tp is not None:
            kw["tile_position"] = tp
        return self.call("t", "matmul", out, lhsT, rhs, start=start, stop=stop,
                         sem=sem, **kw)

    def tpose(self, out, in_, iden, sem=None):
        return self.call("t", "transpose", out, in_, iden, sem=sem)

    def act(self, out, in_, func, bias=0.0, scale=1.0, accum=None, sem=None):
        kw = {}
        if accum is not None:
            kw["accum_out"] = accum
        return self.call("s", "activation", out, in_, func, bias=bias,
                         scale=scale, sem=sem, **kw)

    def tt(self, out, a, c, op, sem=None):
        return self.call("v", "tensor_tensor", out, a, c, op, sem=sem)

    def ts(self, out, a, s1, s2, op0, op1=ALU.bypass, sem=None):
        return self.call("v", "tensor_scalar", out, a, s1, s2, op0, op1,
                         sem=sem)

    def stt(self, out, in0, scalar, in1, op0, op1, sem=None):
        return self.call("v", "scalar_tensor_tensor", out, in0, scalar, in1,
                         op0, op1, sem=sem)

    def red(self, kind, out, in_, sem=None, **kw):
        return self.call("v", kind, out, in_, AX.X, sem=sem, **kw)

    def vcopy(self, out, in_, sem=None):
        return self.call("v", "tensor_copy", out, in_, sem=sem)


def build_program():
    nc = bass.Bass("TRN2", target_bir_lowering=False)

    def din(name, shape, dt=F32):
        return nc.dram_tensor(name, shape, dt, kind="ExternalInput").ap()

    x_own = din("x_own", [TOK, D])
    x_pref = din("x_pref", [S, D])
    offs_d = din("offs_h", [128, NSTEP])
    ropeq_d = din("ropeq", [128, 2 * TOK], BF16)
    ropek_d = din("ropek", [128, 2 * S], BF16)
    dt_d = din("dtiles", [128, 4 * CS], I16)
    idb_d = din("idb_h", [128, 128], BF16)
    idf_d = din("idf_h", [128, 128])
    prot_d = din("prot_h", [128, 128], BF16)
    oh16_d = din("oh16_h", [128, 16 * H], BF16)
    sel16_d = din("sel16_h", [16, 128 * (H // 2)], BF16)
    oh8_d = din("oh8_h", [8, 8 * 128], BF16)
    wqkv_d = din("Wqkv", [D, 3 * D])
    wo_d = din("Wo", [D, D])
    w1_d = din("W1", [D, 4 * D])
    w2_d = din("W2", [4 * D, D])
    wg_d = din("Wg", [D, 8])
    mw1_d = din("MW1", [8, D, 4 * D])
    mw2_d = din("MW2", [8, 4 * D, D])
    ws1_d = din("Ws1", [D, 4 * D])
    ws2_d = din("Ws2", [4 * D, D])
    bq_f_d = din("bq_f", [128, 8])
    bk_f_d = din("bk_f", [128, 8])
    b1_f_d = din("b1_f", [128, 32])
    bs1_f_d = din("bs1_f", [128, 32])
    b1m_f_d = din("b1m_f", [128, 8 * 32])
    g2_f_d = din("g2_f", [128, 8])
    b2l_f_d = din("b2l_f", [128, 8])
    g1_r_d = din("g1_r", [1, D])
    b1l_r_d = din("b1l_r", [1, D])
    bv_r_d = din("bv_r", [1, D])
    bo_r_d = din("bo_r", [1, D])
    b2_r_d = din("b2_r", [1, D])
    bs2_r_d = din("bs2_r", [1, D])
    b2m_r_d = din("b2m_r", [1, 8 * D])
    out_d = nc.dram_tensor("out", [TOK, D], F32, kind="ExternalOutput").ap()

    b = KB()
    from contextlib import ExitStack
    with ExitStack() as stack:
        en = stack.enter_context

        def sb(name, shape, dt):
            return en(nc.sbuf_tensor(name, shape, dt))

        idb = sb("idb", [128, 128], BF16)
        idf = sb("idf", [128, 128], F32)
        oh8 = sb("oh8", [8, 8 * 128], BF16)
        ones_r = sb("ones_r", [1, CS], BF16)
        ones2 = sb("ones2", [128, 128], BF16)
        biasf = sb("biasf", [128, 352], F32)
        BQ, BK, B1C, BS1C, B1MC, LN2C = 0, 8, 16, 48, 80, 336
        brow = sb("brow", [1, D], BF16)
        st1 = sb("st1", [1, 16], F32)
        stats = sb("stats", [128, 48], F32)
        tmp1 = sb("tmp1", [128, 1024], F32)
        xsta = sb("xsta", [128, 4 * D], F32)
        ntok = sb("ntok", [128, 4 * D], BF16)
        nf = sb("nf", [128, 8 * CS], BF16)
        kbuf = sb("kbuf", [128, 8 * CS], BF16)
        vbuf = sb("vbuf", [128, 4 * D], BF16)
        kpre = sb("kpre", [128, 8 * CS], BF16)
        acc = [sb(f"acc{g}", [128, 8 * CS], F32) for g in range(NG)]

        bstack = ExitStack()

        def sbB(name, shape, dt):
            return bstack.enter_context(nc.sbuf_tensor(name, shape, dt))

        prot = sbB("prot", [128, 128], BF16)
        lns = sbB("lns", [128, 640], F32)
        oh16 = sbB("oh16", [128, 16 * H], BF16)
        sel16 = sbB("sel16", [16, 128 * (H // 2)], BF16)
        dti = sbB("dti", [128, 4 * CS], I16)
        offs = sbB("offs", [128, NSTEP], F32)
        bvrow = sbB("bvrow", [1, D], BF16)
        g1row = sbB("g1row", [128, D], BF16)
        b1row = sbB("b1row", [128, D], BF16)
        ropeb = sbB("ropeb", [128, 2 * CS], BF16)
        wstr = sbB("wstr", [128, 4 * D], BF16)
        pt_a = sbB("pt_a", [128, 2 * CS], BF16)
        pt_b = sbB("pt_b", [128, 2 * CS], BF16)
        pt_c = sbB("pt_c", [128, 2 * CS], BF16)
        cmpb = sbB("cmpb", [128, 4 * CS], BF16)
        lsum = [sbB(f"lsum{g}", [16, CS], F32) for g in range(NG)]
        dstack = ExitStack()
        qg = [dstack.enter_context(
            nc.sbuf_tensor(f"qg{g}", [128, 8 * CS], BF16, side="right"))
            for g in range(NG)]
        ffn_bufs = {}

        def ffn_alloc(stage):
            if stage in ffn_bufs:
                return
            if stage == 1:
                ffn_bufs[1] = (sb("actb1", [128, 4 * 2048], BF16),
                               sb("wp1a", [128, 4 * 1024], BF16),
                               sb("wp2a", [128, 4 * 1024], BF16))
            else:
                ffn_bufs[2] = (sb("actb2", [128, 4 * 2048], BF16),
                               sb("wp1b", [128, 4 * 1024], BF16),
                               sb("wp2b", [128, 4 * 1024], BF16),
                               sb("cwT", [8, TOK], BF16),
                               sb("cwb", [128, TOK], BF16),
                               sb("wgb", [128, 64], BF16))

        pst = en(nc.psum_tensor("pst", [128, 2048], F32))
        pav = en(nc.psum_tensor("pav", [128, 1024], F32))
        ps_l = en(nc.psum_tensor("ps_l", [128, 512], F32))
        ps_x = en(nc.psum_tensor("ps_x", [128, 512], F32))

        sx = Sem(en(nc.semaphore("sx")))
        sg = Sem(en(nc.semaphore("sg")))
        sxs = [Sem(en(nc.semaphore(f"sxs{i}"))) for i in range(4)]
        sxr = Sem(en(nc.semaphore("sxr")))
        swa = Sem(en(nc.semaphore("swa")))
        swb = Sem(en(nc.semaphore("swb")))
        swc = Sem(en(nc.semaphore("swc")))
        swd = Sem(en(nc.semaphore("swd")))
        sw1 = Sem(en(nc.semaphore("sw1")))
        sw2 = Sem(en(nc.semaphore("sw2")))
        swr = Sem(en(nc.semaphore("swr")))
        sw = Sem(en(nc.semaphore("sw")))
        st = Sem(en(nc.semaphore("st")))
        sv = Sem(en(nc.semaphore("sv")))
        ss = Sem(en(nc.semaphore("ss")))
        b.auto = {"v": sv, "s": ss}
        so = Sem(en(nc.semaphore("so")))
        block = en(nc.Block())

        def wv4(m4):
            return pst[:, m4 * 512:(m4 + 1) * 512]

        for dst, src in ((idb[:], idb_d), (idf[:], idf_d), (prot[:], prot_d),
                         (oh16[:], oh16_d), (sel16[:], sel16_d),
                         (oh8[:], oh8_d), (dti[:], dt_d), (offs[:], offs_d),
                         (biasf[:, BQ:BQ + 8], bq_f_d),
                         (biasf[:, BK:BK + 8], bk_f_d),
                         (biasf[:, B1C:B1C + 32], b1_f_d),
                         (biasf[:, BS1C:BS1C + 32], bs1_f_d),
                         (biasf[:, B1MC:B1MC + 256], b1m_f_d),
                         (biasf[:, LN2C:LN2C + 8], g2_f_d),
                         (biasf[:, LN2C + 8:LN2C + 16], b2l_f_d)):
            b.dma("y", dst, src, sem=sx)
        b.dma("g", g1row[0:1, :], g1_r_d, sem=sw)
        b.dma("g", b1row[0:1, :], b1l_r_d, sem=sw)
        b.dma("g", bvrow[0:1, :], bv_r_d, sem=sw)
        b.call("g", "memset", ones_r[:], 1.0)
        b.call("g", "memset", ones2[:], 1.0, sem=sg)
        b.call("v", "memset", st1[:], 0.0)
        b.call("v", "memset", stats[:], 0.0, sem=sv)
        for g in range(NG):
            b.call("v", "memset", acc[g][:], 0.0)
            b.call("v", "memset", lsum[g][:], 0.0, sem=sv)
        b.wait("t", sw)
        b.wait("t", sg)
        b.wait("t", sx)
        for row in (g1row, b1row):
            for h4 in range(2):
                sl = slice(h4 * 512, (h4 + 1) * 512)
                b.wait("t", sv)
                b.mm(ps_x[:, 0:512], ones_r[0:1, 0:128], row[0:1, sl], sem=st)
                b.wait("v", st)
                b.vcopy(row[:, sl], ps_x[:, 0:512], sem=sv)
        b.wait("t", sv)

        lncons = [[0, 0, 0] for _ in range(4)]

        def xload(srcap, row0, slot):
            sxk = sxs[slot]
            b.wait("y", sv, lncons[slot][0])
            b.wait("y", ss, lncons[slot][1])
            b.wait("y", st, lncons[slot][2])
            return b.dma("y", xsta[:, slot * D:(slot + 1) * D],
                         srcap[row0:row0 + 128, :], sem=sxk)

        def stat_tile(srcap, pos, t):
            slot = t % 4
            xl = xload(srcap, pos, slot)
            xt = xsta[:, slot * D:(slot + 1) * D]
            b.wait("s", sxs[slot], xl)
            b.act(xt, xt, AF.Identity, accum=lns[:, t:t + 1])
            b.act(xt, xt, AF.Square, accum=lns[:, 80 + t:81 + t], sem=ss)
            lncons[slot][1] = ss.n

        lns_ready = [0]

        def ln_chain(t0, t1, own_xn=False):
            b.wait("v", ss)
            n = t1 - t0
            me = lns[:, 160 + t0:160 + t1]
            va = lns[:, 240 + t0:240 + t1]
            r_ = lns[:, 320 + t0:320 + t1]
            rt = lns[:, 400 + t0:400 + t1]
            bi = lns[:, 480 + t0:480 + t1]
            b.ts(me, lns[:, t0:t1], 1.0 / D, None, ALU.mult)
            b.ts(va, lns[:, 80 + t0:80 + t1], 1.0 / D, None, ALU.mult)
            b.tt(r_, me, me, ALU.mult)
            b.tt(va, va, r_, ALU.subtract)
            b.ts(r_, va, LN_EPS, None, ALU.add, sem=sv)
            b.wait("s", sv)
            b.act(r_, r_, AF.Ln, sem=ss)
            b.wait("s", ss)
            b.act(r_, r_, AF.Exp, scale=-0.5, sem=ss)
            b.wait("v", ss)
            b.ts(rt, va, LN_EPS, None, ALU.add)
            b.call("v", "reciprocal", rt, rt)
            b.tt(rt, va, rt, ALU.mult)
            b.ts(rt, rt, LN_EPS, None, ALU.add, sem=sv)
            b.wait("s", sv)
            b.act(rt, rt, AF.Ln, sem=ss)
            b.wait("s", ss)
            b.act(rt, rt, AF.Exp, scale=-0.5, sem=ss)
            b.wait("v", ss)
            b.tt(rt, r_, rt, ALU.mult)
            b.tt(bi, me, rt, ALU.mult)
            b.ts(bi, bi, -1.0, None, ALU.mult)
            if own_xn:
                bo_ = lns[:, 560 + t0 - 64:560 + t1 - 64]
                b.tt(bo_, me, r_, ALU.mult)
                b.ts(bo_, bo_, -1.0, None, ALU.mult)
            b.call("v", "engine_nop", sem=sv)
            lns_ready[0] = sv.n

        def ln_apply(srcap, pos, tbase, mode):
            xl = [0] * 4
            for tt_ in range(4):
                xl[tt_] = xload(srcap, pos + tt_ * 128, tbase % 4 * 0 + tt_)
            for tt_ in range(4):
                t = tbase + tt_
                xt = xsta[:, tt_ * D:(tt_ + 1) * D]
                b.wait("s", sxs[tt_], xl[tt_])
                b.wait("s", sv, lns_ready[0])
                if mode == "normed":
                    sc, bi = lns[:, 400 + t:401 + t], lns[:, 480 + t:481 + t]
                else:
                    sc, bi = lns[:, 320 + t:321 + t], lns[:, 560 + t - 64:
                                                         561 + t - 64]
                b.act(ntok[:, tt_ * D:(tt_ + 1) * D], xt, AF.Identity,
                      bias=bi, scale=sc, sem=ss)
                lncons[tt_][1] = ss.n
                lncons[tt_][2] = st.n
            ln_s = ss.n
            if mode == "normed":
                b.wait("v", ss)
                for tt_ in range(4):
                    nslot = ntok[:, tt_ * D:(tt_ + 1) * D]
                    b.tt(nslot, nslot, g1row[:], ALU.mult)
                    b.tt(nslot, nslot, b1row[:], ALU.add, sem=sv)
                    lncons[tt_][0] = sv.n
            return ln_s, sv.n

        tp_state = [0, 0]

        tp_half = [0, 0]

        def tpose4(dst, ln_s, ln_v):
            b.wait("t", ss, ln_s)
            b.wait("t", sv, ln_v)
            for tt_ in range(4):
                for h in range(2):
                    b.wait("t", sv, tp_half[h])
                    last = 0
                    for f4 in range(4):
                        f = h * 4 + f4
                        last = b.mm(pst[:, 1024 + h * 512 + f4 * 128:
                                        1024 + h * 512 + (f4 + 1) * 128],
                                    ntok[:, tt_ * D + f * 128:
                                         tt_ * D + (f + 1) * 128],
                                    idb[:], sem=st)
                    tp_state[1] = last
                    lncons[tt_][2] = last
                    b.wait("v", st, last)
                    src3 = pst[:, 1024 + h * 512:1024 + (h + 1) * 512
                               ].rearrange("p (f c) -> p f c", c=128)
                    dst3 = dst.rearrange("p (f c) -> p f c", c=CS)[
                        :, h * 4:(h + 1) * 4, tt_ * 128:(tt_ + 1) * 128]
                    tp_half[h] = b.call("v", "tensor_copy", dst3, src3,
                                        sem=sv)
            tp_state[0] = tp_half[1]
            return tp_half[1]

        bhist = []

        def stream_proj(src2_fn, rhs_fn, psum_fn, nm, nk=8, bias_mm=None,
                        qwaits=None):
            hist = bhist
            w_ = nm * 128
            for kk in range(0, nk, 4):
                u = (kk // 4) % 2
                sl = u * 2 * D
                swk = (swa, swb)[u]
                if len(hist) >= 2:
                    b.wait("g", st, hist[-2])
                wl = b.dma("g",
                           wstr[:, sl:sl + 4 * w_].rearrange(
                               "p (b c) -> p b c", b=4),
                           src2_fn(kk).rearrange("(b p) c -> p b c", p=128),
                           sem=swk)
                if kk == 0 and qwaits is not None:
                    for ws_, wv_2 in qwaits:
                        b.wait("t", ss, ws_)
                        b.wait("t", sv, wv_2)
                b.wait("t", swk, wl)
                last = None
                for bb2 in range(4):
                    k = kk + bb2
                    for m in range(nm):
                        last = b.mm(psum_fn(m),
                                    wstr[:, sl + bb2 * w_ + m * 128:
                                         sl + bb2 * w_ + (m + 1) * 128],
                                    rhs_fn(k), start=(k == 0),
                                    stop=(k == nk - 1 and bias_mm is None),
                                    sem=st)
                hist.append(last)
            if bias_mm is not None:
                bias_mm()
            return hist[-1]

        slot_hist = [(0, 0)] * 4
        rope_hist = [(0, 0), (0, 0)]

        def qk_proj_rope(is_q, dst, nfr=0):
            col0 = 0 if is_q else D
            bcol = BQ if is_q else BK
            evd = proj_waves(col0, bcol, nfr, rope_dst=dst)
            for m in range(4, 8):
                rope_tile(m, evd, dst)

        slot_reader = [0, 0, 0, 0]

        def band_dma(src2, u):
            sl = u * 1024
            swk = (swa, swb, swc, swd)[u]
            b.wait("g", st, slot_reader[u])
            wl = b.dma("g",
                       wstr[:, sl:sl + 1024].rearrange(
                           "p (b c) -> p b c", b=2),
                       src2.rearrange("(b p) c -> p b c", p=128), sem=swk)
            return (u, sl, swk, wl)

        def band_mms(dinfo, kk0, qwaits=None, stop_k=7):
            u, sl, swk, wl = dinfo
            b.wait("t", swk, wl)
            last = None
            for bb2 in range(2):
                k = kk0 + bb2
                for m in range(4):
                    if k == 0 and qwaits is not None:
                        b.wait("t", ss, qwaits[m][0])
                        b.wait("t", sv, qwaits[m][1])
                    last = b.mm(wv4(m),
                                wstr[:, sl + bb2 * 512 + m * 128:
                                     sl + bb2 * 512 + (m + 1) * 128],
                                nf[:, k * CS:(k + 1) * CS],
                                start=(k == 0), stop=(k == stop_k),
                                sem=st)
            slot_reader[u] = last
            bhist.append(last)

        def proj_waves(col0, bcol, nfr, pre=None, rope_dst=None,
                       next_fns=None):
            evd = [0] * 8
            b.wait("s", sv)
            b.wait("t", sv, nfr)
            KK = (0, 2, 4, 6)

            def ksrc(wv, kk0):
                return wqkv_d[kk0 * 128:(kk0 + 2) * 128,
                              col0 + wv * 512:col0 + (wv + 1) * 512]

            cur = pre if pre is not None else \
                [band_dma(ksrc(0, kk0), kk0 // 2) for kk0 in KK]
            for wv in range(2):
                nxt = []
                if wv == 1 and rope_dst is not None:
                    for m in range(4):
                        rope_tile(m, evd, rope_dst)
                for ui, kk0 in enumerate(KK):
                    band_mms(cur[ui], kk0,
                             qwaits=(slot_hist if kk0 == 0 else None))
                    if wv == 0:
                        nxt.append(band_dma(ksrc(1, kk0), ui))
                    elif next_fns is not None:
                        next_fns[ui]()
                for m4 in range(4):
                    m = wv * 4 + m4
                    b.wait("s", st, bhist[-1])
                    evd[m] = b.act(kpre[:, m * CS:(m + 1) * CS], wv4(m4),
                                   AF.Identity,
                                   bias=biasf[:, bcol + m:bcol + m + 1],
                                   sem=ss)
                    slot_hist[m4] = (evd[m], sv.n)
                cur = nxt
            return evd

        def rope_tile(m, evd, dst):
            pc = pav[:, (m % 2) * 512:(m % 2) * 512 + 512]
            b.wait("t", ss, evd[m])
            b.wait("t", sv, max(rope_hist[m % 2][0], fl_acc[m % 2]))
            rot = b.mm(pc, prot[:], kpre[:, m * CS:(m + 1) * CS], sem=st)
            b.wait("v", st, rot)
            b.tt(tmp1[:, 0:512], kpre[:, m * CS:(m + 1) * CS],
                 ropeb[:, 0:CS], ALU.mult)
            b.tt(tmp1[:, 512:1024], pc, ropeb[:, CS:2 * CS], ALU.mult)
            rv = b.tt(dst[:, m * CS:(m + 1) * CS], tmp1[:, 0:512],
                      tmp1[:, 512:1024], ALU.add, sem=sv)
            rope_hist[m % 2] = (rv, rot)

        def norms_update(src, stcol):
            sqd = [0, 0]
            rdd = [0, 0]
            for m in range(8):
                sq = pt_a[:, (m % 2) * 512:(m % 2) * 512 + 512]
                b.wait("s", st, sqd[m % 2])
                sqe = b.act(sq, src[:, m * CS:(m + 1) * CS], AF.Square,
                            sem=ss)
                psn = (ps_l, ps_x)[m % 2]
                b.wait("t", ss, sqe)
                if m >= 2:
                    b.wait("t", sv, rdd[m % 2])
                sqd[m % 2] = b.mm(psn[0:1, 0:512], ones2[:, 0:1], sq, sem=st)
                b.wait("v", st, sqd[m % 2])
                b.red("reduce_max", st1[0:1, 8:9], psn[0:1, 0:512])
                rdd[m % 2] = b.tt(st1[0:1, stcol:stcol + 1],
                                  st1[0:1, stcol:stcol + 1],
                                  st1[0:1, 8:9], ALU.max, sem=sv)
            b.wait("t", sv)

        fl_exp = [0, 0]
        fl_avl = [0, 0, 0]
        fl_acc = [0, 0]
        fl_lev = [0]
        fl_stage = [0]

        for g in range(NG):
            for tt_ in range(4):
                stat_tile(x_own, (g * 4 + tt_) * 128, 64 + g * 4 + tt_)
        ln_chain(64, 80, own_xn=True)
        for g in range(NG):
            b.wait("y", sv)
            b.dma("y", ropeb[:, 0:CS], ropeq_d[0:128, g * CS:(g + 1) * CS],
                  sem=sxr)
            rope_ld = b.dma("y", ropeb[:, CS:2 * CS],
                            ropeq_d[0:128, TOK + g * CS: TOK + (g + 1) * CS],
                            sem=sxr)
            ln_s, ln_v = ln_apply(x_own, g * CS, 64 + g * 4, "normed")
            nfr = tpose4(nf, ln_s, ln_v)
            b.wait("v", sxr, rope_ld)
            qk_proj_rope(True, qg[g], nfr)
            norms_update(kpre, 0)
            for cc in range(4 * g, 4 * g + 4):
                for tt_ in range(4):
                    stat_tile(x_pref, cc * CS + tt_ * 128, cc * 4 + tt_)

        sched_i = 0
        ln_chain(0, 64)
        ln_pipe = ln_apply(x_pref, 0, 0, "normed")
        kw0_pre = [band_dma(wqkv_d[kk0 * 128:(kk0 + 2) * 128, D:D + 512],
                            kk0 // 2) for kk0 in (0, 2, 4, 6)]
        for c in range(NCHUNK):
            b.wait("y", sv)
            b.dma("y", ropeb[:, 0:CS], ropek_d[0:128, c * CS:(c + 1) * CS],
                  sem=sxr)
            rope_ld = b.dma("y", ropeb[:, CS:2 * CS],
                            ropek_d[0:128, S + c * CS: S + (c + 1) * CS],
                            sem=sxr)
            ln_s, ln_v = ln_pipe
            nfr = tpose4(nf, ln_s, ln_v)
            if c + 1 < NCHUNK:
                ln_pipe = ln_apply(x_pref, (c + 1) * CS, (c + 1) * 4,
                                   "normed")
            b.wait("v", sxr, rope_ld)
            def vsrc(vh, kk0):
                return wqkv_d[kk0 * 128:(kk0 + 2) * 128,
                              2 * D + vh * 512:2 * D + (vh + 1) * 512]

            vd = {}

            def mk_vd(vh, ui):
                def f():
                    vd[(vh, ui)] = band_dma(vsrc(vh, 2 * ui), ui)
                return f

            def v_mms(kk0, dinfo):
                u, sl, swk, wl = dinfo
                if kk0 == 0:
                    b.wait("t", sv, nfr)
                    for m4 in range(4):
                        b.wait("t", ss, slot_hist[m4][0])
                        b.wait("t", sv, slot_hist[m4][1])
                b.wait("t", swk, wl)
                last = None
                for bb2 in range(2):
                    k = kk0 + bb2
                    for tt_ in range(4):
                        last = b.mm(wv4(tt_),
                                    nf[:, k * CS + tt_ * 128:
                                       k * CS + (tt_ + 1) * 128],
                                    wstr[:, sl + bb2 * 512:
                                         sl + bb2 * 512 + 512],
                                    start=(k == 0), stop=False, sem=st)
                slot_reader[u] = last
                bhist.append(last)

            def v_fin(vh):
                for tt_ in range(4):
                    b.mm(wv4(tt_), ones_r[0:1, 0:128],
                         bvrow[0:1, vh * 512:(vh + 1) * 512],
                         start=False, stop=True, sem=st)
                vlast = st.n
                for tt_ in range(4):
                    b.wait("s", st, vlast)
                    ev = b.act(vbuf[:, tt_ * D + vh * 512:
                                    tt_ * D + (vh + 1) * 512],
                               wv4(tt_), AF.Identity, sem=ss)
                    slot_hist[tt_] = (ev, sv.n)

            evd = proj_waves(D, BK, nfr, pre=kw0_pre, rope_dst=kbuf,
                             next_fns=[mk_vd(0, ui) for ui in range(4)])
            rope_tile(4, evd, kbuf)
            v_mms(0, vd[(0, 0)])
            mk_vd(1, 0)()
            rope_tile(5, evd, kbuf)
            v_mms(2, vd[(0, 1)])
            mk_vd(1, 1)()
            rope_tile(6, evd, kbuf)
            v_mms(4, vd[(0, 2)])
            mk_vd(1, 2)()
            rope_tile(7, evd, kbuf)
            v_mms(6, vd[(0, 3)])
            mk_vd(1, 3)()
            v_fin(0)
            v_mms(0, vd[(1, 0)])
            v_mms(2, vd[(1, 1)])
            v_mms(4, vd[(1, 2)])
            v_mms(6, vd[(1, 3)])
            v_fin(1)
            norms_update(kpre, 1)
            if c + 1 < NCHUNK:
                kw0_pre = [band_dma(wqkv_d[kk0 * 128:(kk0 + 2) * 128,
                                           D:D + 512], kk0 // 2)
                           for kk0 in (0, 2, 4, 6)]
            b.wait("s", sv)
            b.act(st1[0:1, 2:3], st1[0:1, 0:1], AF.Ln, sem=ss)
            b.act(st1[0:1, 3:4], st1[0:1, 1:2], AF.Ln, sem=ss)
            b.wait("v", ss)
            b.tt(st1[0:1, 8:9], st1[0:1, 2:3], st1[0:1, 3:4], ALU.add)
            b.ts(st1[0:1, 8:9], st1[0:1, 8:9], 0.5, -math.log(8.0),
                 ALU.mult, ALU.add, sem=sv)
            b.wait("s", sv)
            b.act(st1[0:1, 5:6], st1[0:1, 8:9], AF.Exp, sem=ss)
            b.wait("v", ss)
            b.vcopy(pt_a[0:1, 0:1], st1[0:1, 5:6])
            b.vcopy(st1[0:1, 5:6], pt_a[0:1, 0:1])
            b.tt(st1[0:1, 8:9], st1[0:1, 4:5], st1[0:1, 5:6], ALU.subtract)
            b.ts(st1[0:1, 7:8], st1[0:1, 5:6], -1.0, None, ALU.mult, sem=sv)
            b.wait("s", sv)
            b.act(st1[0:1, 6:7], st1[0:1, 8:9], AF.Exp, sem=ss)
            b.wait("v", ss)
            b.vcopy(st1[0:1, 4:5], st1[0:1, 5:6])
            b.vcopy(pt_a[0:1, 0:2], st1[0:1, 6:8], sem=sv)
            b.wait("t", sv)
            b.wait("t", ss)
            b.mm(ps_x[:, 0:2], ones_r[0:1, 0:128], pt_a[0:1, 0:2], sem=st)
            b.wait("v", st)
            b.wait("v", ss)
            b.vcopy(stats[:, 1:2], ps_x[:, 0:1])
            stats_rdy = b.vcopy(stats[:, 0:1], ps_x[:, 1:2], sem=sv)
            kv_rdy_v = sv.n
            kv_rdy_s = ss.n

            first_of_chunk = True
            while sched_i < NSTEP and SCHED[sched_i][0] == c:
                _, g = SCHED[sched_i]
                step = sched_i
                sched_i += 1
                candidate = (4 * g <= c)
                cmp_done = 0
                if candidate:
                    for kt in range(4):
                        cmp_done = b.ts(cmpb[:, kt * CS:(kt + 1) * CS],
                                        dti[:, kt * CS:(kt + 1) * CS],
                                        offs[:, step:step + 1], None,
                                        ALU.is_le, sem=sv)
                first_l = True

                def emit_avl(p):
                    nonlocal first_l
                    pt3, rdy, rdy_sem, hp_, kt_, g_, last = p
                    ptb = (pt_a, pt_b, pt_c)[pt3]
                    b.wait("t", rdy_sem, rdy)
                    if first_l:
                        b.wait("t", sv, fl_lev[0])
                    b.mm(ps_l[0:16, 0:512],
                         oh16[:, 16 * (2 * hp_):16 * (2 * hp_) + 16],
                         ptb[:, 0:512], start=first_l, stop=False)
                    b.mm(ps_l[0:16, 0:512],
                         oh16[:, 16 * (2 * hp_ + 1):16 * (2 * hp_ + 1) + 16],
                         ptb[:, 512:1024], start=False, stop=last)
                    first_l = False
                    if kt_ == 0:
                        b.wait("t", sv, max(fl_acc[0], fl_acc[1],
                                            rope_hist[0][0],
                                            rope_hist[1][0]))
                    b.mm(pav[0:64, 0:512],
                         vbuf[:, kt_ * D + hp_ * 128:kt_ * D + hp_ * 128 + 64],
                         ptb[:, 0:512], tp=(0, 0),
                         start=(kt_ == 0), stop=(kt_ == 3))
                    av = b.mm(pav[64:128, 512:1024],
                              vbuf[:, kt_ * D + hp_ * 128 + 64:
                                   kt_ * D + hp_ * 128 + 128],
                              ptb[:, 512:1024], tp=(0, 64),
                              start=(kt_ == 0), stop=(kt_ == 3), sem=st)
                    fl_avl[pt3] = av
                    if kt_ == 3:
                        b.wait("v", st, av)
                        aslice = acc[g_][:, hp_ * CS:(hp_ + 1) * CS]
                        fl_acc[0] = b.stt(aslice[0:64, :], aslice[0:64, :],
                                          stats[0:64, 1:2], pav[0:64, 0:512],
                                          ALU.mult, ALU.add, sem=sv)
                        fl_acc[1] = b.stt(aslice[64:128, :],
                                          aslice[64:128, :],
                                          stats[64:128, 1:2],
                                          pav[64:128, 512:1024],
                                          ALU.mult, ALU.add, sem=sv)

                pending = []
                for hp in range(8):
                    for kt in range(4):
                        sl_ = fl_stage[0] % 2
                        pt3 = fl_stage[0] % 3
                        fl_stage[0] += 1
                        pslot = pst[:, sl_ * 1024:(sl_ + 1) * 1024]
                        ptb = (pt_a, pt_b, pt_c)[pt3]
                        kc = hp * CS + kt * 128
                        b.wait("t", ss, fl_exp[sl_])
                        if first_of_chunk:
                            b.wait("t", sv, kv_rdy_v)
                            b.wait("t", ss, kv_rdy_s)
                            first_of_chunk = False
                        b.mm(pslot[:, 0:512], kbuf[0:64, kc:kc + 128],
                             qg[g][0:64, hp * CS:(hp + 1) * CS], tp=(0, 0))
                        p2 = b.mm(pslot[:, 512:1024],
                                  kbuf[64:128, kc:kc + 128],
                                  qg[g][64:128, hp * CS:(hp + 1) * CS],
                                  tp=(64, 0), sem=st)
                        b.wait("s", st, p2)
                        b.wait("s", st, fl_avl[pt3])
                        ex = b.act(ptb[:, 0:1024], pslot[:, 0:1024], AF.Exp,
                                   bias=stats[:, 0:1], sem=ss)
                        fl_exp[sl_] = ex
                        if candidate:
                            b.wait("v", ss, ex)
                            b.tt(ptb[:, 0:512], ptb[:, 0:512],
                                 cmpb[:, kt * CS:(kt + 1) * CS], ALU.mult)
                            mk = b.tt(ptb[:, 512:1024], ptb[:, 512:1024],
                                      cmpb[:, kt * CS:(kt + 1) * CS],
                                      ALU.mult, sem=sv)
                            rdy, rdy_sem = mk, sv
                        else:
                            rdy, rdy_sem = ex, ss
                        if len(pending) >= 2:
                            emit_avl(pending.pop(0))
                        pending.append((pt3, rdy, rdy_sem, hp, kt, g,
                                        hp == 7 and kt == 3))
                for p_ in pending:
                    emit_avl(p_)
                pending = []
                b.wait("v", st)
                fl_lev[0] = b.stt(lsum[g][0:16, :], lsum[g][0:16, :],
                                  stats[0:16, 1:2], ps_l[0:16, 0:512],
                                  ALU.mult, ALU.add, sem=sv)
                if c == 4 * g + 3:
                    b.call("v", "reciprocal", lsum[g][0:16, :],
                           lsum[g][0:16, :])
                    b.vcopy(pt_a[0:16, 0:512], lsum[g][0:16, :], sem=sv)
                    b.wait("t", sv)
                    for hp in range(8):
                        pc = pav[:, (hp % 2) * 512:(hp % 2) * 512 + 512]
                        b.wait("t", sv)
                        b.mm(pc, sel16[:, hp * 128:(hp + 1) * 128],
                             pt_a[0:16, 0:512], sem=st)
                        b.wait("v", st)
                        b.tt(qg[g][:, hp * CS:(hp + 1) * CS],
                             acc[g][:, hp * CS:(hp + 1) * CS], pc, ALU.mult,
                             sem=sv)
                    b.wait("t", sv)
                    fl_exp[0] = ss.n
                    fl_exp[1] = ss.n
                    fl_avl[0] = st.n
                    fl_avl[1] = st.n
                    fl_avl[2] = st.n
                    fl_acc[0] = sv.n
                    fl_acc[1] = sv.n
                    rope_hist[0] = (sv.n, st.n)
                    rope_hist[1] = (sv.n, st.n)

        def racc(f, g):
            return acc[g][:, f * CS:(f + 1) * CS]

        xnf = [kbuf, vbuf, kpre, nf]

        def x1t(k):
            if k < 2:
                return pt_a[:, k * 512:(k + 1) * 512]
            if k < 4:
                return pt_b[:, (k - 2) * 512:(k - 2) * 512 + 512]
            return cmpb[:, (k - 4) * 512:(k - 4) * 512 + 512]

        b.wait("g", st)
        brow_ld = b.dma("g", brow[0:1, :], bo_r_d, sem=swr)
        for g in range(NG):
            for tt_ in range(4):
                sl = tt_ * D
                sxk = sxs[tt_]
                t_ = 64 + g * 4 + tt_
                xl = xload(x_own, g * CS + tt_ * 128, tt_)
                b.wait("s", sxs[tt_], xl)
                b.wait("s", sv, lns_ready[0])
                b.act(ntok[:, tt_ * D:(tt_ + 1) * D], xsta[:, sl:sl + D],
                      AF.Identity, bias=lns[:, 560 + t_ - 64:561 + t_ - 64],
                      scale=lns[:, 320 + t_:321 + t_], sem=ss)
                lncons[tt_][1] = ss.n
                b.wait("t", sxs[tt_], xl)
                b.wait("t", ss)
                b.wait("t", sv)
                for f in range(8):
                    b.mm(pst[:, f * 128:(f + 1) * 128],
                         xsta[:, sl + f * 128: sl + (f + 1) * 128], idf[:],
                         sem=None)
                for f in range(8):
                    b.mm(pst[:, 1024 + f * 128:1024 + (f + 1) * 128],
                         ntok[:, tt_ * D + f * 128: tt_ * D + (f + 1) * 128],
                         idb[:], sem=(st if f == 7 else None))
                b.wait("v", st)
                xn3 = xnf[g].rearrange("p (f c) -> p f c", c=CS)[
                    :, :, tt_ * 128:(tt_ + 1) * 128]
                b.vcopy(xn3, pst[:, 1024:2048].rearrange(
                    "p (f c) -> p f c", c=128))
                ac3 = acc[g].rearrange("p (f c) -> p f c", c=CS)[
                    :, :, tt_ * 128:(tt_ + 1) * 128]
                b.tt(ac3, pst[:, 0:1024].rearrange("p (f c) -> p f c", c=128),
                     xn3, ALU.add, sem=sv)
                lncons[tt_] = [sv.n, ss.n, st.n]
                b.wait("t", sv)
            b.wait("t", ss)
            b.wait("t", swr, brow_ld)
            for wv_ in range(2):
                def bias_mm(wv2=wv_):
                    for m4 in range(4):
                        m = wv2 * 4 + m4
                        b.mm(wv4(m4), brow[0:1, m * 128:(m + 1) * 128],
                             ones_r[0:1, 0:512], start=False, stop=True,
                             sem=st)
                b.wait("t", sv)
                stream_proj(lambda kk, w2=wv_: wo_d[kk * 128:(kk + 4) * 128,
                                                    w2 * 512:(w2 + 1) * 512],
                            lambda k: qg[g][:, k * CS:(k + 1) * CS],
                            wv4, nm=4, bias_mm=bias_mm)
                b.wait("v", st)
                for m4 in range(4):
                    m = wv_ * 4 + m4
                    b.tt(racc(m, g), racc(m, g), wv4(m4), ALU.add)
                    b.tt(x1t(m), xnf[g][:, m * CS:(m + 1) * CS], wv4(m4),
                         ALU.add, sem=sv)
            if DEV_STAGE >= 2:
                b.wait("t", sv)
                sqb = ntok
                for f in range(8):
                    b.mm(pst[:, 0:512], ones2[:], x1t(f), start=(f == 0),
                         stop=(f == 7), sem=st)
                sqmm = [0, 0]
                for f in range(8):
                    sq = sqb[:, (f % 2) * 512:(f % 2) * 512 + 512]
                    if f >= 2:
                        b.wait("v", st, sqmm[f % 2])
                    b.tt(sq, x1t(f), x1t(f), ALU.mult, sem=sv)
                    b.wait("t", sv)
                    sqmm[f % 2] = b.mm(pst[:, 512:1024], ones2[:], sq,
                                       start=(f == 0), stop=(f == 7), sem=st)
                b.wait("v", st)
                mb = tmp1[:, 0:512]
                rb = tmp1[:, 512:1024]
                b.ts(mb, pst[:, 0:512], 1.0 / D, None, ALU.mult)
                b.ts(rb, pst[:, 512:1024], 1.0 / D, None, ALU.mult)
                b.tt(xsta[:, 0:512], mb, mb, ALU.mult)
                b.tt(rb, rb, xsta[:, 0:512], ALU.subtract)
                b.ts(rb, rb, LN_EPS, None, ALU.add, sem=sv)
                b.wait("s", sv)
                b.act(rb, rb, AF.Ln, sem=ss)
                b.wait("s", ss)
                b.act(rb, rb, AF.Exp, scale=-0.5, sem=ss)
                b.wait("v", ss)
                for f in range(8):
                    qsl = qg[g][:, f * CS:(f + 1) * CS]
                    b.tt(qsl, x1t(f), mb, ALU.subtract)
                    b.tt(qsl, qsl, rb, ALU.mult)
                    b.ts(qsl, qsl,
                         biasf[:, LN2C + f:LN2C + f + 1],
                         biasf[:, LN2C + 8 + f:LN2C + 8 + f + 1],
                         ALU.mult, ALU.add, sem=sv)
                for tt_ in range(4):
                    lncons[tt_] = [sv.n, ss.n, st.n]
                b.wait("t", sv)

        ffn_ld = [0, 0]

        def ffn_phase(rhs_fn, w1_src, w2_src, b1col, actfn, b2rhs_fn,
                      evac, Fp, prescale=False):
            nf_ = Fp // 128
            npiece = 4096 // Fp

            def w1sl(k):
                if Fp == 512:
                    return ffn_bufs[1][1][:, k * Fp:(k + 1) * Fp]
                buf = ffn_bufs[1][1] if k < 4 else ffn_bufs[2][1]
                return buf[:, (k % 4) * Fp:(k % 4 + 1) * Fp]

            def w2sl(kt):
                if nf_ <= 4:
                    return ffn_bufs[1][2][:, kt * 1024:(kt + 1) * 1024]
                buf = ffn_bufs[1][2] if kt < 4 else ffn_bufs[2][2]
                return buf[:, (kt % 4) * 1024:(kt % 4 + 1) * 1024]

            def asl(f, g4):
                buf = ffn_bufs[1][0] if f < 4 else ffn_bufs[2][0]
                return buf[:, (f % 4) * TOK + g4 * CS:
                           (f % 4) * TOK + (g4 + 1) * CS]

            cwbuf = ffn_bufs[2][4] if 2 in ffn_bufs else None
            for p in range(npiece):
                b.wait("g", st, ffn_ld[0])
                for k in range(8):
                    b.dma("g", w1sl(k), w1_src(k, p), sem=sw1)
                w1_ld = sw1.n
                b.wait("g", st, ffn_ld[1])
                for kt in range(nf_):
                    b.dma("g", w2sl(kt), w2_src(kt, p), sem=sw2)
                w2_ld = sw2.n
                b.wait("t", sw1, w1_ld)
                for f in range(nf_):
                    for g4 in range(4):
                        sl = (f * 4 + g4) % 4
                        b.wait("t", ss, slot_hist[sl][0])
                        b.wait("t", sv, slot_hist[sl][1])
                        for k in range(8):
                            b.mm(wv4(sl),
                                 w1sl(k)[:, f * 128:(f + 1) * 128],
                                 rhs_fn(g4, k), start=(k == 0), stop=(k == 7),
                                 sem=(st if k == 7 else None))
                        b.wait("s", st)
                        if f == nf_ - 1 and g4 == 3:
                            ffn_ld[0] = st.n
                        aslice = asl(f, g4)
                        sev = b.act(aslice, wv4(sl), actfn,
                                    bias=b1col(p, f), sem=ss)
                        vev = sv.n
                        if prescale:
                            b.wait("v", ss)
                            vev = b.tt(aslice, aslice,
                                       cwbuf[:, g4 * CS:(g4 + 1) * CS],
                                       ALU.mult, sem=sv)
                        slot_hist[sl] = (sev, vev)
                b.wait("t", sw2, w2_ld)
                b.wait("t", ss)
                if prescale:
                    b.wait("t", sv)
                for m in range(8):
                    for g4 in range(4):
                        sl = (m * 4 + g4) % 4
                        b.wait("t", ss, slot_hist[sl][0])
                        b.wait("t", sv, slot_hist[sl][1])
                        last_bias = (p == 0 and b2rhs_fn)
                        for kt in range(nf_):
                            b.mm(wv4(sl),
                                 w2sl(kt)[:, m * 128:(m + 1) * 128],
                                 asl(kt, g4),
                                 start=(kt == 0),
                                 stop=(kt == nf_ - 1 and not last_bias),
                                 sem=(None if (kt < nf_ - 1 or last_bias)
                                      else st))
                        if last_bias:
                            b.mm(wv4(sl), brow[0:1, m * 128:(m + 1) * 128],
                                 b2rhs_fn(g4), start=False, stop=True,
                                 sem=st)
                        b.wait("v", st)
                        if m == 7 and g4 == 3:
                            ffn_ld[1] = st.n
                        vev = evac(m, g4, wv4(sl))
                        slot_hist[sl] = (ss.n, vev)

        bstack.close()

        if DEV_STAGE >= 2:
            ffn_alloc(1)
            b.wait("g", st)
            b.wait("g", sv)
            rl = b.dma("g", brow[0:1, :], b2_r_d, sem=swr)
            b.wait("t", swr, rl)

            def mlp_evac(m, g4, ps):
                return b.tt(racc(m, g4), racc(m, g4), ps, ALU.add, sem=sv)

            ffn_phase(lambda g4, k: qg[g4][:, k * CS:(k + 1) * CS],
                      lambda k, p: w1_d[k * 128:(k + 1) * 128,
                                        p * 512:(p + 1) * 512],
                      lambda kt, p: w2_d[p * 512 + kt * 128:
                                         p * 512 + (kt + 1) * 128, 0:1024],
                      lambda p, f: biasf[:, B1C + p * 4 + f:
                                         B1C + p * 4 + f + 1],
                      AF.Gelu,
                      lambda g4: ones_r[0:1, 0:512],
                      mlp_evac, Fp=512)

        dstack.close()

        if DEV_STAGE >= 3:
            ffn_alloc(1)
            ffn_alloc(2)
            cwT = ffn_bufs[2][3]
            cwb = ffn_bufs[2][4]
            wgb = ffn_bufs[2][5]
            b.wait("g", st)
            for k in range(8):
                rl = b.dma("g", wgb[:, k * 8:(k + 1) * 8],
                           wg_d[k * 128:(k + 1) * 128, :], sem=swr)
            b.wait("t", swr, rl)
            b.wait("t", sv)
            b.wait("t", ss)
            for t16 in range(16):
                g4, tt_ = divmod(t16, 4)
                for k in range(8):
                    b.mm(ps_l[:, t16 * 8:(t16 + 1) * 8],
                         xnf[g4][:, k * CS + tt_ * 128:
                                 k * CS + (tt_ + 1) * 128],
                         wgb[:, k * 8:(k + 1) * 8], start=(k == 0),
                         stop=(k == 7), sem=(st if k == 7 else None))
            b.wait("s", st)
            b.act(tmp1[:, 0:128], ps_l[:, 0:128], AF.Exp, sem=ss)
            b.wait("v", ss)
            e8a = tmp1[:, 0:128].rearrange("p (g e) -> p g e", e=8)
            scr3 = tmp1[:, 128:256].rearrange("p (g e) -> p g e", e=8)
            sc23 = tmp1[:, 320:448].rearrange("p (g e) -> p g e", e=8)
            sm = tmp1[:, 256:272]
            m1 = tmp1[:, 272:288]
            m2 = tmp1[:, 288:304]
            dn = tmp1[:, 304:320]

            def col3(ap16):
                return ap16.rearrange("p (g one) -> p g one", one=1)

            def bc8(ap16):
                return col3(ap16).broadcast_to([128, 16, 8])

            b.red("reduce_sum", col3(sm), e8a)
            b.red("reduce_max", col3(m1), e8a)
            b.tt(scr3, e8a, bc8(m1), ALU.is_ge)
            b.stt(sc23, scr3, -1e30, e8a, ALU.mult, ALU.add)
            b.red("reduce_max", col3(m2), sc23)
            b.tt(scr3, e8a, bc8(m2), ALU.is_ge)
            b.tt(dn, m1, m2, ALU.add)
            b.stt(dn, sm, 1e-9, dn, ALU.mult, ALU.add)
            b.call("v", "reciprocal", dn, dn)
            b.tt(scr3, e8a, scr3, ALU.mult)
            b.tt(ntok[:, 0:128].rearrange("p (g e) -> p g e", e=8),
                 scr3, bc8(dn), ALU.mult, sem=sv)
            for t16 in range(16):
                b.wait("t", sv)
                b.mm(ps_x[0:8, (t16 % 2) * 128:(t16 % 2) * 128 + 128],
                     ntok[:, t16 * 8:(t16 + 1) * 8], idb[:], sem=st)
                b.wait("v", st)
                b.vcopy(cwT[0:8, t16 * 128:(t16 + 1) * 128],
                        ps_x[0:8, (t16 % 2) * 128:(t16 % 2) * 128 + 128],
                        sem=sv)
            for e in range(8):
                b.wait("t", sv)
                b.wait("t", ss)
                for g4 in range(4):
                    b.wait("t", ss)
                    b.mm(pav[:, (g4 % 2) * 512:(g4 % 2) * 512 + 512],
                         oh8[:, e * 128:(e + 1) * 128],
                         cwT[0:8, g4 * 512:(g4 + 1) * 512], sem=st)
                    b.wait("s", st)
                    b.wait("s", sv)
                    b.act(cwb[:, g4 * CS:(g4 + 1) * CS],
                          pav[:, (g4 % 2) * 512:(g4 % 2) * 512 + 512],
                          AF.Identity, sem=ss)
                b.wait("g", ss)
                b.wait("g", st)
                rl = b.dma("g", brow[0:1, :],
                           b2m_r_d[0:1, e * D:(e + 1) * D], sem=swr)
                b.wait("t", swr, rl)
                b.wait("t", ss)

                def moe_evac(m, g4, ps):
                    return b.tt(racc(m, g4), racc(m, g4), ps, ALU.add,
                                sem=sv)

                ffn_phase(lambda g4, k: xnf[g4][:, k * CS:(k + 1) * CS],
                          lambda k, p, e2=e: mw1_d[e2][k * 128:(k + 1) * 128,
                                                       p * 1024:(p + 1) * 1024],
                          lambda kt, p, e2=e: mw2_d[e2][p * 1024 + kt * 128:
                                                        p * 1024 + (kt + 1) * 128,
                                                        0:1024],
                          lambda p, f, e2=e: biasf[:, B1MC + e2 * 32 + p * 8 + f:
                                                   B1MC + e2 * 32 + p * 8 + f + 1],
                          AF.Silu,
                          lambda g4: cwb[0:1, g4 * CS:(g4 + 1) * CS],
                          moe_evac, Fp=1024, prescale=True)

        if DEV_STAGE >= 4:
            b.wait("g", st)
            b.wait("g", sv)
            rl = b.dma("g", brow[0:1, :], bs2_r_d, sem=swr)
            b.wait("t", swr, rl)

            def sh_evac(m, g4, ps):
                return b.stt(racc(m, g4), ps, 0.1, racc(m, g4), ALU.mult,
                             ALU.add, sem=sv)

            ffn_phase(lambda g4, k: xnf[g4][:, k * CS:(k + 1) * CS],
                      lambda k, p: ws1_d[k * 128:(k + 1) * 128,
                                         p * 1024:(p + 1) * 1024],
                      lambda kt, p: ws2_d[p * 1024 + kt * 128:
                                          p * 1024 + (kt + 1) * 128, 0:1024],
                      lambda p, f: biasf[:, BS1C + p * 8 + f:
                                         BS1C + p * 8 + f + 1],
                      AF.Silu,
                      lambda g4: ones_r[0:1, 0:512],
                      sh_evac, Fp=1024)

        for g in range(NG):
            for tt_ in range(4):
                b.wait("t", sv)
                b.wait("t", ss)
                for f in range(8):
                    b.mm(pst[:, f * 128:(f + 1) * 128],
                         racc(f, g)[:, tt_ * 128:(tt_ + 1) * 128], idf[:],
                         sem=(st if f == 7 else None))
                b.wait("v", st)
                b.wait("v", so)
                b.vcopy(tmp1[:, 0:1024], pst[:, 0:1024], sem=sv)
                b.wait("y", sv)
                row0 = g * CS + tt_ * 128
                b.dma("y", out_d[row0:row0 + 128, :], tmp1[:, 0:1024], sem=so)
        b.wait("y", so)

        @block.tensor
        def _(E):
            for fn in b.q["t"]:
                fn(E)

        @block.vector
        def _(E):
            for fn in b.q["v"]:
                fn(E)

        @block.scalar
        def _(E):
            for fn in b.q["s"]:
                fn(E)

        @block.gpsimd
        def _(E):
            for fn in b.q["g"]:
                fn(E)

        @block.sync
        def _(E):
            for fn in b.q["y"]:
                fn(E)

    return nc


def _host_constants():
    inv = 1.0 / (10000.0 ** (np.arange(0, HD, 2, dtype=np.float64) / HD))
    t = np.arange(S, dtype=np.float64)
    fr = t[:, None] * inv[None, :]
    emb = np.concatenate([fr, fr], axis=1)
    cos = np.cos(emb).astype(np.float32)
    sin = np.sin(emb).astype(np.float32)
    cosf = np.concatenate([cos.T, cos.T], axis=0)
    sinf = np.concatenate([sin.T, sin.T], axis=0)
    iden = np.eye(128, dtype=np.float32)
    prot = np.zeros((128, 128), dtype=np.float32)
    for blk in range(2):
        o = blk * 64
        for m in range(32):
            prot[o + m + 32, o + m] = -1.0
            prot[o + m, o + m + 32] = 1.0
    oh16 = np.zeros((128, 16 * H), dtype=np.float32)
    for h in range(H):
        oh16[:, 16 * h + h] = 1.0
    sel16 = np.zeros((16, 128 * (H // 2)), dtype=np.float32)
    for hp in range(H // 2):
        sel16[2 * hp, hp * 128: hp * 128 + 64] = 1.0
        sel16[2 * hp + 1, hp * 128 + 64: hp * 128 + 128] = 1.0
    oh8 = np.zeros((8, 8 * 128), dtype=np.float32)
    for e in range(8):
        oh8[e, e * 128:(e + 1) * 128] = 1.0
    dt4 = np.zeros((128, 4 * CS), dtype=np.int16)
    for kt in range(4):
        p = np.arange(128)[:, None]
        q = np.arange(CS)[None, :]
        dt4[:, kt * CS:(kt + 1) * CS] = (128 * kt + p - q).astype(np.int16)
    return cosf, sinf, iden, prot, oh16, sel16, oh8, dt4


_PROG_CACHE = {}


def kernel(**inp):
    cosf, sinf, iden, prot, oh16, sel16, oh8, dt4 = _host_constants()
    x = np.asarray(inp["L0_attn_x"], dtype=np.float32)
    wqkv = np.ascontiguousarray(inp["L0_attn_Wqkv"], dtype=np.float32)
    bqkv = np.asarray(inp["L0_attn_bqkv"], dtype=np.float32)

    def bf(a):
        return np.ascontiguousarray(a).astype(BF)

    common = {
        "offs_h": None,
        "ropek": bf(np.concatenate([cosf, sinf], axis=1)),
        "dtiles": dt4,
        "idb_h": bf(iden), "idf_h": iden, "prot_h": bf(prot),
        "oh16_h": bf(oh16), "sel16_h": bf(sel16), "oh8_h": bf(oh8),
        "Wqkv": wqkv,
        "Wo": np.ascontiguousarray(inp["L0_attn_Wo"], dtype=np.float32),
        "W1": np.ascontiguousarray(inp["L0_attn_W1"], dtype=np.float32),
        "W2": np.ascontiguousarray(inp["L0_attn_W2"], dtype=np.float32),
        "Wg": np.ascontiguousarray(inp["L0_ffn_Wg"], dtype=np.float32),
        "MW1": np.ascontiguousarray(inp["L0_ffn_W1"], dtype=np.float32),
        "MW2": np.ascontiguousarray(inp["L0_ffn_W2"], dtype=np.float32),
        "Ws1": np.ascontiguousarray(inp["L0_ffn_Ws1"], dtype=np.float32),
        "Ws2": np.ascontiguousarray(inp["L0_ffn_Ws2"], dtype=np.float32),
        "bq_f": np.ascontiguousarray(bqkv[0:D].reshape(8, 128).T),
        "bk_f": np.ascontiguousarray(bqkv[D:2 * D].reshape(8, 128).T),
        "b1_f": np.ascontiguousarray(
            np.asarray(inp["L0_attn_b1"], np.float32).reshape(32, 128).T),
        "bs1_f": np.ascontiguousarray(
            np.asarray(inp["L0_ffn_bs1"], np.float32).reshape(32, 128).T),
        "b1m_f": np.ascontiguousarray(
            np.asarray(inp["L0_ffn_b1"], np.float32).reshape(8, 32, 128)
            .transpose(2, 0, 1).reshape(128, 256)),
        "g2_f": np.ascontiguousarray(
            np.asarray(inp["L0_attn_ln2_g"], np.float32).reshape(8, 128).T),
        "b2l_f": np.ascontiguousarray(
            np.asarray(inp["L0_attn_ln2_b"], np.float32).reshape(8, 128).T),
        "g1_r": np.asarray(inp["L0_attn_ln1_g"], np.float32)[None, :],
        "b1l_r": np.asarray(inp["L0_attn_ln1_b"], np.float32)[None, :],
        "bv_r": bqkv[2 * D:3 * D][None, :],
        "bo_r": np.asarray(inp["L0_attn_bo"], np.float32)[None, :],
        "b2_r": np.asarray(inp["L0_attn_b2"], np.float32)[None, :],
        "bs2_r": np.asarray(inp["L0_ffn_bs2"], np.float32)[None, :],
        "b2m_r": np.ascontiguousarray(
            np.asarray(inp["L0_ffn_b2"], np.float32).reshape(1, 8 * D)),
    }

    in_maps = []
    for core in range(8):
        bb, j = divmod(core, 4)
        chunks = [4 * g + j for g in range(NG)]
        m = dict(common)
        m["x_pref"] = x[bb]
        m["x_own"] = np.ascontiguousarray(
            np.concatenate([x[bb, ch * CS:(ch + 1) * CS] for ch in chunks]))
        offs = np.zeros((128, NSTEP), dtype=np.float32)
        for si, (c, g) in enumerate(SCHED):
            offs[:, si] = (4 * g + j - c) * CS
        m["offs_h"] = offs
        cols = np.concatenate(
            [np.arange(ch * CS, (ch + 1) * CS) for ch in chunks])
        m["ropeq"] = bf(np.concatenate(
            [cosf[:, cols] / 8.0, sinf[:, cols] / 8.0], axis=1))
        in_maps.append(m)

    key = DEV_STAGE
    if key not in _PROG_CACHE:
        _PROG_CACHE[key] = build_program()
    nc = _PROG_CACHE[key]
    res = run_bass_kernel_spmd(nc, in_maps, core_ids=list(range(8)))
    out = np.zeros((B, S, D), dtype=np.float32)
    for core in range(8):
        bb, j = divmod(core, 4)
        r = res.results[core]["out"]
        for g in range(NG):
            ch = 4 * g + j
            out[bb, ch * CS:(ch + 1) * CS] = r[g * CS:(g + 1) * CS]
    return out
